# version 10
# baseline (speedup 1.0000x reference)
import math
import numpy as np
import ml_dtypes

import concourse.bass as bass
import concourse.mybir as mybir
from concourse.bass_utils import run_bass_kernel_spmd

F32 = mybir.dt.float32
BF16 = mybir.dt.bfloat16
U8 = mybir.dt.uint8
AL = mybir.AluOpType
AF = mybir.ActivationFunctionType
AX = mybir.AxisListType.X

NCORES = 8
S = 2048
D = 1024
H = 16
HD = 64
HID = 4096
BPC = 2
TOK = BPC * S
NT = TOK // 128
EXP_SHIFT = 20.0
MAGIC = 12582912.0
EPS_LN = 1e-5
N1 = 16 * S * D
N2 = 16 * S * HID
G1_STAR = float((N1 - 1) * 0.005)
G2_STAR = float((N2 - 1) * 0.005)
NBIS = 20
FAW = 9216


class _Inst:
    __slots__ = ("k", "inst")

    def __init__(self, k, inst):
        self.k, self.inst = k, inst

    def inc(self, sem, amt=None, ev=None):
        k = self.k
        amt = amt if amt is not None else (16 if sem in k.dma_sems else 1)
        k.counts[sem] += amt
        if self.inst is not None:
            self.inst.then_inc(k.sems[sem], amt)
        if ev is not None:
            k.event(ev, sem)
        return self


class _Eng:
    def __init__(self, k, name):
        self.k, self.name = k, name

    @property
    def real(self):
        return self.k.cur == self.name

    def __getattr__(self, attr):
        k = self.k
        if self.real:
            m = getattr(k.engines[self.name], attr)
            return lambda *a, **kw: _Inst(k, m(*a, **kw))
        return lambda *a, **kw: _Inst(k, None)

    def wait(self, ev, extra=0):
        if self.real:
            sem, val = self.k.events[ev]
            self.k.engines[self.name].wait_ge(self.k.sems[sem], val + extra)

    def wait_if(self, ev):
        if self.real and ev in self.k.events:
            sem, val = self.k.events[ev]
            self.k.engines[self.name].wait_ge(self.k.sems[sem], val)

    def drain(self):
        if self.real:
            self.k.engines[self.name].drain()


class _KB:
    DMA_STREAMS = ["sCON", "sGA", "sGB", "sXT0", "sXT1", "sXT2", "sXT3",
                   "sWS0", "sWS1", "sWS2", "sWS3", "sWS4", "sWS5", "sWS6", "sWS7",
                   "sWV", "sWV2", "sWV3", "sWV4",
                   "sWO0", "sWO1", "sWO2", "sWO3", "sXR0", "sXR1",
                   "sX1S0", "sX1S1", "sX1S2", "sX1S3", "sH0S", "sDBG",
                   "sQ1a", "sQ1b", "sXQa", "sXQb", "sW1L", "sW1La", "sW1G", "sW1Gb",
                   "sHSa", "sHSb", "sQ2L", "sQ2b", "sQ2c", "sQ2d",
                   "sW2L", "sW2La", "sW2G", "sW2Gb",
                   "sHB", "sHB2", "sHB3", "sHB4", "sX1R", "sX1R2",
                   "gPRM", "aPRM", "gPR2", "aPR2", "gARB", "gOUT"]

    def __init__(self):
        self.dma_sems = set(self.DMA_STREAMS)
        self.sem_names = self.DMA_STREAMS + ["PE", "DV", "AC", "GP", "CC"]
        self.sems = {}
        self.events = {}
        self.counts = {}
        self.cur = "___dry___"
        self.engines = {}

    @property
    def dry(self):
        return self.cur == "___dry___"

    def begin(self, name):
        self.cur = name
        self.counts = {s: 0 for s in self.sem_names}

    def event(self, name, sem):
        v = (sem, self.counts[sem])
        if self.dry:
            assert name not in self.events, f"duplicate event {name}"
            self.events[name] = v
        else:
            assert self.events.get(name) == v, (self.cur, name, self.events.get(name), v)


def _host_tables():
    pos = np.arange(S, dtype=np.float64)
    inv_freq = np.exp(np.arange(0, HD, 2, dtype=np.float64) * (-math.log(10000.0) / HD))
    ang = pos[:, None] * inv_freq[None, :]
    cos128 = np.ascontiguousarray(np.tile(np.cos(ang).T, (4, 1))).astype(ml_dtypes.bfloat16)
    sin128 = np.ascontiguousarray(np.tile(np.sin(ang).T, (4, 1))).astype(ml_dtypes.bfloat16)
    ident = np.eye(128, dtype=ml_dtypes.bfloat16)
    u2 = (np.arange(128)[None, :] < np.arange(128)[:, None]).astype(ml_dtypes.bfloat16)
    dneg = (np.eye(128) * -10000.0).astype(ml_dtypes.bfloat16)
    ones_row_b = np.ones((1, 128), dtype=ml_dtypes.bfloat16)
    ones64_b = np.ones((128, 64), dtype=ml_dtypes.bfloat16)
    ones_col_f = np.ones((128, 1), dtype=np.float32)
    ones_row_f = np.ones((1, 128), dtype=np.float32)
    cvals = np.tile(np.array([[-EXP_SHIFT, 0.125, -0.5, -1.0, 0.5, 1.0, 0.0, 0.0]],
                             dtype=np.float32), (128, 1))
    return dict(cos128=cos128, sin128=sin128, ident=ident, u2=u2, dneg=dneg,
                ones_row_b=ones_row_b, ones64_b=ones64_b, ones_col_f=ones_col_f,
                ones_row_f=ones_row_f, cvals=cvals)


def _perm_index():
    idx = np.zeros(D, dtype=np.int64)
    for g in range(4):
        for j in range(4):
            h = 4 * g + j
            idx[(2 * g) * 128 + j * 32:(2 * g) * 128 + (j + 1) * 32] = np.arange(h * HD, h * HD + 32)
            idx[(2 * g + 1) * 128 + j * 32:(2 * g + 1) * 128 + (j + 1) * 32] = np.arange(h * HD + 32, h * HD + 64)
    return idx


def _reshape_r(w):
    R, C = w.shape
    nch = R // 128
    return np.ascontiguousarray(w.reshape(nch, 128, C).transpose(1, 0, 2).reshape(128, nch * C))


def _reshape_et(w):
    return np.ascontiguousarray(
        w.reshape(8, 128, 8, 128).transpose(1, 2, 0, 3).reshape(128, 8192))


def build_program(stage=99):
    nc = bass.Bass("TRN2", target_bir_lowering=False)

    dx = nc.dram_tensor("x", [TOK, D], F32, kind="ExternalInput").ap()
    dxT = nc.dram_tensor("xT_r", [BPC, 128, 8 * S], F32, kind="ExternalInput").ap()
    dWq = nc.dram_tensor("Wq_r", [128, 8 * D], F32, kind="ExternalInput").ap()
    dWk = nc.dram_tensor("Wk_r", [128, 8 * D], F32, kind="ExternalInput").ap()
    dWv = nc.dram_tensor("Wv_r", [128, 8 * D], F32, kind="ExternalInput").ap()
    dWo = nc.dram_tensor("Wo_r", [128, 8 * D], F32, kind="ExternalInput").ap()
    dbqp = nc.dram_tensor("bqp", [128, 8], F32, kind="ExternalInput").ap()
    dbkp = nc.dram_tensor("bkp", [128, 8], F32, kind="ExternalInput").ap()
    dbv = nc.dram_tensor("bv", [1, D], F32, kind="ExternalInput").ap()
    dbo = nc.dram_tensor("bo", [1, D], F32, kind="ExternalInput").ap()
    dlng = nc.dram_tensor("lng", [128, D], F32, kind="ExternalInput").ap()
    dlnb = nc.dram_tensor("lnb", [128, D], F32, kind="ExternalInput").ap()
    dw1 = nc.dram_tensor("w1T_r", [128, 8 * HID], F32, kind="ExternalInput").ap()
    dw2 = nc.dram_tensor("w2T_r", [128, 32 * D], F32, kind="ExternalInput").ap()
    dcos = nc.dram_tensor("cos128", [128, S], BF16, kind="ExternalInput").ap()
    dsin = nc.dram_tensor("sin128", [128, S], BF16, kind="ExternalInput").ap()
    dident = nc.dram_tensor("ident", [128, 128], BF16, kind="ExternalInput").ap()
    du2 = nc.dram_tensor("u2", [128, 128], BF16, kind="ExternalInput").ap()
    ddneg = nc.dram_tensor("dneg", [128, 128], BF16, kind="ExternalInput").ap()
    donesrb = nc.dram_tensor("ones_row_b", [1, 128], BF16, kind="ExternalInput").ap()
    dones64 = nc.dram_tensor("ones64_b", [128, 64], BF16, kind="ExternalInput").ap()
    donescf = nc.dram_tensor("ones_col_f", [128, 1], F32, kind="ExternalInput").ap()
    donesrf = nc.dram_tensor("ones_row_f", [1, 128], F32, kind="ExternalInput").ap()
    dcval = nc.dram_tensor("cvals", [128, 8], F32, kind="ExternalInput").ap()
    dout = nc.dram_tensor("out", [TOK, D], F32, kind="ExternalOutput").ap()
    ddbg = nc.dram_tensor("dbg", [128, 40], F32, kind="ExternalOutput").ap()

    dx1 = nc.dram_tensor("x1sp", [TOK, D], F32).ap()
    dh0 = nc.dram_tensor("h0sp", [TOK, D], F32).ap()
    dh = nc.dram_tensor("hsp", [32, 128, TOK], F32).ap()
    cbi = [nc.dram_tensor("cbi%d" % i, [1, 8], F32).ap() for i in range(4)]
    cbo = [nc.dram_tensor("cbo%d" % i, [1, 8], F32).ap() for i in range(4)]

    K = _KB()
    V = _Eng(K, "vector")
    A = _Eng(K, "scalar")
    T = _Eng(K, "tensor")
    G = _Eng(K, "gpsimd")
    SY = _Eng(K, "sync")

    import contextlib
    stack = contextlib.ExitStack()

    def sbuf(name, shape, dtype):
        return stack.enter_context(nc.sbuf_tensor(name, shape, dtype))

    cosT = sbuf("cosT", [128, S], BF16)
    sinT = sbuf("sinT", [128, S], BF16)
    identT = sbuf("identT", [128, 128], BF16)
    u2T = sbuf("u2T", [128, 128], BF16)
    dnegT = sbuf("dnegT", [128, 128], BF16)
    onesrb = sbuf("onesrb", [1, 128], BF16)
    ones64 = sbuf("ones64", [128, 64], BF16)
    onescf = sbuf("onescf", [128, 1], F32)
    onesrf = sbuf("onesrf", [1, 128], F32)
    cval = sbuf("cval", [128, 8], F32)
    bqpT = sbuf("bqpT", [128, 8], F32)
    bkpT = sbuf("bkpT", [128, 8], F32)
    bvT = sbuf("bvT", [1, D], BF16)
    boT = sbuf("boT", [1, D], BF16)
    lngT = sbuf("lngT", [128, D], F32)
    lnbT = sbuf("lnbT", [128, D], F32)
    S1 = sbuf("S1", [128, 1024], BF16)
    S2 = sbuf("S2", [128, 1024], BF16)
    qlo = sbuf("qlo", [128, 1], F32)
    qhi = sbuf("qhi", [128, 1], F32)
    qmid = sbuf("qmid", [128, 1], F32)
    qnmid = sbuf("qnmid", [128, 1], F32)
    qcnt = sbuf("qcnt", [128, 1], F32)
    qcnt2 = sbuf("qcnt2", [128, 1], F32)
    qmsk = sbuf("qmsk", [128, 1], U8)
    qvec = sbuf("qvec", [128, 2], F32)
    ario = sbuf("ario", [1, 8], F32)
    rowa = sbuf("rowa", [1, 16], F32)
    rowb = sbuf("rowb", [1, 16], F32)
    thr4 = sbuf("thr4", [128, 4], F32)
    cntall = sbuf("cntall", [128, 256], F32)
    cnt4 = sbuf("cnt4", [128, 4], F32)
    sca = sbuf("sca", [128, 4], F32)
    gq = sbuf("gq", [128, 2], F32)
    gsum = sbuf("gsum", [128, 34], F32)
    stat = sbuf("stat", [128, 16], F32)
    junk = sbuf("junk", [128, 2048], BF16)

    dbgT = sbuf("dbgT", [128, 40], F32)
    BFA = sbuf("BFA", [128, 71680], BF16)
    FA = sbuf("FA", [128, FAW], F32)
    PS = stack.enter_context(nc.psum_tensor("PS", [128, 4096], F32))

    def bk(i, off=0, w=512, p0=0, pn=128):
        return PS[p0:p0 + pn, i * 512 + off: i * 512 + off + w]

    def bf(off, w, p0=0, pn=128):
        return BFA[p0:p0 + pn, off:off + w]

    def fa(off, w, p0=0, pn=128):
        return FA[p0:p0 + pn, off:off + w]

    def bff(off_bf, w, p0=0, pn=128):
        return BFA[p0:p0 + pn, off_bf:off_bf + 2 * w].bitcast(F32)

    O_XTB, O_SCR, O_WSL, O_QC, O_KC, O_VN = 0, 16384, 20480, 22528, 38912, 55296
    O_RT = 55296
    O_VAS, O_PT, O_CXN, O_WO = 0, 2048, 4096, 8192
    O_XQT, O_W1Q, O_XQG = 0, 32768, 65536
    O_W2Q, O_HQ = 0, 32768
    F_STG, F_X1G, F_H0G, F_XRS, F_RCP = 0, 2048, 4096, 6144, 1024
    F_XPA = 7168
    F_WST = F_H0G

    def xsl(tt):
        m = tt % 4
        return (F_X1G + m * 1024) if m < 2 else (F_XPA + (m - 2) * 1024)

    def vdr():
        V.drain()

    def adr():
        A.drain()

    uid = [0]

    def _u():
        uid[0] += 1
        return uid[0]

    def replicate(row_ap, out_ap, n, tag, wait_ev=None):
        if wait_ev is not None:
            T.wait(wait_ev)
        T.matmul(PS[:, 7 * 512:7 * 512 + n], onesrf[0:1, :], row_ap, start=True, stop=True)\
            .inc("PE", ev=tag + "_mm")
        V.wait(tag + "_mm")
        V.tensor_copy(out=out_ap, in_=PS[:, 7 * 512:7 * 512 + n]).inc("DV", ev=tag)
        vdr()

    def cross_sum(cols_ap, n, tag, wait_ev=None):
        if wait_ev is not None:
            T.wait(wait_ev)
        T.matmul(PS[0:1, 7 * 512:7 * 512 + n], onescf[:, 0:1], cols_ap, start=True, stop=True)\
            .inc("PE", ev=tag)

    def emit_bisect(Sbuf, width, target, tag):
        V.memset(qlo[:, :], 0.0)
        V.tensor_reduce(out=qhi[:, :], in_=Sbuf[:, 0:width], axis=AX, op=AL.max,
                        apply_absolute_value=True).inc("DV")
        vdr()
        for _ in range(NBIS):
            V.tensor_tensor(out=qmid[:, :], in0=qlo[:, :], in1=qhi[:, :], op=AL.add).inc("DV")
            vdr()
            V.tensor_single_scalar(out=qmid[:, :], in_=qmid[:, :], scalar=0.5, op=AL.mult).inc("DV")
            vdr()
            V.tensor_single_scalar(out=qnmid[:, :], in_=qmid[:, :], scalar=-1.0, op=AL.mult).inc("DV")
            V.tensor_scalar(out=junk[:, 0:width], in0=Sbuf[:, 0:width], scalar1=qmid[:, 0:1],
                            scalar2=None, op0=AL.is_gt, op1=AL.add, accum_out=qcnt[:, 0:1]).inc("DV")
            vdr()
            V.tensor_scalar(out=junk[:, 0:width], in0=Sbuf[:, 0:width], scalar1=qnmid[:, 0:1],
                            scalar2=None, op0=AL.is_lt, op1=AL.add, accum_out=qcnt2[:, 0:1]).inc("DV")
            vdr()
            V.tensor_tensor(out=qcnt[:, :], in0=qcnt[:, :], in1=qcnt2[:, :], op=AL.add).inc("DV")
            vdr()
            V.tensor_single_scalar(out=qmsk[:, :], in_=qcnt[:, :], scalar=float(target), op=AL.is_gt).inc("DV")
            vdr()
            V.copy_predicated(out=qlo[:, :], mask=qmsk[:, :], data=qmid[:, :]).inc("DV")
            vdr()
            V.tensor_single_scalar(out=qmsk[:, :], in_=qcnt[:, :], scalar=float(target), op=AL.is_le).inc("DV")
            vdr()
            V.copy_predicated(out=qhi[:, :], mask=qmsk[:, :], data=qmid[:, :]).inc("DV")
            vdr()
        V.tensor_copy(out=qvec[:, 0:1], in_=qhi[:, :]).inc("DV")
        V.tensor_tensor(out=qvec[:, 1:2], in0=qhi[:, :], in1=qhi[:, :], op=AL.mult).inc("DV", ev=tag + "_qv")
        vdr()

    def emit_ar(idx, tag):
        G.wait(tag + "_arin")
        G.dma_start(out=cbi[idx], in_=ario[0:1, :]).inc("gARB", ev=tag + "_cbi")
        G.wait(tag + "_cbi")
        G.collective_compute("AllReduce", AL.add, replica_groups=[list(range(NCORES))],
                             ins=[cbi[idx].opt()], outs=[cbo[idx].opt()]).inc("CC", ev=tag + "_cc")
        G.wait(tag + "_cc")
        G.dma_start(out=ario[0:1, :], in_=cbo[idx]).inc("gARB", ev=tag + "_arout")

    def emit_stats_to_ario(tag):
        cross_sum(qvec[:, 0:2], 2, tag + "_cs", wait_ev=tag + "_qv")
        A.wait(tag + "_cs")
        A.copy(out=ario[0:1, 0:2], in_=PS[0:1, 7 * 512:7 * 512 + 2]).inc("AC", ev=tag + "_arin")
        adr()

    def emit_thresholds(tag, nrows):
        V.wait(tag + "_arout")
        V.tensor_single_scalar(out=rowa[0:1, 0:1], in_=ario[0:1, 0:1], scalar=1.0 / nrows, op=AL.mult).inc("DV")
        V.tensor_single_scalar(out=rowa[0:1, 1:2], in_=ario[0:1, 1:2], scalar=1.0 / nrows, op=AL.mult).inc("DV")
        vdr()
        V.tensor_tensor(out=rowa[0:1, 2:3], in0=rowa[0:1, 0:1], in1=rowa[0:1, 0:1], op=AL.mult).inc("DV")
        vdr()
        V.tensor_tensor(out=rowa[0:1, 3:4], in0=rowa[0:1, 1:2], in1=rowa[0:1, 2:3], op=AL.subtract).inc("DV")
        vdr()
        V.tensor_scalar(out=rowa[0:1, 4:5], in0=rowa[0:1, 3:4], scalar1=1.0 / nrows,
                        scalar2=1e-32, op0=AL.mult, op1=AL.add).inc("DV", ev=tag + "_var")
        vdr()
        A.wait(tag + "_var")
        A.activation(out=rowa[0:1, 5:6], in_=rowa[0:1, 4:5], func=AF.Ln).inc("AC")
        adr()
        A.activation(out=rowa[0:1, 6:7], in_=rowa[0:1, 5:6], func=AF.Exp, scale=cval[0:1, 4:5])\
            .inc("AC", ev=tag + "_sqrt")
        V.wait(tag + "_sqrt")
        V.tensor_single_scalar(out=rowa[0:1, 6:7], in_=rowa[0:1, 6:7], scalar=1.6, op=AL.mult).inc("DV")
        vdr()
        for j, kk in enumerate([-14.0, -3.0, 3.0, 14.0]):
            V.tensor_single_scalar(out=rowb[0:1, j:j + 1], in_=rowa[0:1, 6:7], scalar=kk, op=AL.mult).inc("DV")
        vdr()
        for j in range(4):
            V.tensor_tensor(out=rowb[0:1, j:j + 1], in0=rowb[0:1, j:j + 1], in1=rowa[0:1, 0:1],
                            op=AL.add).inc("DV")
        vdr()
        V.tensor_single_scalar(out=rowb[0:1, 0:4], in_=rowb[0:1, 0:4], scalar=1e-20, op=AL.max)\
            .inc("DV", ev=tag + "_thr")
        vdr()
        replicate(rowb[0:1, 0:4], thr4[:, 0:4], 4, tag + "_rep", wait_ev=tag + "_thr")

    def emit_counts_finish(tag, nchunks):
        for j in range(4):
            V.tensor_reduce(out=cnt4[:, j:j + 1], in_=cntall[:, j * 64:j * 64 + nchunks],
                            axis=AX, op=AL.add).inc("DV")
        vdr()
        _e = V.tensor_copy(out=cnt4[:, 0:1], in_=cnt4[:, 0:1])
        _e.inc("DV", ev=tag + "_cnt4")
        vdr()
        cross_sum(cnt4[:, 0:4], 4, tag + "_ccs", wait_ev=tag + "_cnt4")
        A.wait(tag + "_ccs")
        A.copy(out=ario[0:1, 0:4], in_=PS[0:1, 7 * 512:7 * 512 + 4]).inc("AC", ev=tag + "_arin")
        adr()

    def emit_interp(tag, gstar, ntot, out_col, gamma_col):
        V.wait(tag + "_arout")
        V.memset(rowa[0:1, 8:9], 0.0)
        V.memset(rowa[0:1, 9:10], float(ntot))
        V.tensor_copy(out=rowa[0:1, 10:11], in_=rowb[0:1, 0:1]).inc("DV")
        V.tensor_copy(out=rowa[0:1, 11:12], in_=ario[0:1, 0:1]).inc("DV")
        vdr()
        for i in range(3):
            V.tensor_single_scalar(out=qmsk[0:1, 0:1], in_=ario[0:1, i:i + 1], scalar=gstar,
                                   op=AL.is_ge).inc("DV")
            vdr()
            V.copy_predicated(out=rowa[0:1, 8:9], mask=qmsk[0:1, 0:1], data=rowb[0:1, i:i + 1]).inc("DV")
            V.copy_predicated(out=rowa[0:1, 9:10], mask=qmsk[0:1, 0:1], data=ario[0:1, i:i + 1]).inc("DV")
            V.copy_predicated(out=rowa[0:1, 10:11], mask=qmsk[0:1, 0:1], data=rowb[0:1, i + 1:i + 2]).inc("DV")
            V.copy_predicated(out=rowa[0:1, 11:12], mask=qmsk[0:1, 0:1], data=ario[0:1, i + 1:i + 2]).inc("DV")
            vdr()
        V.tensor_tensor(out=rowa[0:1, 12:13], in0=rowa[0:1, 9:10], in1=rowa[0:1, 11:12], op=AL.subtract).inc("DV")
        V.tensor_single_scalar(out=rowa[0:1, 13:14], in_=rowa[0:1, 9:10], scalar=gstar, op=AL.subtract).inc("DV")
        V.tensor_tensor(out=rowa[0:1, 14:15], in0=rowa[0:1, 10:11], in1=rowa[0:1, 8:9], op=AL.subtract).inc("DV")
        vdr()
        V.tensor_single_scalar(out=rowa[0:1, 12:13], in_=rowa[0:1, 12:13], scalar=0.5, op=AL.max).inc("DV")
        vdr()
        V.reciprocal(out=rowa[0:1, 12:13], in_=rowa[0:1, 12:13]).inc("DV")
        vdr()
        V.tensor_tensor(out=rowa[0:1, 13:14], in0=rowa[0:1, 13:14], in1=rowa[0:1, 12:13], op=AL.mult).inc("DV")
        vdr()
        V.tensor_tensor(out=rowa[0:1, 13:14], in0=rowa[0:1, 13:14], in1=rowa[0:1, 14:15], op=AL.mult).inc("DV")
        vdr()
        V.tensor_tensor(out=rowa[0:1, 13:14], in0=rowa[0:1, 13:14], in1=rowa[0:1, 8:9], op=AL.add).inc("DV")
        vdr()
        V.tensor_single_scalar(out=rowa[0:1, 13:14], in_=rowa[0:1, 13:14], scalar=1e-5, op=AL.max).inc("DV")
        vdr()
        V.reciprocal(out=rowa[0:1, 14:15], in_=rowa[0:1, 13:14]).inc("DV")
        vdr()
        V.tensor_single_scalar(out=rowb[0:1, 8:9], in_=rowa[0:1, 14:15], scalar=127.0, op=AL.mult).inc("DV")
        V.tensor_single_scalar(out=rowb[0:1, 9:10], in_=rowa[0:1, 13:14], scalar=1.0 / 127.0, op=AL.mult).inc("DV")
        vdr()
        V.tensor_tensor(out=rowb[0:1, 9:10], in0=rowb[0:1, 9:10], in1=rowb[0:1, gamma_col:gamma_col + 1],
                        op=AL.mult).inc("DV", ev=tag + "_sca")
        vdr()
        replicate(rowb[0:1, 8:10], sca[:, out_col:out_col + 2], 2, tag + "_screp", wait_ev=tag + "_sca")

    def finish_debug():
        V.memset(junk[:, :], 0.0)
        vdr()
        _e = V.tensor_copy(out=junk[:, 0:1024], in_=junk[:, 0:1024])
        _e.inc("DV", ev="dbg_fin")
        G.wait("dbg_fin")
        G.dma_start(out=dout[0:128, :], in_=junk[:, 0:1024]).inc("gOUT", ev="dbg_out")
        G.wait("dbg_out")

    def program():
        for dst, src in [(cosT[:, :], dcos), (sinT[:, :], dsin), (identT[:, :], dident),
                         (u2T[:, :], du2), (dnegT[:, :], ddneg), (onesrb[:, :], donesrb),
                         (ones64[:, :], dones64), (onescf[:, :], donescf), (onesrf[:, :], donesrf),
                         (cval[:, :], dcval), (bqpT[:, :], dbqp), (bkpT[:, :], dbkp),
                         (lngT[:, :], dlng), (lnbT[:, :], dlnb)]:
            SY.dma_start(out=dst, in_=src).inc("sCON")
        SY.dma_start(out=fa(F_STG, 1024, 0, 1), in_=dbv).inc("sCON")
        SY.dma_start(out=fa(F_X1G, 1024, 0, 1), in_=dbo).inc("sCON", ev="consts_ld")
        V.wait("consts_ld")
        V.memset(ario[0:1, :], 0.0)
        V.memset(dbgT[:, :], 0.0)
        V.tensor_copy(out=bvT[:, :], in_=fa(F_STG, 1024, 0, 1)).inc("DV")
        V.tensor_copy(out=boT[:, :], in_=fa(F_X1G, 1024, 0, 1)).inc("DV", ev="bias_cast")

        def emit_gamma_load(ci):
            if ci >= 64:
                return
            wi, cc = divmod(ci, 32)
            dsrc = dw1 if wi == 0 else dw2
            if ci >= 2:
                G.wait(f"gred_{ci - 2}")
            else:
                G.wait("vp_0_1_15")
            G.dma_start(out=bff(O_SCR + (ci % 2) * 2048, 1024),
                        in_=dsrc[:, cc * 1024:(cc + 1) * 1024])\
                .inc("sGA" if ci % 2 == 0 else "sGB", ev=f"gld_{ci}")

        def emit_gamma_red(ci):
            wi, cc = divmod(ci, 32)
            V.wait(f"gld_{ci}")
            V.tensor_reduce(out=gsum[:, 2 + cc:3 + cc],
                            in_=bff(O_SCR + (ci % 2) * 2048, 1024), axis=AX, op=AL.add,
                            apply_absolute_value=True).inc("DV", ev=f"gred_{ci}")
            if cc == 31:
                vdr()
                V.tensor_reduce(out=gsum[:, wi:wi + 1], in_=gsum[:, 2:34], axis=AX,
                                op=AL.add).inc("DV", ev=f"gtot_{wi}")
                vdr()

        def emit_gamma_finish():
            T.wait_if("x1p_0_15_1")
            cross_sum(gsum[:, 0:2], 2, "g_cs", wait_ev="gtot_1")
            A.wait("g_cs")
            A.copy(out=rowb[0:1, 10:12], in_=PS[0:1, 7 * 512:7 * 512 + 2]).inc("AC", ev="g_row")
            adr()
            V.wait("g_row")
            V.tensor_scalar(out=rowb[0:1, 12:13], in0=rowb[0:1, 10:11], scalar1=1.0 / (HID * D),
                            scalar2=1e-5, op0=AL.mult, op1=AL.add).inc("DV")
            V.tensor_scalar(out=rowb[0:1, 13:14], in0=rowb[0:1, 11:12], scalar1=1.0 / (HID * D),
                            scalar2=1e-5, op0=AL.mult, op1=AL.add).inc("DV")
            vdr()
            V.reciprocal(out=rowb[0:1, 14:15], in_=rowb[0:1, 12:13]).inc("DV")
            V.reciprocal(out=rowb[0:1, 15:16], in_=rowb[0:1, 13:14]).inc("DV", ev="ginv")
            vdr()
            replicate(rowb[0:1, 14:16], gq[:, 0:2], 2, "gq_rep", wait_ev="ginv")

        if stage == 11:
            finish_debug()
            return

        for b in range(BPC):
            for c8 in range(8):
                sl = c8 % 2
                if c8 >= 2:
                    SY.wait(f"xTc_{b}_{c8 - 2}")
                    G.wait(f"xTc_{b}_{c8 - 2}")
                elif b == 0:
                    SY.wait("bias_cast")
                    G.wait("bias_cast")
                elif b == 1:
                    SY.wait_if("x1p_0_15_1")
                    SY.wait_if("cxn_0_3_15")
                    G.wait_if("h0_0_13")
                    G.wait_if("x1s_0_12")
                    G.wait_if("x1s_0_13")
                    G.wait_if("o1_0_13")
                    G.wait_if("o16_0_15")
                xt_base = b * (128 * 8 * S) + c8 * S
                SY.dma_start(out=fa(F_STG + sl * 1024, 1024),
                             in_=bass.AP(dxT.tensor, xt_base, [[8 * S, 128], [1, 1024]]))\
                    .inc(("sXT0" if sl == 0 else "sXT1"), ev=f"xTlA_{b}_{c8}")
                G.dma_start(out=fa(F_X1G + sl * 1024, 1024),
                            in_=bass.AP(dxT.tensor, xt_base + 1024, [[8 * S, 128], [1, 1024]]))\
                    .inc(("sXT2" if sl == 0 else "sXT3"), ev=f"xTl_{b}_{c8}")
                A.wait(f"xTlA_{b}_{c8}")
                A.wait(f"xTl_{b}_{c8}")
                if b == 1 and c8 == 0:
                    A.wait_if("x1p_0_15_1")
                A.activation(out=bf(O_XTB + c8 * S, 1024), in_=fa(F_STG + sl * 1024, 1024),
                             func=AF.Copy).inc("AC")
                A.activation(out=bf(O_XTB + c8 * S + 1024, 1024), in_=fa(F_X1G + sl * 1024, 1024),
                             func=AF.Copy).inc("AC", ev=f"xTc_{b}_{c8}")

            for which, (dW, biasT, O_DST) in enumerate([(dWk, bkpT, O_KC), (dWq, bqpT, O_QC)]):
                for g4 in range(4):
                    for sub in range(2):
                        et = 2 * g4 + sub
                        wsl = et % 2
                        wsl4 = et % 4
                        wstg = (F_WST + wsl4 * 1024) if wsl4 < 2 else (F_XPA + (wsl4 - 2) * 1024)
                        if et >= 4:
                            SY.wait(f"wcast_{b}_{which}_{et - 4}")
                            G.wait(f"wcast_{b}_{which}_{et - 4}")
                        elif which == 1:
                            SY.wait(f"wcast_{b}_0_{4 + et}")
                            G.wait(f"wcast_{b}_0_{4 + et}")
                        elif b == 1:
                            if wsl4 < 2:
                                for eng_ in (SY, G):
                                    eng_.wait_if("h0s_0_15")
                                    eng_.wait_if("sub1_0_15")
                                    eng_.wait_if("o2_0_15")
                            else:
                                for eng_ in (SY, G):
                                    eng_.wait_if("x1s_0_14")
                                    eng_.wait_if("x1s_0_15")
                                    eng_.wait_if("h0_0_15")
                                    eng_.wait_if("o1_0_15")
                                    eng_.wait_if("o16_0_15")
                        for half, ldeng in ((0, SY), (1, G)):
                            ldeng.dma_start(out=fa(wstg + half * 512, 512),
                                            in_=bass.AP(dW.tensor, et * 1024 + half * 512,
                                                        [[8 * D, 128], [1, 512]]))\
                                .inc(f"sWS{wsl4 * 2 + half}", ev=f"wldq{half}_{b}_{which}_{et}")
                        A.wait(f"wldq0_{b}_{which}_{et}")
                        A.wait(f"wldq1_{b}_{which}_{et}")
                        if et >= 2:
                            A.wait(f"pj_{b}_{which}_{(et - 2) * 4 + 3}")
                        elif which == 1:
                            A.wait(f"pj_{b}_0_{(6 + et) * 4 + 3}")
                        A.activation(out=bf(O_WSL + wsl * 1024, 1024), in_=fa(wstg, 1024),
                                     func=AF.Copy).inc("AC", ev=f"wcast_{b}_{which}_{et}")
                        gi = which * 4 + g4
                        PNL = O_SCR if gi % 2 == 0 else O_RT + 8192
                        for half in range(4):
                            pb = (et * 4 + half) % 2
                            T.wait(f"wcast_{b}_{which}_{et}")
                            if et * 4 + half >= 2:
                                T.wait(f"pje_{b}_{which}_{et * 4 + half - 2}")
                            for dc in range(8):
                                T.matmul(bk(pb), bf(O_WSL + wsl * 1024 + dc * 128, 128),
                                         bf(O_XTB + dc * S + half * 512, 512),
                                         start=(dc == 0), stop=(dc == 7))\
                                    .inc("PE", ev=(f"pj_{b}_{which}_{et * 4 + half}" if dc == 7 else None))
                            A.wait(f"pj_{b}_{which}_{et * 4 + half}")
                            if sub == 0 and half == 0:
                                if gi >= 2:
                                    wp, gp = divmod(gi - 2, 4)
                                    A.wait(f"perm_{b}_{wp}_{gp}_1")
                                    A.wait(f"perm_{b}_{wp}_{gp}_3")
                                elif b == 1 and gi == 0:
                                    A.wait_if("gred_63")
                            A.activation(out=bf(PNL + sub * 2048 + half * 512, 512),
                                         in_=bk(pb), func=AF.Identity, bias=biasT[:, et:et + 1])\
                                .inc("AC", ev=f"pje_{b}_{which}_{et * 4 + half}")
                    V.wait(f"pje_{b}_{which}_{(2 * g4 + 1) * 4 + 3}")
                    vdr()
                    x1 = bf(PNL, 2048)
                    x2 = bf(PNL + 2048, 2048)
                    tA = bf(O_RT, 2048)
                    tB = bf(O_RT + 2048, 2048)
                    tC = bf(O_RT + 4096, 2048)
                    tD = bf(O_RT + 6144, 2048)
                    V.tensor_tensor(out=tA, in0=x1, in1=cosT[:, :], op=AL.mult).inc("DV")
                    V.tensor_tensor(out=tB, in0=x2, in1=sinT[:, :], op=AL.mult).inc("DV")
                    V.tensor_tensor(out=tC, in0=x1, in1=sinT[:, :], op=AL.mult).inc("DV")
                    V.tensor_tensor(out=tD, in0=x2, in1=cosT[:, :], op=AL.mult).inc("DV")
                    vdr()
                    V.tensor_tensor(out=x1, in0=tA, in1=tB, op=AL.subtract).inc("DV")
                    V.tensor_tensor(out=x2, in0=tC, in1=tD, op=AL.add).inc("DV", ev=f"rope_{b}_{which}_{g4}")
                    SY.wait(f"rope_{b}_{which}_{g4}")
                    G.wait(f"rope_{b}_{which}_{g4}")
                    for j in range(4):
                        h = 4 * g4 + j
                        pc = h // 2
                        po = (h % 2) * 64
                        if j < 2:
                            peng, pstr = SY, ("gPRM" if gi % 2 == 0 else "gPR2")
                        else:
                            peng, pstr = G, ("aPRM" if gi % 2 == 0 else "aPR2")
                        peng.dma_start(out=bf(O_DST + pc * S, S, po, 32),
                                       in_=bf(PNL, S, j * 32, 32)).inc(pstr)
                        peng.dma_start(out=bf(O_DST + pc * S, S, po + 32, 32),
                                       in_=bf(PNL + 2048, S, j * 32, 32))\
                            .inc(pstr, ev=f"perm_{b}_{which}_{g4}_{j}")

            for ec in range(2):
                if ec == 0:
                    A.wait(f"xTc_{b}_6")
                    A.wait(f"xTc_{b}_7")
                else:
                    A.wait(f"wvc_{b}_0")
                for half in range(2):
                    A.dma_start(out=fa(F_STG + half * 1024, 1024),
                                in_=bass.AP(dWv.tensor, ec * 512 + half * 2 * D,
                                            [[8 * D, 128], [D, 2], [1, 512]]))\
                        .inc("sWV" if half == 0 else "sWV2", ev=f"wvlA{half}_{b}_{ec}")
                    A.dma_start(out=fa(F_X1G + half * 1024, 1024),
                                in_=bass.AP(dWv.tensor, 4 * D + ec * 512 + half * 2 * D,
                                            [[8 * D, 128], [D, 2], [1, 512]]))\
                        .inc("sWV3" if half == 0 else "sWV4", ev=f"wvlB{half}_{b}_{ec}")
                for evn in (f"wvlA0_{b}_{ec}", f"wvlA1_{b}_{ec}",
                            f"wvlB0_{b}_{ec}", f"wvlB1_{b}_{ec}"):
                    A.wait(evn)
                if ec == 0:
                    A.wait(f"perm_{b}_1_2_1")
                    A.wait(f"perm_{b}_1_2_3")
                else:
                    A.wait(f"vpe_{b}_0_15")
                A.activation(out=bf(O_SCR, 2048), in_=fa(F_STG, 2048), func=AF.Copy).inc("AC")
                A.activation(out=bf(O_SCR + 2048, 2048), in_=fa(F_X1G, 2048), func=AF.Copy)\
                    .inc("AC", ev=f"wvc_{b}_{ec}")
                for tt in range(16):
                    pb = tt % 2
                    T.wait(f"wvc_{b}_{ec}")
                    if tt >= 2:
                        T.wait(f"vpe_{b}_{ec}_{tt - 2}")
                    for dc in range(8):
                        T.matmul(bk(pb), bf(O_XTB + dc * S + tt * 128, 128),
                                 bf(O_SCR + dc * 512, 512),
                                 start=(dc == 0), stop=False).inc("PE")
                    T.matmul(bk(pb), onesrb[0:1, :], bvT[0:1, ec * 512:(ec + 1) * 512],
                             start=False, stop=True).inc("PE", ev=f"vp_{b}_{ec}_{tt}")
                    V.wait(f"vp_{b}_{ec}_{tt}")
                    if tt == 0 and ec == 0:
                        vdr()
                    if tt == 8:
                        V.wait(f"perm_{b}_1_3_1")
                        V.wait(f"perm_{b}_1_3_3")
                    V.tensor_copy(out=bf(O_VN + tt * 1024 + ec * 512, 512), in_=bk(pb))\
                        .inc("DV", ev=f"vpe_{b}_{ec}_{tt}")

            for c8 in range(8):
                sl = c8 % 2
                if c8 >= 2:
                    A.wait(f"woc_{b}_{c8 - 2}")
                else:
                    A.wait(f"wvc_{b}_1")
                for half in range(2):
                    A.dma_start(out=fa(F_STG + sl * 1024 + half * 512, 512),
                                in_=dWo[:, c8 * 1024 + half * 512: c8 * 1024 + half * 512 + 512])\
                        .inc(f"sWO{sl * 2 + half}", ev=f"wol{half}_{b}_{c8}")
                A.wait(f"wol0_{b}_{c8}")
                A.wait(f"wol1_{b}_{c8}")
                if c8 == 0:
                    A.wait(f"vpe_{b}_1_15")
                A.activation(out=bf(O_WO + c8 * 1024, 1024), in_=fa(F_STG + sl * 1024, 1024),
                             func=AF.Copy).inc("AC", ev=f"woc_{b}_{c8}")
            T.wait("woc_%d_7" % b)

            if stage == 12:
                continue
            pend = []
            if b == 0:
                emit_gamma_load(0)
                emit_gamma_load(1)
            for qq in range(4):
                if qq > 0:
                    T.wait(f"x1p_{b}_{qq * 4 - 2}_0")
                else:
                    T.wait(f"perm_{b}_1_3_1")
                    T.wait(f"perm_{b}_1_3_3")
                    if b == 1:
                        T.wait("x1p_0_14_0")
                for h in range(H):
                    if b == 0:
                        emit_gamma_load(qq * 16 + h + 2)
                        emit_gamma_red(qq * 16 + h)
                    if pend and h >= 1:
                        pend.pop(0)()
                        if pend and h <= 5:
                            pend.pop(0)()
                    pc = h // 2
                    po = (h % 2) * 64
                    cb = 4 + 2 * (h % 2)
                    nkt = 4 * qq + 4

                    def emit_scores(kt):
                        c0 = max(kt * 128 - qq * 512, 0)
                        sb_ = kt % 2
                        if kt >= 2:
                            T.wait(f"exp_{b}_{qq}_{h}_{kt - 2}")
                        T.matmul(bk(sb_, c0, 512 - c0),
                                 bf(O_KC + pc * S + kt * 128, 128, po, 64),
                                 bf(O_QC + pc * S + qq * 512 + c0, 512 - c0, po, 64),
                                 start=True, stop=(kt < 4 * qq)).inc("PE")
                        if kt >= 4 * qq:
                            T.matmul(bk(sb_, c0, 128), dnegT[:, :], u2T[:, :],
                                     start=False, stop=True).inc("PE")
                        K.event(f"sc_{b}_{qq}_{h}_{kt}", "PE")

                    def emit_exp(kt):
                        c0 = max(kt * 128 - qq * 512, 0)
                        sb_ = kt % 2
                        A.wait(f"sc_{b}_{qq}_{h}_{kt}")
                        if kt >= 2:
                            A.wait(f"cx_{b}_{qq}_{h}_{kt - 2}")
                        A.activation(out=bf(O_PT + sb_ * 1024 + c0, 512 - c0),
                                     in_=bk(sb_, c0, 512 - c0),
                                     func=AF.Exp, bias=cval[:, 0:1], scale=cval[:, 1:2])\
                            .inc("AC", ev=f"exp_{b}_{qq}_{h}_{kt}")

                    def emit_ctx(kt):
                        c0 = max(kt * 128 - qq * 512, 0)
                        sb_ = kt % 2
                        T.wait(f"exp_{b}_{qq}_{h}_{kt}")
                        if kt == 0 and h < 2:
                            if qq > 0:
                                T.wait(f"x1p_{b}_{qq * 4 - 1}_{h}")
                            elif b == 1:
                                T.wait(f"x1p_0_15_{h}")
                        if kt == 0 and (qq > 0 or h >= 2):
                            ph = h - 2 if h >= 2 else H - 2 + h
                            pq = qq if h >= 2 else qq - 1
                            T.wait(f"cxn_{b}_{pq}_{ph}")
                        if b == 1 and qq == 0 and h == 1 and kt == 0:
                            T.wait_if("gq_rep")
                            T.wait_if("g_row")
                        T.matmul(PS[0:64, cb * 512 + c0: cb * 512 + 512],
                                 bf(O_VN + kt * 1024 + h * 64, 64),
                                 bf(O_PT + sb_ * 1024 + c0, 512 - c0),
                                 start=(kt == 0), stop=(kt == nkt - 1)).inc("PE")
                        T.matmul(PS[0:64, (cb + 1) * 512 + c0: (cb + 1) * 512 + 512],
                                 ones64[:, 0:64],
                                 bf(O_PT + sb_ * 1024 + c0, 512 - c0),
                                 start=(kt == 0), stop=(kt == nkt - 1)).inc("PE", ev=f"cx_{b}_{qq}_{h}_{kt}")

                    for kt in range(nkt):
                        emit_scores(kt)
                        emit_exp(kt)
                        if kt > 0:
                            emit_ctx(kt - 1)
                    emit_ctx(nkt - 1)
                    rp = (h % 2) * 64
                    A.wait(f"cx_{b}_{qq}_{h}_{nkt - 1}")
                    A.activation(out=fa(F_STG, 512, rp, 64),
                                 in_=PS[0:64, (cb + 1) * 512:(cb + 1) * 512 + 512],
                                 func=AF.Ln).inc("AC")
                    adr()
                    A.activation(out=fa(F_STG + 512, 512, rp, 64), in_=fa(F_STG, 512, rp, 64),
                                 func=AF.Exp, scale=cval[rp:rp + 64, 3:4]).inc("AC", ev=f"rcp_{b}_{qq}_{h}")
                    V.wait(f"rcp_{b}_{qq}_{h}")
                    V.tensor_tensor(out=bf(O_CXN + pc * 512, 512, po, 64),
                                    in0=PS[0:64, cb * 512:cb * 512 + 512],
                                    in1=fa(F_STG + 512, 512, rp, 64), op=AL.mult)\
                        .inc("DV", ev=f"cxn_{b}_{qq}_{h}")
                if stage == 13:
                    continue
                T.wait(f"cxn_{b}_{qq}_{H - 1}")
                T.wait(f"cxn_{b}_{qq}_{H - 2}")
                for ti in range(4):
                    tt = qq * 4 + ti
                    xrs_off = F_XRS if tt % 2 == 0 else F_RCP
                    slo = xsl(tt)
                    if True:
                        if tt >= 2:
                            SY.wait_if(f"x1p_{b}_{tt - 2}_1")
                        elif tt == 1:
                            SY.wait(f"woc_{b}_7")
                        elif b == 1:
                            SY.wait_if("x1p_0_14_1")
                        SY.dma_start(out=fa(xrs_off, 1024), in_=dx[(b * S + tt * 128):(b * S + (tt + 1) * 128), :])\
                            .inc(("sXR0" if tt % 2 == 0 else "sXR1"), ev=f"xres_{b}_{tt}")
                    for ec in range(2):
                        pbA = (ti % 2) * 4 + ec * 2
                        pbB = pbA + 1
                        if tt >= 2:
                            T.wait(f"x1p_{b}_{tt - 2}_{ec}")
                        elif b == 1:
                            T.wait(f"x1p_0_{14 + tt}_{ec}")
                        for h in range(0, H, 2):
                            T.matmul(bk(pbA), bf(O_CXN + (h // 2) * 512 + ti * 128, 128, 0, 64),
                                     bf(O_WO + (h // 2) * 1024 + ec * 512, 512, 0, 64),
                                     start=(h == 0), stop=False).inc("PE")
                        T.matmul(bk(pbA), onesrb[0:1, :], boT[0:1, ec * 512:(ec + 1) * 512],
                                 start=False, stop=True).inc("PE")
                        for h in range(1, H, 2):
                            T.matmul(bk(pbB), bf(O_CXN + (h // 2) * 512 + ti * 128, 128, 64, 64),
                                     bf(O_WO + (h // 2) * 1024 + ec * 512, 512, 64, 64),
                                     start=(h == 1), stop=(h == H - 1)).inc("PE")
                        K.event(f"wo_{b}_{tt}_{ec}", "PE")
                        V.wait(f"wo_{b}_{tt}_{ec}")
                        if tt >= 4:
                            V.wait_if(f"x1s_{b}_{tt - 4}")
                            V.wait_if(f"h0_{b}_{tt - 4}")
                            V.wait_if(f"o1_{b}_{(tt - 4) | 1}")
                            V.wait_if(f"o16_{b}_{(tt - 4) | 3}")
                        elif b == 1:
                            V.wait_if(f"x1s_0_{12 + tt}")
                            V.wait_if(f"h0_0_{12 + tt}")
                            V.wait_if(f"o1_0_{(12 + tt) | 1}")
                            V.wait_if("o16_0_15")
                        V.wait(f"xres_{b}_{tt}")
                        if stage == 16:
                            V.wait_if(f"o16_{b}_{tt - 1}")
                        V.tensor_copy(out=fa(slo + ec * 512, 512), in_=bk(pbA)).inc("DV")
                        vdr()
                        V.tensor_tensor(out=fa(slo + ec * 512, 512),
                                        in0=fa(slo + ec * 512, 512),
                                        in1=bk(pbB), op=AL.add).inc("DV")
                        vdr()
                        V.tensor_tensor(out=fa(slo + ec * 512, 512),
                                        in0=fa(slo + ec * 512, 512),
                                        in1=fa(xrs_off + ec * 512, 512), op=AL.add)\
                            .inc("DV", ev=f"x1p_{b}_{tt}_{ec}")
                    if stage == 16:
                        SY.wait(f"x1p_{b}_{tt}_1")
                        SY.dma_start(out=dout[(b * S + tt * 128):(b * S + (tt + 1) * 128), :],
                                     in_=fa(slo, 1024)).inc("sDBG", ev=f"o16_{b}_{tt}")
                        continue
                if stage in (14, 16):
                    continue

                def mk_ln1(b_=b, qq_=qq, grp_=0):
                    base = F_X1G if grp_ == 0 else F_XPA

                    def s1():
                        vdr()
                        for k2 in range(2):
                            V.tensor_scalar(out=junk[:, k2 * 1024:k2 * 1024 + 1024],
                                            in0=fa(base + k2 * 1024, 1024),
                                            scalar1=0.0, scalar2=None, op0=AL.add, op1=AL.add,
                                            accum_out=stat[:, k2:k2 + 1]).inc("DV")
                        vdr()
                        V.tensor_single_scalar(out=stat[:, 2:4], in_=stat[:, 0:2], scalar=1.0 / D,
                                               op=AL.mult).inc("DV")
                        vdr()

                    def s2():
                        for k2 in range(2):
                            V.tensor_scalar(out=fa(base + k2 * 1024, 1024),
                                            in0=fa(base + k2 * 1024, 1024),
                                            scalar1=stat[:, 2 + k2:3 + k2], scalar2=None,
                                            op0=AL.subtract, op1=AL.bypass).inc("DV")
                        vdr()

                    def s3():
                        for k2 in range(2):
                            V.scalar_tensor_tensor(out=junk[:, k2 * 1024:k2 * 1024 + 1024],
                                                   in0=fa(base + k2 * 1024, 1024),
                                                   scalar=0.0, in1=fa(base + k2 * 1024, 1024),
                                                   op0=AL.add, op1=AL.mult,
                                                   accum_out=stat[:, 4 + k2:5 + k2]).inc("DV")
                        vdr()
                        V.tensor_scalar(out=stat[:, 6:8], in0=stat[:, 4:6], scalar1=1.0 / D,
                                        scalar2=EPS_LN, op0=AL.mult, op1=AL.add)\
                            .inc("DV", ev=f"v1_{b_}_{qq_}_{grp_}")
                        A.wait(f"v1_{b_}_{qq_}_{grp_}")
                        A.activation(out=stat[:, 8:10], in_=stat[:, 6:8], func=AF.Ln).inc("AC")
                        adr()
                        A.activation(out=stat[:, 6:8], in_=stat[:, 8:10], func=AF.Exp,
                                     scale=cval[:, 2:3]).inc("AC", ev=f"rs1_{b_}_{qq_}_{grp_}")

                    def mk_s4(k2):
                        def s4():
                            tt2 = qq_ * 4 + grp_ * 2 + k2
                            if k2 == 0:
                                V.wait(f"rs1_{b_}_{qq_}_{grp_}")
                            V.tensor_scalar(out=fa(base + k2 * 1024, 1024),
                                            in0=fa(base + k2 * 1024, 1024),
                                            scalar1=stat[:, 6 + k2:7 + k2], scalar2=None,
                                            op0=AL.mult, op1=AL.bypass).inc("DV", ev=f"x1_{b_}_{tt2}")
                            SY.wait(f"x1_{b_}_{tt2}")
                            SY.dma_start(out=dx1[(b_ * S + tt2 * 128):(b_ * S + (tt2 + 1) * 128), :],
                                         in_=fa(base + k2 * 1024, 1024))\
                                .inc(f"sX1S{tt2 % 4}", ev=f"x1s_{b_}_{tt2}")
                            if stage == 1:
                                SY.dma_start(out=dout[(b_ * S + tt2 * 128):(b_ * S + (tt2 + 1) * 128), :],
                                             in_=fa(base + k2 * 1024, 1024)).inc("sDBG", ev=f"o1_{b_}_{tt2}")
                        return s4

                    return [s1, s2, s3, mk_s4(0), mk_s4(1)]

                def mk_ln2(b_=b, qq_=qq, grp_=0):
                    t0 = qq_ * 4 + grp_ * 2
                    base = F_X1G if grp_ == 0 else F_XPA

                    def t1():
                        if t0 >= 2:
                            V.wait_if(f"h0s_{b_}_{t0 - 1}")
                            V.wait_if(f"o2_{b_}_{t0 - 1}")
                        elif b_ == 1:
                            V.wait_if("h0s_0_15")
                            V.wait_if("o2_0_15")
                        vdr()
                        for k2 in range(2):
                            V.tensor_scalar(out=junk[:, k2 * 1024:k2 * 1024 + 1024],
                                            in0=fa(base + k2 * 1024, 1024),
                                            scalar1=0.0, scalar2=None, op0=AL.add, op1=AL.add,
                                            accum_out=stat[:, k2:k2 + 1]).inc("DV")
                        vdr()
                        V.tensor_single_scalar(out=stat[:, 2:4], in_=stat[:, 0:2], scalar=1.0 / D,
                                               op=AL.mult).inc("DV")
                        vdr()

                    def t2():
                        for k2 in range(2):
                            V.tensor_scalar(out=fa(F_H0G + k2 * 1024, 1024),
                                            in0=fa(base + k2 * 1024, 1024),
                                            scalar1=stat[:, 2 + k2:3 + k2], scalar2=None,
                                            op0=AL.subtract, op1=AL.bypass).inc("DV")
                        vdr()

                    def t3():
                        for k2 in range(2):
                            V.scalar_tensor_tensor(out=junk[:, k2 * 1024:k2 * 1024 + 1024],
                                                   in0=fa(F_H0G + k2 * 1024, 1024),
                                                   scalar=0.0, in1=fa(F_H0G + k2 * 1024, 1024),
                                                   op0=AL.add, op1=AL.mult,
                                                   accum_out=stat[:, 4 + k2:5 + k2]).inc("DV")
                        vdr()
                        V.tensor_scalar(out=stat[:, 6:8], in0=stat[:, 4:6], scalar1=1.0 / D,
                                        scalar2=EPS_LN, op0=AL.mult, op1=AL.add)\
                            .inc("DV", ev=f"v2_{b_}_{qq_}_{grp_}")
                        A.wait(f"v2_{b_}_{qq_}_{grp_}")
                        A.activation(out=stat[:, 8:10], in_=stat[:, 6:8], func=AF.Ln).inc("AC")
                        adr()
                        A.activation(out=stat[:, 6:8], in_=stat[:, 8:10], func=AF.Exp,
                                     scale=cval[:, 2:3]).inc("AC", ev=f"rs2_{b_}_{qq_}_{grp_}")

                    def mk_t4(k2):
                        def t4():
                            tt2 = qq_ * 4 + grp_ * 2 + k2
                            if k2 == 0:
                                V.wait(f"rs2_{b_}_{qq_}_{grp_}")
                            V.tensor_scalar(out=fa(F_H0G + k2 * 1024, 1024),
                                            in0=fa(F_H0G + k2 * 1024, 1024),
                                            scalar1=stat[:, 6 + k2:7 + k2], scalar2=None,
                                            op0=AL.mult, op1=AL.bypass).inc("DV")
                            vdr()
                            V.tensor_tensor(out=fa(F_H0G + k2 * 1024, 1024),
                                            in0=fa(F_H0G + k2 * 1024, 1024), in1=lngT[:, :],
                                            op=AL.mult).inc("DV")
                            vdr()
                            V.tensor_tensor(out=fa(F_H0G + k2 * 1024, 1024),
                                            in0=fa(F_H0G + k2 * 1024, 1024), in1=lnbT[:, :],
                                            op=AL.add).inc("DV", ev=f"h0_{b_}_{tt2}")
                            vdr()
                            SY.wait(f"h0_{b_}_{tt2}")
                            SY.dma_start(out=dh0[(b_ * S + tt2 * 128):(b_ * S + (tt2 + 1) * 128), :],
                                         in_=fa(F_H0G + k2 * 1024, 1024)).inc("sH0S", ev=f"h0s_{b_}_{tt2}")
                            if stage == 2:
                                SY.dma_start(out=dout[(b_ * S + tt2 * 128):(b_ * S + (tt2 + 1) * 128), :],
                                             in_=fa(F_H0G + k2 * 1024, 1024)).inc("sDBG", ev=f"o2_{b_}_{tt2}")
                            gt = b_ * 16 + tt2
                            V.tensor_copy(out=S1[:, gt * 32:(gt + 1) * 32],
                                          in_=bass.AP(FA[:, :].tensor, F_H0G + k2 * 1024,
                                                      [[FAW, 128], [32, 32]]))\
                                .inc("DV", ev=f"sub1_{b_}_{tt2}")
                        return t4

                    return [t1, t2, t3, mk_t4(0), mk_t4(1)]

                assert not pend
                pend.extend(mk_ln1(grp_=0))
                if stage != 15:
                    pend.extend(mk_ln2(grp_=0))
                pend.extend(mk_ln1(grp_=1))
                if stage != 15:
                    pend.extend(mk_ln2(grp_=1))
                if qq == 3:
                    while pend:
                        pend.pop(0)()

            if b == 0 and stage != 12:
                emit_gamma_finish()

        if stage in (12, 13, 14, 15, 16):
            if stage == 16:
                G.wait(f"o16_{BPC - 1}_{15}")
                return
            finish_debug()
            return
        if stage <= 2:
            if stage == 1:
                G.wait(f"o1_{BPC - 1}_{15}")
            else:
                G.wait(f"o2_{BPC - 1}_{15}")
            return

        V.wait(f"sub1_{BPC - 1}_{15}")
        emit_bisect(S1, 1024, 5, "q1")
        emit_stats_to_ario("q1")
        emit_ar(0, "q1")
        emit_thresholds("q1", 1024)
        for tt in range(NT):
            sl = tt % 2
            if tt >= 2:
                SY.wait(f"q1ct_{tt - 2}")
            elif tt == 0:
                SY.wait_if("cxn_1_3_15")
                SY.wait_if("h0s_1_15")
            else:
                SY.wait_if("x1p_1_15_1")
            SY.dma_start(out=fa(F_STG + sl * 1024, 1024), in_=dh0[tt * 128:(tt + 1) * 128, :])\
                .inc(("sQ1a" if sl == 0 else "sQ1b"), ev=f"q1l_{tt}")
            A.wait(f"q1l_{tt}")
            if tt == 0:
                A.wait("q1_rep")
                A.wait_if("h0s_1_15")
            elif tt >= 2:
                A.wait(f"q1ct_{tt - 2}")
            A.activation(out=fa(F_H0G + sl * 1024, 1024), in_=fa(F_STG + sl * 1024, 1024), func=AF.Abs)\
                .inc("AC", ev=f"q1a_{tt}")
            V.wait(f"q1a_{tt}")
            for j in range(4):
                V.tensor_scalar(out=bf(O_XQT + (j + 4 * sl) * 1024, 1024), in0=fa(F_H0G + sl * 1024, 1024),
                                scalar1=thr4[:, j:j + 1],
                                scalar2=None, op0=AL.is_gt, op1=AL.add,
                                accum_out=cntall[:, j * 64 + tt:j * 64 + tt + 1])\
                    .inc("DV", ev=(f"q1ct_{tt}" if j == 3 else None))
        vdr()
        emit_counts_finish("q1c", NT)
        emit_ar(1, "q1c")
        emit_interp("q1c", G1_STAR, N1, 0, 12)
        V.tensor_copy(out=dbgT[:, 0:4], in_=thr4[:, :]).inc("DV")
        V.tensor_copy(out=dbgT[:, 4:8], in_=cnt4[:, :]).inc("DV")
        V.tensor_copy(out=dbgT[:, 8:10], in_=sca[:, 0:2]).inc("DV")
        V.tensor_copy(out=dbgT[:, 10:12], in_=gq[:, 0:2]).inc("DV")
        V.tensor_copy(out=dbgT[0:1, 12:20], in_=ario[0:1, :]).inc("DV")

        if stage == 21:
            finish_debug()
            return
        for grp in range(8):
            for ti in range(4):
                tt = grp * 4 + ti
                sl = ti % 2
                if ti == 0 and grp >= 1:
                    V.wait(f"xqt_{grp - 1}_7")
                if tt >= 2:
                    SY.wait(f"xq_{tt - 2}")
                else:
                    SY.wait(f"q1ct_{30 + tt}")
                SY.dma_start(out=fa(F_STG + sl * 1024, 1024), in_=dh0[tt * 128:(tt + 1) * 128, :])\
                    .inc(("sXQa" if sl == 0 else "sXQb"), ev=f"xql_{tt}")
                V.wait(f"xql_{tt}")
                if tt == 0:
                    V.wait("q1c_screp")
                V.tensor_scalar(out=fa(F_H0G, 1024), in0=fa(F_STG + sl * 1024, 1024),
                                scalar1=sca[:, 0:1], scalar2=MAGIC, op0=AL.mult, op1=AL.add).inc("DV")
                vdr()
                V.tensor_scalar(out=fa(F_H0G, 1024), in0=fa(F_H0G, 1024),
                                scalar1=MAGIC, scalar2=-128.0, op0=AL.subtract, op1=AL.max).inc("DV")
                vdr()
                V.tensor_single_scalar(out=bf(O_XQG + ti * 1024, 1024), in_=fa(F_H0G, 1024),
                                       scalar=127.0, op=AL.min).inc("DV", ev=f"xq_{tt}")
                vdr()
            PSB = PS[:, :].bitcast(BF16)
            for dc in range(8):
                pb = dc % 2
                for ti in range(4):
                    T.wait(f"xq_{grp * 4 + ti}")
                if dc >= 2:
                    T.wait(f"xqe_{grp}_{dc - 2}")
                for ti in range(4):
                    T.transpose(PSB[:, pb * 1024 + ti * 128: pb * 1024 + (ti + 1) * 128],
                                bf(O_XQG + ti * 1024 + dc * 128, 128),
                                identT[:, :]).inc("PE", ev=(f"xqt_{grp}_{dc}" if ti == 3 else None))
                V.wait(f"xqt_{grp}_{dc}")
                V.tensor_copy(out=bf(O_XQT + dc * 4096 + grp * 512, 512),
                              in_=PSB[:, pb * 1024: pb * 1024 + 512])\
                    .inc("DV", ev=f"xqe_{grp}_{dc}")

        if stage == 22:
            finish_debug()
            return
        for ci in range(16):
            wsl = F_STG if ci % 2 == 0 else F_X1G
            if ci >= 2:
                SY.wait(f"w1q_{ci - 2}")
                G.wait(f"w1q_{ci - 2}")
            else:
                SY.wait("xq_31")
                G.wait("xq_31")
                for eng_ in (SY, G):
                    eng_.wait_if("x1s_1_12")
                    eng_.wait_if("x1s_1_13")
                    eng_.wait_if("h0_1_13")
                    eng_.wait_if("o1_1_13")
            SY.dma_start(out=fa(wsl, 1024), in_=dw1[:, ci * 2048:ci * 2048 + 1024])\
                .inc("sW1L" if ci % 2 == 0 else "sW1La", ev=f"w1lA_{ci}")
            G.dma_start(out=fa(wsl + 1024, 1024), in_=dw1[:, ci * 2048 + 1024:ci * 2048 + 2048])\
                .inc("sW1G" if ci % 2 == 0 else "sW1Gb", ev=f"w1l_{ci}")
            V.wait(f"w1lA_{ci}")
            V.wait(f"w1l_{ci}")
            if ci == 0:
                V.wait("gq_rep")
            V.tensor_scalar(out=fa(F_H0G, 1024), in0=fa(wsl, 1024), scalar1=gq[:, 0:1],
                            scalar2=MAGIC, op0=AL.mult, op1=AL.add).inc("DV")
            V.tensor_scalar(out=fa(F_H0G + 1024, 1024), in0=fa(wsl + 1024, 1024), scalar1=gq[:, 0:1],
                            scalar2=MAGIC, op0=AL.mult, op1=AL.add).inc("DV")
            vdr()
            V.tensor_scalar(out=fa(F_H0G, 2048), in0=fa(F_H0G, 2048),
                            scalar1=MAGIC, scalar2=-1.0, op0=AL.subtract, op1=AL.max).inc("DV")
            vdr()
            V.tensor_single_scalar(out=bf(O_W1Q + ci * 2048, 2048), in_=fa(F_H0G, 2048),
                                   scalar=1.0, op=AL.min).inc("DV", ev=f"w1q_{ci}")
            vdr()

        if stage == 23:
            finish_debug()
            return
        for jt in range(32):
            for tc in range(8):
                pb = (jt * 8 + tc) % 2
                if jt == 0 and tc == 0:
                    T.wait("w1q_15")
                    T.wait("xqe_7_7")
                if jt * 8 + tc >= 2:
                    T.wait(f"ge_{(jt * 8 + tc - 2) // 8}_{(jt * 8 + tc - 2) % 8}")
                for dc in range(8):
                    T.matmul(bk(pb), bf(O_W1Q + dc * 4096 + jt * 128, 128),
                             bf(O_XQT + dc * 4096 + tc * 512, 512),
                             start=(dc == 0), stop=(dc == 7)).inc("PE", ev=(f"m1_{jt}_{tc}" if dc == 7 else None))
                A.wait(f"m1_{jt}_{tc}")
                hsl = tc % 2
                if jt * 8 + tc >= 2:
                    A.wait(f"hs_{(jt * 8 + tc - 2) // 8}_{(jt * 8 + tc - 2) % 8}")
                    A.wait(f"sub2_{(jt * 8 + tc - 2) // 8}_{(jt * 8 + tc - 2) % 8}")
                else:
                    A.wait_if(f"x1s_1_{12 + hsl}")
                A.activation(out=fa(F_X1G + hsl * 1024, 512), in_=bk(pb), func=AF.Gelu,
                             scale=sca[:, 1:2]).inc("AC", ev=f"ge_{jt}_{tc}")
                speng = SY if hsl == 0 else G
                speng.wait(f"ge_{jt}_{tc}")
                speng.dma_start(out=bass.AP(dh.tensor, jt * (128 * TOK) + tc * 512, [[TOK, 128], [1, 512]]),
                                in_=fa(F_X1G + hsl * 1024, 512))\
                    .inc(("sHSa" if hsl == 0 else "sHSb"), ev=f"hs_{jt}_{tc}")
                V.wait(f"ge_{jt}_{tc}")
                V.tensor_copy(out=S2[:, jt * 32 + tc * 4: jt * 32 + tc * 4 + 4],
                              in_=bass.AP(FA[:, :].tensor, F_X1G + hsl * 1024, [[FAW, 128], [128, 4]]))\
                    .inc("DV", ev=f"sub2_{jt}_{tc}")

        if stage == 24:
            finish_debug()
            return
        V.wait("sub2_31_7")
        emit_bisect(S2, 1024, 5, "q2")
        emit_stats_to_ario("q2")
        emit_ar(2, "q2")
        emit_thresholds("q2", 1024)
        for pi in range(32):
            for hf in range(2):
                ii = pi * 2 + hf
                pr = ii % 2
                if ii >= 2:
                    SY.wait(f"q2ct_{ii - 2}")
                    G.wait(f"q2ct_{ii - 2}")
                elif ii == 0:
                    for eng_ in (SY, G):
                        eng_.wait("hs_31_6")
                        eng_.wait("hs_31_7")
                        eng_.wait("sub2_31_7")
                ldA = fa(F_STG, 1024) if pr == 0 else bff(0, 1024)
                ldB = fa(F_X1G, 1024) if pr == 0 else bff(2048, 1024)
                dst = fa(F_H0G, 2048) if pr == 0 else bff(4096, 2048)
                SY.dma_start(out=ldA,
                             in_=bass.AP(dh.tensor, pi * (128 * TOK) + hf * 2048, [[TOK, 128], [1, 1024]]))\
                    .inc("sQ2L" if pr == 0 else "sQ2c", ev=f"q2lA_{ii}")
                G.dma_start(out=ldB,
                            in_=bass.AP(dh.tensor, pi * (128 * TOK) + hf * 2048 + 1024, [[TOK, 128], [1, 1024]]))\
                    .inc("sQ2b" if pr == 0 else "sQ2d", ev=f"q2l_{ii}")
                A.wait(f"q2lA_{ii}")
                A.wait(f"q2l_{ii}")
                if ii == 0:
                    A.wait("q2_rep")
                elif ii >= 2:
                    A.wait(f"q2ct_{ii - 2}")
                A.activation(out=dst[:, 0:1024], in_=ldA, func=AF.Abs).inc("AC")
                A.activation(out=dst[:, 1024:2048], in_=ldB, func=AF.Abs).inc("AC", ev=f"q2a_{ii}")
                V.wait(f"q2a_{ii}")
                for j in range(4):
                    V.tensor_scalar(out=bf(16384 + (j + 4 * pr) * 2048, 2048), in0=dst,
                                    scalar1=thr4[:, j:j + 1],
                                    scalar2=None, op0=AL.is_gt, op1=AL.add,
                                    accum_out=cntall[:, j * 64 + ii:j * 64 + ii + 1])\
                        .inc("DV", ev=(f"q2ct_{ii}" if j == 3 else None))
        vdr()
        emit_counts_finish("q2c", 64)
        emit_ar(3, "q2c")
        emit_interp("q2c", G2_STAR, N2, 2, 13)
        V.tensor_copy(out=dbgT[:, 20:24], in_=thr4[:, :]).inc("DV")
        V.tensor_copy(out=dbgT[:, 24:28], in_=cnt4[:, :]).inc("DV")
        V.tensor_copy(out=dbgT[:, 28:30], in_=sca[:, 2:4]).inc("DV")
        V.tensor_copy(out=dbgT[0:1, 30:38], in_=ario[0:1, :]).inc("DV", ev="dbg_done")

        if stage == 25:
            finish_debug()
            return
        for ci in range(16):
            wsl = F_STG if ci % 2 == 0 else F_X1G
            if ci >= 2:
                SY.wait(f"w2q_{ci - 2}")
                G.wait(f"w2q_{ci - 2}")
            else:
                SY.wait("q2a_63")
                G.wait("q2a_63")
            SY.dma_start(out=fa(wsl, 1024), in_=dw2[:, ci * 2048:ci * 2048 + 1024])\
                .inc("sW2L" if ci % 2 == 0 else "sW2La", ev=f"w2lA_{ci}")
            G.dma_start(out=fa(wsl + 1024, 1024), in_=dw2[:, ci * 2048 + 1024:ci * 2048 + 2048])\
                .inc("sW2G" if ci % 2 == 0 else "sW2Gb", ev=f"w2l_{ci}")
            V.wait(f"w2lA_{ci}")
            V.wait(f"w2l_{ci}")
            if ci == 0:
                V.wait("q2c_screp")
            V.tensor_scalar(out=fa(F_H0G, 1024), in0=fa(wsl, 1024), scalar1=gq[:, 1:2],
                            scalar2=MAGIC, op0=AL.mult, op1=AL.add).inc("DV")
            V.tensor_scalar(out=fa(F_H0G + 1024, 1024), in0=fa(wsl + 1024, 1024), scalar1=gq[:, 1:2],
                            scalar2=MAGIC, op0=AL.mult, op1=AL.add).inc("DV")
            vdr()
            V.tensor_scalar(out=fa(F_H0G, 2048), in0=fa(F_H0G, 2048),
                            scalar1=MAGIC, scalar2=-1.0, op0=AL.subtract, op1=AL.max).inc("DV")
            vdr()
            V.tensor_single_scalar(out=bf(O_W2Q + ci * 2048, 2048), in_=fa(F_H0G, 2048),
                                   scalar=1.0, op=AL.min).inc("DV", ev=f"w2q_{ci}")
            vdr()

        if stage == 26:
            finish_debug()
            return
        for tb in range(32):
            pr = tb % 2
            hblk = fa(0, 4096) if pr == 0 else bff(40960, 4096)
            hqs = O_HQ + pr * 4096
            ogc = 4096 if pr == 0 else None
            ogx = fa(4096, 1024) if pr == 0 else bff(51200, 1024)
            oox = fa(5120, 1024) if pr == 0 else bff(53248, 1024)
            x1r = fa(6144, 1024) if pr == 0 else bff(49152, 1024)
            sldx = "sX1R" if pr == 0 else "sX1R2"
            for jt in range(32):
                if jt == 0:
                    if tb >= 2:
                        SY.wait(f"hqq_{tb - 2}")
                    elif tb == 0:
                        SY.wait("w2q_15")
                SY.dma_start(out=hblk[:, jt * 128:(jt + 1) * 128],
                             in_=bass.AP(dh.tensor, jt * (128 * TOK) + tb * 128, [[TOK, 128], [1, 128]]))\
                    .inc("sHB" if pr == 0 else "sHB3",
                         ev=(f"hbl_{tb}" if jt == 31 else None))
            V.wait(f"hbl_{tb}")
            if tb >= 2:
                V.wait(f"m2_{tb - 2}_1")
            V.tensor_scalar(out=hblk, in0=hblk, scalar1=sca[:, 2:3],
                            scalar2=MAGIC, op0=AL.mult, op1=AL.add).inc("DV")
            vdr()
            V.tensor_scalar(out=hblk, in0=hblk,
                            scalar1=MAGIC, scalar2=-128.0, op0=AL.subtract, op1=AL.max).inc("DV")
            vdr()
            V.tensor_single_scalar(out=bf(hqs, 4096), in_=hblk, scalar=127.0, op=AL.min)\
                .inc("DV", ev=f"hqq_{tb}")
            if tb >= 2:
                SY.wait(f"oge_{tb - 2}_1")
            elif tb == 0:
                for k4 in range(4):
                    SY.wait_if(f"x1s_1_{12 + k4}")
            SY.dma_start(out=x1r, in_=dx1[tb * 128:(tb + 1) * 128, :])\
                .inc(sldx, ev=f"x1r_{tb}")
            for ec in range(2):
                pb = pr * 2 + ec
                T.wait(f"hqq_{tb}")
                if tb >= 2:
                    T.wait(f"oge_{tb - 2}_{ec}")
                for jc in range(32):
                    T.matmul(bk(pb), bf(hqs + jc * 128, 128),
                             bf(O_W2Q + jc * 1024 + ec * 512, 512),
                             start=(jc == 0), stop=(jc == 31)).inc("PE", ev=(f"m2_{tb}_{ec}" if jc == 31 else None))
                V.wait(f"m2_{tb}_{ec}")
                V.wait(f"x1r_{tb}")
                V.tensor_scalar(out=ogx[:, ec * 512:(ec + 1) * 512], in0=bk(pb), scalar1=sca[:, 3:4],
                                scalar2=None, op0=AL.mult, op1=AL.bypass).inc("DV")
                vdr()
                V.tensor_tensor(out=ogx[:, ec * 512:(ec + 1) * 512], in0=ogx[:, ec * 512:(ec + 1) * 512],
                                in1=x1r[:, ec * 512:(ec + 1) * 512], op=AL.add).inc("DV", ev=f"oge_{tb}_{ec}")
            vdr()
            V.tensor_scalar(out=junk[:, 0:1024], in0=ogx, scalar1=0.0, scalar2=None,
                            op0=AL.add, op1=AL.add, accum_out=stat[:, 0:1]).inc("DV")
            vdr()
            V.tensor_single_scalar(out=stat[:, 1:2], in_=stat[:, 0:1], scalar=1.0 / D, op=AL.mult).inc("DV")
            vdr()
            V.tensor_scalar(out=ogx, in0=ogx, scalar1=stat[:, 1:2],
                            scalar2=None, op0=AL.subtract, op1=AL.bypass).inc("DV")
            vdr()
            V.scalar_tensor_tensor(out=junk[:, 0:1024], in0=ogx, scalar=0.0, in1=ogx,
                                   op0=AL.add, op1=AL.mult, accum_out=stat[:, 2:3]).inc("DV")
            vdr()
            V.tensor_scalar(out=stat[:, 3:4], in0=stat[:, 2:3], scalar1=1.0 / D, scalar2=EPS_LN,
                            op0=AL.mult, op1=AL.add).inc("DV", ev=f"v3_{tb}")
            A.wait(f"v3_{tb}")
            A.activation(out=stat[:, 4:5], in_=stat[:, 3:4], func=AF.Ln).inc("AC")
            adr()
            A.activation(out=stat[:, 3:4], in_=stat[:, 4:5], func=AF.Exp, scale=cval[:, 2:3])\
                .inc("AC", ev=f"rs3_{tb}")
            V.wait(f"rs3_{tb}")
            if tb >= 2:
                V.wait(f"os_{tb - 2}")
            V.tensor_scalar(out=oox, in0=ogx, scalar1=stat[:, 3:4],
                            scalar2=None, op0=AL.mult, op1=AL.bypass).inc("DV", ev=f"o3_{tb}")
            G.wait(f"o3_{tb}")
            G.dma_start(out=dout[tb * 128:(tb + 1) * 128, :], in_=oox)\
                .inc("gOUT" if pr == 0 else "sDBG", ev=f"os_{tb}")
        G.wait("dbg_done")
        G.dma_start(out=ddbg, in_=dbgT[:, :]).inc("gOUT", ev="dbg_st")
        G.wait("dbg_st")
        G.wait("os_31")

    K.begin("___dry___")
    program()

    s_ctx = [stack.enter_context(nc.semaphore("sem_" + n)) for n in K.sem_names]
    K.sems = dict(zip(K.sem_names, s_ctx))
    with nc.Block() as block:

        @block.sync
        def _(eng):
            K.engines["sync"] = eng
            K.begin("sync")
            program()

        @block.vector
        def _(eng):
            K.engines["vector"] = eng
            K.begin("vector")
            program()

        @block.scalar
        def _(eng):
            K.engines["scalar"] = eng
            K.begin("scalar")
            program()

        @block.tensor
        def _(eng):
            K.engines["tensor"] = eng
            K.begin("tensor")
            program()

        @block.gpsimd
        def _(eng):
            K.engines["gpsimd"] = eng
            K.begin("gpsimd")
            program()

    stack.close()
    return nc


_PROGRAM_CACHE = {}


def _get_program(stage=99):
    if stage not in _PROGRAM_CACHE:
        _PROGRAM_CACHE[stage] = build_program(stage)
    return _PROGRAM_CACHE[stage]


def kernel(L0_attn_x, L0_attn_Wq, L0_attn_bq, L0_attn_Wk, L0_attn_bk, L0_attn_Wv,
           L0_attn_bv, L0_attn_Wo, L0_attn_bo, L0_ffn_ln_g, L0_ffn_ln_b,
           L0_ffn_w1, L0_ffn_w2, _stage=99):
    x = np.ascontiguousarray(np.asarray(L0_attn_x, dtype=np.float32))
    Wq = np.asarray(L0_attn_Wq, dtype=np.float32)
    Wk = np.asarray(L0_attn_Wk, dtype=np.float32)
    Wv = np.asarray(L0_attn_Wv, dtype=np.float32)
    Wo = np.asarray(L0_attn_Wo, dtype=np.float32)
    bq = np.asarray(L0_attn_bq, dtype=np.float32)
    bk_ = np.asarray(L0_attn_bk, dtype=np.float32)
    bv = np.asarray(L0_attn_bv, dtype=np.float32)
    bo = np.asarray(L0_attn_bo, dtype=np.float32)
    g = np.asarray(L0_ffn_ln_g, dtype=np.float32)
    be = np.asarray(L0_ffn_ln_b, dtype=np.float32)
    w1 = np.asarray(L0_ffn_w1, dtype=np.float32)
    w2 = np.asarray(L0_ffn_w2, dtype=np.float32)

    perm = _perm_index()
    Wq_r = _reshape_et(np.ascontiguousarray(Wq[:, perm]))
    Wk_r = _reshape_et(np.ascontiguousarray(Wk[:, perm]))
    Wv_r = _reshape_r(Wv)
    Wo_r = _reshape_r(Wo)
    bqp = np.ascontiguousarray(bq[perm].reshape(8, 128).T)
    bkp = np.ascontiguousarray(bk_[perm].reshape(8, 128).T)
    w1T_r = _reshape_r(np.ascontiguousarray(w1.T))
    w2T_r = _reshape_r(np.ascontiguousarray(w2.T))
    lng = np.ascontiguousarray(np.tile(g[None, :], (128, 1)))
    lnb = np.ascontiguousarray(np.tile(be[None, :], (128, 1)))
    tabs = _host_tables()

    nc = _get_program(_stage)
    in_maps = []
    for c in range(NCORES):
        xc = x[2 * c:2 * c + 2]
        xflat = np.ascontiguousarray(xc.reshape(TOK, D))
        xT_r = np.stack([_reshape_r(np.ascontiguousarray(xc[bb].T)) for bb in range(BPC)])
        in_maps.append(dict(
            x=xflat, xT_r=xT_r, Wq_r=Wq_r, Wk_r=Wk_r, Wv_r=Wv_r, Wo_r=Wo_r,
            bqp=bqp, bkp=bkp, bv=bv.reshape(1, D), bo=bo.reshape(1, D),
            lng=lng, lnb=lnb, w1T_r=w1T_r, w2T_r=w2T_r,
            cos128=tabs["cos128"], sin128=tabs["sin128"], ident=tabs["ident"],
            u2=tabs["u2"], dneg=tabs["dneg"], ones_row_b=tabs["ones_row_b"],
            ones64_b=tabs["ones64_b"], ones_col_f=tabs["ones_col_f"],
            ones_row_f=tabs["ones_row_f"], cvals=tabs["cvals"],
        ))
    res = run_bass_kernel_spmd(nc, in_maps, core_ids=list(range(NCORES)))
    outs = [res.results[c]["out"].reshape(BPC, S, D) for c in range(NCORES)]
    return np.ascontiguousarray(np.concatenate(outs, axis=0), dtype=np.float32)


# version 11 (final)
# speedup vs baseline: 1.0289x; 1.0289x over previous
import math
import numpy as np
import ml_dtypes

import concourse.bass as bass
import concourse.mybir as mybir
from concourse.bass_utils import run_bass_kernel_spmd

F32 = mybir.dt.float32
BF16 = mybir.dt.bfloat16
U8 = mybir.dt.uint8
AL = mybir.AluOpType
AF = mybir.ActivationFunctionType
AX = mybir.AxisListType.X

NCORES = 8
S = 2048
D = 1024
H = 16
HD = 64
HID = 4096
BPC = 2
TOK = BPC * S
NT = TOK // 128
EXP_SHIFT = 20.0
MAGIC = 12582912.0
EPS_LN = 1e-5
N1 = 16 * S * D
N2 = 16 * S * HID
G1_STAR = float((N1 - 1) * 0.005)
G2_STAR = float((N2 - 1) * 0.005)
NBIS = 20
FAW = 9216


class _Inst:
    __slots__ = ("k", "inst")

    def __init__(self, k, inst):
        self.k, self.inst = k, inst

    def inc(self, sem, amt=None, ev=None):
        k = self.k
        amt = amt if amt is not None else (16 if sem in k.dma_sems else 1)
        k.counts[sem] += amt
        if self.inst is not None:
            self.inst.then_inc(k.sems[sem], amt)
        if ev is not None:
            k.event(ev, sem)
        return self


class _Eng:
    def __init__(self, k, name):
        self.k, self.name = k, name

    @property
    def real(self):
        return self.k.cur == self.name

    def __getattr__(self, attr):
        k = self.k
        if self.real:
            m = getattr(k.engines[self.name], attr)
            return lambda *a, **kw: _Inst(k, m(*a, **kw))
        return lambda *a, **kw: _Inst(k, None)

    def wait(self, ev, extra=0):
        if self.real:
            sem, val = self.k.events[ev]
            self.k.engines[self.name].wait_ge(self.k.sems[sem], val + extra)

    def wait_if(self, ev):
        if self.real and ev in self.k.events:
            sem, val = self.k.events[ev]
            self.k.engines[self.name].wait_ge(self.k.sems[sem], val)

    def drain(self):
        if self.real:
            self.k.engines[self.name].drain()


class _KB:
    DMA_STREAMS = ["sCON", "sGA", "sGB", "sXT0", "sXT1", "sXT2", "sXT3",
                   "sWS0", "sWS1", "sWS2", "sWS3", "sWS4", "sWS5", "sWS6", "sWS7",
                   "sWV", "sWV2", "sWV3", "sWV4",
                   "sWO0", "sWO1", "sWO2", "sWO3", "sXR0", "sXR1",
                   "sX1S0", "sX1S1", "sX1S2", "sX1S3", "sH0S", "sDBG",
                   "sQ1a", "sQ1b", "sXQa", "sXQb", "sW1L", "sW1La", "sW1G", "sW1Gb",
                   "sHSa", "sHSb", "sQ2L", "sQ2b", "sQ2c", "sQ2d",
                   "sW2L", "sW2La", "sW2G", "sW2Gb",
                   "sHB", "sHB2", "sHB3", "sHB4", "sX1R", "sX1R2",
                   "gPRM", "aPRM", "gPR2", "aPR2", "gARB", "gOUT"]

    def __init__(self):
        self.dma_sems = set(self.DMA_STREAMS)
        self.sem_names = self.DMA_STREAMS + ["PE", "DV", "AC", "GP", "CC"]
        self.sems = {}
        self.events = {}
        self.counts = {}
        self.cur = "___dry___"
        self.engines = {}

    @property
    def dry(self):
        return self.cur == "___dry___"

    def begin(self, name):
        self.cur = name
        self.counts = {s: 0 for s in self.sem_names}

    def event(self, name, sem):
        v = (sem, self.counts[sem])
        if self.dry:
            assert name not in self.events, f"duplicate event {name}"
            self.events[name] = v
        else:
            assert self.events.get(name) == v, (self.cur, name, self.events.get(name), v)


def _host_tables():
    pos = np.arange(S, dtype=np.float64)
    inv_freq = np.exp(np.arange(0, HD, 2, dtype=np.float64) * (-math.log(10000.0) / HD))
    ang = pos[:, None] * inv_freq[None, :]
    cos128 = np.ascontiguousarray(np.tile(np.cos(ang).T, (4, 1))).astype(ml_dtypes.bfloat16)
    sin128 = np.ascontiguousarray(np.tile(np.sin(ang).T, (4, 1))).astype(ml_dtypes.bfloat16)
    ident = np.eye(128, dtype=ml_dtypes.bfloat16)
    u2 = (np.arange(128)[None, :] < np.arange(128)[:, None]).astype(ml_dtypes.bfloat16)
    dneg = (np.eye(128) * -10000.0).astype(ml_dtypes.bfloat16)
    ones_row_b = np.ones((1, 128), dtype=ml_dtypes.bfloat16)
    ones64_b = np.ones((128, 64), dtype=ml_dtypes.bfloat16)
    ones_col_f = np.ones((128, 1), dtype=np.float32)
    ones_row_f = np.ones((1, 128), dtype=np.float32)
    cvals = np.tile(np.array([[-EXP_SHIFT, 0.125, -0.5, -1.0, 0.5, 1.0, 0.0, 0.0]],
                             dtype=np.float32), (128, 1))
    return dict(cos128=cos128, sin128=sin128, ident=ident, u2=u2, dneg=dneg,
                ones_row_b=ones_row_b, ones64_b=ones64_b, ones_col_f=ones_col_f,
                ones_row_f=ones_row_f, cvals=cvals)


def _perm_index():
    idx = np.zeros(D, dtype=np.int64)
    for g in range(4):
        for j in range(4):
            h = 4 * g + j
            idx[(2 * g) * 128 + j * 32:(2 * g) * 128 + (j + 1) * 32] = np.arange(h * HD, h * HD + 32)
            idx[(2 * g + 1) * 128 + j * 32:(2 * g + 1) * 128 + (j + 1) * 32] = np.arange(h * HD + 32, h * HD + 64)
    return idx


def _reshape_r(w):
    R, C = w.shape
    nch = R // 128
    return np.ascontiguousarray(w.reshape(nch, 128, C).transpose(1, 0, 2).reshape(128, nch * C))


def _reshape_et(w):
    return np.ascontiguousarray(
        w.reshape(8, 128, 8, 128).transpose(1, 2, 0, 3).reshape(128, 8192))


def build_program(stage=99):
    nc = bass.Bass("TRN2", target_bir_lowering=False)

    dx = nc.dram_tensor("x", [TOK, D], F32, kind="ExternalInput").ap()
    dxT = nc.dram_tensor("xT_r", [BPC, 128, 8 * S], F32, kind="ExternalInput").ap()
    dWq = nc.dram_tensor("Wq_r", [128, 8 * D], F32, kind="ExternalInput").ap()
    dWk = nc.dram_tensor("Wk_r", [128, 8 * D], F32, kind="ExternalInput").ap()
    dWv = nc.dram_tensor("Wv_r", [128, 8 * D], F32, kind="ExternalInput").ap()
    dWo = nc.dram_tensor("Wo_r", [128, 8 * D], F32, kind="ExternalInput").ap()
    dbqp = nc.dram_tensor("bqp", [128, 8], F32, kind="ExternalInput").ap()
    dbkp = nc.dram_tensor("bkp", [128, 8], F32, kind="ExternalInput").ap()
    dbv = nc.dram_tensor("bv", [1, D], F32, kind="ExternalInput").ap()
    dbo = nc.dram_tensor("bo", [1, D], F32, kind="ExternalInput").ap()
    dlng = nc.dram_tensor("lng", [128, D], F32, kind="ExternalInput").ap()
    dlnb = nc.dram_tensor("lnb", [128, D], F32, kind="ExternalInput").ap()
    dw1 = nc.dram_tensor("w1T_r", [128, 8 * HID], F32, kind="ExternalInput").ap()
    dw2 = nc.dram_tensor("w2T_r", [128, 32 * D], F32, kind="ExternalInput").ap()
    dcos = nc.dram_tensor("cos128", [128, S], BF16, kind="ExternalInput").ap()
    dsin = nc.dram_tensor("sin128", [128, S], BF16, kind="ExternalInput").ap()
    dident = nc.dram_tensor("ident", [128, 128], BF16, kind="ExternalInput").ap()
    du2 = nc.dram_tensor("u2", [128, 128], BF16, kind="ExternalInput").ap()
    ddneg = nc.dram_tensor("dneg", [128, 128], BF16, kind="ExternalInput").ap()
    donesrb = nc.dram_tensor("ones_row_b", [1, 128], BF16, kind="ExternalInput").ap()
    dones64 = nc.dram_tensor("ones64_b", [128, 64], BF16, kind="ExternalInput").ap()
    donescf = nc.dram_tensor("ones_col_f", [128, 1], F32, kind="ExternalInput").ap()
    donesrf = nc.dram_tensor("ones_row_f", [1, 128], F32, kind="ExternalInput").ap()
    dcval = nc.dram_tensor("cvals", [128, 8], F32, kind="ExternalInput").ap()
    dout = nc.dram_tensor("out", [TOK, D], F32, kind="ExternalOutput").ap()
    ddbg = nc.dram_tensor("dbg", [128, 40], F32, kind="ExternalOutput").ap()

    dx1 = nc.dram_tensor("x1sp", [TOK, D], F32).ap()
    dh0 = nc.dram_tensor("h0sp", [TOK, D], F32).ap()
    dh = nc.dram_tensor("hsp", [32, 128, TOK], F32).ap()
    cbi = [nc.dram_tensor("cbi%d" % i, [1, 8], F32).ap() for i in range(4)]
    cbo = [nc.dram_tensor("cbo%d" % i, [1, 8], F32).ap() for i in range(4)]

    K = _KB()
    V = _Eng(K, "vector")
    A = _Eng(K, "scalar")
    T = _Eng(K, "tensor")
    G = _Eng(K, "gpsimd")
    SY = _Eng(K, "sync")

    import contextlib
    stack = contextlib.ExitStack()

    def sbuf(name, shape, dtype):
        return stack.enter_context(nc.sbuf_tensor(name, shape, dtype))

    cosT = sbuf("cosT", [128, S], BF16)
    sinT = sbuf("sinT", [128, S], BF16)
    identT = sbuf("identT", [128, 128], BF16)
    u2T = sbuf("u2T", [128, 128], BF16)
    dnegT = sbuf("dnegT", [128, 128], BF16)
    onesrb = sbuf("onesrb", [1, 128], BF16)
    ones64 = sbuf("ones64", [128, 64], BF16)
    onescf = sbuf("onescf", [128, 1], F32)
    onesrf = sbuf("onesrf", [1, 128], F32)
    cval = sbuf("cval", [128, 8], F32)
    bqpT = sbuf("bqpT", [128, 8], F32)
    bkpT = sbuf("bkpT", [128, 8], F32)
    bvT = sbuf("bvT", [1, D], BF16)
    boT = sbuf("boT", [1, D], BF16)
    lngT = sbuf("lngT", [128, D], F32)
    lnbT = sbuf("lnbT", [128, D], F32)
    S1 = sbuf("S1", [128, 1024], BF16)
    S2 = sbuf("S2", [128, 1024], BF16)
    qlo = sbuf("qlo", [128, 1], F32)
    qhi = sbuf("qhi", [128, 1], F32)
    qmid = sbuf("qmid", [128, 1], F32)
    qnmid = sbuf("qnmid", [128, 1], F32)
    qcnt = sbuf("qcnt", [128, 1], F32)
    qcnt2 = sbuf("qcnt2", [128, 1], F32)
    qmsk = sbuf("qmsk", [128, 1], U8)
    qvec = sbuf("qvec", [128, 2], F32)
    ario = sbuf("ario", [1, 8], F32)
    rowa = sbuf("rowa", [1, 16], F32)
    rowb = sbuf("rowb", [1, 16], F32)
    thr4 = sbuf("thr4", [128, 4], F32)
    cntall = sbuf("cntall", [128, 256], F32)
    cnt4 = sbuf("cnt4", [128, 4], F32)
    sca = sbuf("sca", [128, 4], F32)
    gq = sbuf("gq", [128, 2], F32)
    gsum = sbuf("gsum", [128, 34], F32)
    stat = sbuf("stat", [128, 16], F32)
    junk = sbuf("junk", [128, 2048], BF16)

    dbgT = sbuf("dbgT", [128, 40], F32)
    BFA = sbuf("BFA", [128, 71680], BF16)
    FA = sbuf("FA", [128, FAW], F32)
    PS = stack.enter_context(nc.psum_tensor("PS", [128, 4096], F32))

    def bk(i, off=0, w=512, p0=0, pn=128):
        return PS[p0:p0 + pn, i * 512 + off: i * 512 + off + w]

    def bf(off, w, p0=0, pn=128):
        return BFA[p0:p0 + pn, off:off + w]

    def fa(off, w, p0=0, pn=128):
        return FA[p0:p0 + pn, off:off + w]

    def bff(off_bf, w, p0=0, pn=128):
        return BFA[p0:p0 + pn, off_bf:off_bf + 2 * w].bitcast(F32)

    O_XTB, O_SCR, O_WSL, O_QC, O_KC, O_VN = 0, 16384, 20480, 22528, 38912, 55296
    O_RT = 55296
    O_VAS, O_PT, O_CXN, O_WO = 0, 2048, 4096, 8192
    O_XQT, O_W1Q, O_XQG = 0, 32768, 65536
    O_W2Q, O_HQ = 0, 32768
    F_STG, F_X1G, F_H0G, F_XRS, F_RCP = 0, 2048, 4096, 6144, 1024
    F_XPA = 7168
    F_WST = F_H0G

    def xsl(tt):
        m = tt % 4
        return (F_X1G + m * 1024) if m < 2 else (F_XPA + (m - 2) * 1024)

    def vdr():
        V.drain()

    def adr():
        A.drain()

    uid = [0]

    def _u():
        uid[0] += 1
        return uid[0]

    def replicate(row_ap, out_ap, n, tag, wait_ev=None):
        if wait_ev is not None:
            T.wait(wait_ev)
        T.matmul(PS[:, 7 * 512:7 * 512 + n], onesrf[0:1, :], row_ap, start=True, stop=True)\
            .inc("PE", ev=tag + "_mm")
        V.wait(tag + "_mm")
        V.tensor_copy(out=out_ap, in_=PS[:, 7 * 512:7 * 512 + n]).inc("DV", ev=tag)
        vdr()

    def cross_sum(cols_ap, n, tag, wait_ev=None):
        if wait_ev is not None:
            T.wait(wait_ev)
        T.matmul(PS[0:1, 7 * 512:7 * 512 + n], onescf[:, 0:1], cols_ap, start=True, stop=True)\
            .inc("PE", ev=tag)

    def emit_bisect(Sbuf, width, target, tag, abs_wait):
        A.wait(abs_wait)
        A.activation(out=junk[:, 0:width], in_=Sbuf[:, 0:width], func=AF.Abs)\
            .inc("AC", ev=tag + "_abs")
        V.memset(qlo[:, :], 0.0)
        V.tensor_reduce(out=qhi[:, :], in_=Sbuf[:, 0:width], axis=AX, op=AL.max,
                        apply_absolute_value=True).inc("DV")
        vdr()
        V.wait(tag + "_abs")
        for _ in range(NBIS):
            V.tensor_scalar(out=qmid[:, :], in0=qlo[:, :], scalar1=qhi[:, 0:1],
                            scalar2=0.5, op0=AL.add, op1=AL.mult).inc("DV")
            vdr()
            V.tensor_scalar(out=junk[:, 1024:1024 + width], in0=junk[:, 0:width],
                            scalar1=qmid[:, 0:1], scalar2=None, op0=AL.is_gt, op1=AL.add,
                            accum_out=qcnt[:, 0:1]).inc("DV")
            vdr()
            V.tensor_single_scalar(out=qmsk[:, :], in_=qcnt[:, :], scalar=float(target), op=AL.is_gt).inc("DV")
            vdr()
            V.copy_predicated(out=qlo[:, :], mask=qmsk[:, :], data=qmid[:, :]).inc("DV")
            vdr()
            V.tensor_single_scalar(out=qmsk[:, :], in_=qcnt[:, :], scalar=float(target), op=AL.is_le).inc("DV")
            vdr()
            V.copy_predicated(out=qhi[:, :], mask=qmsk[:, :], data=qmid[:, :]).inc("DV")
            vdr()
        V.tensor_copy(out=qvec[:, 0:1], in_=qhi[:, :]).inc("DV")
        V.tensor_tensor(out=qvec[:, 1:2], in0=qhi[:, :], in1=qhi[:, :], op=AL.mult).inc("DV", ev=tag + "_qv")
        vdr()

    def emit_ar(idx, tag):
        G.wait(tag + "_arin")
        G.dma_start(out=cbi[idx], in_=ario[0:1, :]).inc("gARB", ev=tag + "_cbi")
        G.wait(tag + "_cbi")
        G.collective_compute("AllReduce", AL.add, replica_groups=[list(range(NCORES))],
                             ins=[cbi[idx].opt()], outs=[cbo[idx].opt()]).inc("CC", ev=tag + "_cc")
        G.wait(tag + "_cc")
        G.dma_start(out=ario[0:1, :], in_=cbo[idx]).inc("gARB", ev=tag + "_arout")

    def emit_stats_to_ario(tag):
        cross_sum(qvec[:, 0:2], 2, tag + "_cs", wait_ev=tag + "_qv")
        A.wait(tag + "_cs")
        A.copy(out=ario[0:1, 0:2], in_=PS[0:1, 7 * 512:7 * 512 + 2]).inc("AC", ev=tag + "_arin")
        adr()

    def emit_thresholds(tag, nrows):
        V.wait(tag + "_arout")
        V.tensor_single_scalar(out=rowa[0:1, 0:1], in_=ario[0:1, 0:1], scalar=1.0 / nrows, op=AL.mult).inc("DV")
        V.tensor_single_scalar(out=rowa[0:1, 1:2], in_=ario[0:1, 1:2], scalar=1.0 / nrows, op=AL.mult).inc("DV")
        vdr()
        V.tensor_tensor(out=rowa[0:1, 2:3], in0=rowa[0:1, 0:1], in1=rowa[0:1, 0:1], op=AL.mult).inc("DV")
        vdr()
        V.tensor_tensor(out=rowa[0:1, 3:4], in0=rowa[0:1, 1:2], in1=rowa[0:1, 2:3], op=AL.subtract).inc("DV")
        vdr()
        V.tensor_scalar(out=rowa[0:1, 4:5], in0=rowa[0:1, 3:4], scalar1=1.0 / nrows,
                        scalar2=1e-32, op0=AL.mult, op1=AL.add).inc("DV", ev=tag + "_var")
        vdr()
        A.wait(tag + "_var")
        A.activation(out=rowa[0:1, 5:6], in_=rowa[0:1, 4:5], func=AF.Ln).inc("AC")
        adr()
        A.activation(out=rowa[0:1, 6:7], in_=rowa[0:1, 5:6], func=AF.Exp, scale=cval[0:1, 4:5])\
            .inc("AC", ev=tag + "_sqrt")
        V.wait(tag + "_sqrt")
        V.tensor_single_scalar(out=rowa[0:1, 6:7], in_=rowa[0:1, 6:7], scalar=1.6, op=AL.mult).inc("DV")
        vdr()
        for j, kk in enumerate([-14.0, -3.0, 3.0, 14.0]):
            V.tensor_single_scalar(out=rowb[0:1, j:j + 1], in_=rowa[0:1, 6:7], scalar=kk, op=AL.mult).inc("DV")
        vdr()
        for j in range(4):
            V.tensor_tensor(out=rowb[0:1, j:j + 1], in0=rowb[0:1, j:j + 1], in1=rowa[0:1, 0:1],
                            op=AL.add).inc("DV")
        vdr()
        V.tensor_single_scalar(out=rowb[0:1, 0:4], in_=rowb[0:1, 0:4], scalar=1e-20, op=AL.max)\
            .inc("DV", ev=tag + "_thr")
        vdr()
        replicate(rowb[0:1, 0:4], thr4[:, 0:4], 4, tag + "_rep", wait_ev=tag + "_thr")

    def emit_counts_finish(tag, nchunks):
        for j in range(4):
            V.tensor_reduce(out=cnt4[:, j:j + 1], in_=cntall[:, j * 64:j * 64 + nchunks],
                            axis=AX, op=AL.add).inc("DV")
        vdr()
        _e = V.tensor_copy(out=cnt4[:, 0:1], in_=cnt4[:, 0:1])
        _e.inc("DV", ev=tag + "_cnt4")
        vdr()
        cross_sum(cnt4[:, 0:4], 4, tag + "_ccs", wait_ev=tag + "_cnt4")
        A.wait(tag + "_ccs")
        A.copy(out=ario[0:1, 0:4], in_=PS[0:1, 7 * 512:7 * 512 + 4]).inc("AC", ev=tag + "_arin")
        adr()

    def emit_interp(tag, gstar, ntot, out_col, gamma_col):
        V.wait(tag + "_arout")
        V.memset(rowa[0:1, 8:9], 0.0)
        V.memset(rowa[0:1, 9:10], float(ntot))
        V.tensor_copy(out=rowa[0:1, 10:11], in_=rowb[0:1, 0:1]).inc("DV")
        V.tensor_copy(out=rowa[0:1, 11:12], in_=ario[0:1, 0:1]).inc("DV")
        vdr()
        for i in range(3):
            V.tensor_single_scalar(out=qmsk[0:1, 0:1], in_=ario[0:1, i:i + 1], scalar=gstar,
                                   op=AL.is_ge).inc("DV")
            vdr()
            V.copy_predicated(out=rowa[0:1, 8:9], mask=qmsk[0:1, 0:1], data=rowb[0:1, i:i + 1]).inc("DV")
            V.copy_predicated(out=rowa[0:1, 9:10], mask=qmsk[0:1, 0:1], data=ario[0:1, i:i + 1]).inc("DV")
            V.copy_predicated(out=rowa[0:1, 10:11], mask=qmsk[0:1, 0:1], data=rowb[0:1, i + 1:i + 2]).inc("DV")
            V.copy_predicated(out=rowa[0:1, 11:12], mask=qmsk[0:1, 0:1], data=ario[0:1, i + 1:i + 2]).inc("DV")
            vdr()
        V.tensor_tensor(out=rowa[0:1, 12:13], in0=rowa[0:1, 9:10], in1=rowa[0:1, 11:12], op=AL.subtract).inc("DV")
        V.tensor_single_scalar(out=rowa[0:1, 13:14], in_=rowa[0:1, 9:10], scalar=gstar, op=AL.subtract).inc("DV")
        V.tensor_tensor(out=rowa[0:1, 14:15], in0=rowa[0:1, 10:11], in1=rowa[0:1, 8:9], op=AL.subtract).inc("DV")
        vdr()
        V.tensor_single_scalar(out=rowa[0:1, 12:13], in_=rowa[0:1, 12:13], scalar=0.5, op=AL.max).inc("DV")
        vdr()
        V.reciprocal(out=rowa[0:1, 12:13], in_=rowa[0:1, 12:13]).inc("DV")
        vdr()
        V.tensor_tensor(out=rowa[0:1, 13:14], in0=rowa[0:1, 13:14], in1=rowa[0:1, 12:13], op=AL.mult).inc("DV")
        vdr()
        V.tensor_tensor(out=rowa[0:1, 13:14], in0=rowa[0:1, 13:14], in1=rowa[0:1, 14:15], op=AL.mult).inc("DV")
        vdr()
        V.tensor_tensor(out=rowa[0:1, 13:14], in0=rowa[0:1, 13:14], in1=rowa[0:1, 8:9], op=AL.add).inc("DV")
        vdr()
        V.tensor_single_scalar(out=rowa[0:1, 13:14], in_=rowa[0:1, 13:14], scalar=1e-5, op=AL.max).inc("DV")
        vdr()
        V.reciprocal(out=rowa[0:1, 14:15], in_=rowa[0:1, 13:14]).inc("DV")
        vdr()
        V.tensor_single_scalar(out=rowb[0:1, 8:9], in_=rowa[0:1, 14:15], scalar=127.0, op=AL.mult).inc("DV")
        V.tensor_single_scalar(out=rowb[0:1, 9:10], in_=rowa[0:1, 13:14], scalar=1.0 / 127.0, op=AL.mult).inc("DV")
        vdr()
        V.tensor_tensor(out=rowb[0:1, 9:10], in0=rowb[0:1, 9:10], in1=rowb[0:1, gamma_col:gamma_col + 1],
                        op=AL.mult).inc("DV", ev=tag + "_sca")
        vdr()
        replicate(rowb[0:1, 8:10], sca[:, out_col:out_col + 2], 2, tag + "_screp", wait_ev=tag + "_sca")

    def finish_debug():
        V.memset(junk[:, :], 0.0)
        vdr()
        _e = V.tensor_copy(out=junk[:, 0:1024], in_=junk[:, 0:1024])
        _e.inc("DV", ev="dbg_fin")
        G.wait("dbg_fin")
        G.dma_start(out=dout[0:128, :], in_=junk[:, 0:1024]).inc("gOUT", ev="dbg_out")
        G.wait("dbg_out")

    def program():
        for dst, src in [(cosT[:, :], dcos), (sinT[:, :], dsin), (identT[:, :], dident),
                         (u2T[:, :], du2), (dnegT[:, :], ddneg), (onesrb[:, :], donesrb),
                         (ones64[:, :], dones64), (onescf[:, :], donescf), (onesrf[:, :], donesrf),
                         (cval[:, :], dcval), (bqpT[:, :], dbqp), (bkpT[:, :], dbkp),
                         (lngT[:, :], dlng), (lnbT[:, :], dlnb)]:
            SY.dma_start(out=dst, in_=src).inc("sCON")
        SY.dma_start(out=fa(F_STG, 1024, 0, 1), in_=dbv).inc("sCON")
        SY.dma_start(out=fa(F_X1G, 1024, 0, 1), in_=dbo).inc("sCON", ev="consts_ld")
        V.wait("consts_ld")
        V.memset(ario[0:1, :], 0.0)
        V.memset(dbgT[:, :], 0.0)
        V.tensor_copy(out=bvT[:, :], in_=fa(F_STG, 1024, 0, 1)).inc("DV")
        V.tensor_copy(out=boT[:, :], in_=fa(F_X1G, 1024, 0, 1)).inc("DV", ev="bias_cast")

        def emit_gamma_load(ci):
            if ci >= 64:
                return
            wi, cc = divmod(ci, 32)
            dsrc = dw1 if wi == 0 else dw2
            if ci >= 2:
                G.wait(f"gred_{ci - 2}")
            else:
                G.wait("vp_0_1_15")
            G.dma_start(out=bff(O_SCR + (ci % 2) * 2048, 1024),
                        in_=dsrc[:, cc * 1024:(cc + 1) * 1024])\
                .inc("sGA" if ci % 2 == 0 else "sGB", ev=f"gld_{ci}")

        def emit_gamma_red(ci):
            wi, cc = divmod(ci, 32)
            V.wait(f"gld_{ci}")
            V.tensor_reduce(out=gsum[:, 2 + cc:3 + cc],
                            in_=bff(O_SCR + (ci % 2) * 2048, 1024), axis=AX, op=AL.add,
                            apply_absolute_value=True).inc("DV", ev=f"gred_{ci}")
            if cc == 31:
                vdr()
                V.tensor_reduce(out=gsum[:, wi:wi + 1], in_=gsum[:, 2:34], axis=AX,
                                op=AL.add).inc("DV", ev=f"gtot_{wi}")
                vdr()

        def emit_gamma_finish():
            T.wait_if("x1p_0_15_1")
            cross_sum(gsum[:, 0:2], 2, "g_cs", wait_ev="gtot_1")
            A.wait("g_cs")
            A.copy(out=rowb[0:1, 10:12], in_=PS[0:1, 7 * 512:7 * 512 + 2]).inc("AC", ev="g_row")
            adr()
            V.wait("g_row")
            V.tensor_scalar(out=rowb[0:1, 12:13], in0=rowb[0:1, 10:11], scalar1=1.0 / (HID * D),
                            scalar2=1e-5, op0=AL.mult, op1=AL.add).inc("DV")
            V.tensor_scalar(out=rowb[0:1, 13:14], in0=rowb[0:1, 11:12], scalar1=1.0 / (HID * D),
                            scalar2=1e-5, op0=AL.mult, op1=AL.add).inc("DV")
            vdr()
            V.reciprocal(out=rowb[0:1, 14:15], in_=rowb[0:1, 12:13]).inc("DV")
            V.reciprocal(out=rowb[0:1, 15:16], in_=rowb[0:1, 13:14]).inc("DV", ev="ginv")
            vdr()
            replicate(rowb[0:1, 14:16], gq[:, 0:2], 2, "gq_rep", wait_ev="ginv")

        if stage == 11:
            finish_debug()
            return

        for b in range(BPC):
            for c8 in range(8):
                sl = c8 % 2
                if c8 >= 2:
                    SY.wait(f"xTc_{b}_{c8 - 2}")
                    G.wait(f"xTc_{b}_{c8 - 2}")
                elif b == 0:
                    SY.wait("bias_cast")
                    G.wait("bias_cast")
                elif b == 1:
                    SY.wait_if("x1p_0_15_1")
                    SY.wait_if("cxn_0_3_15")
                    G.wait_if("h0_0_13")
                    G.wait_if("x1s_0_12")
                    G.wait_if("x1s_0_13")
                    G.wait_if("o1_0_13")
                    G.wait_if("o16_0_15")
                xt_base = b * (128 * 8 * S) + c8 * S
                SY.dma_start(out=fa(F_STG + sl * 1024, 1024),
                             in_=bass.AP(dxT.tensor, xt_base, [[8 * S, 128], [1, 1024]]))\
                    .inc(("sXT0" if sl == 0 else "sXT1"), ev=f"xTlA_{b}_{c8}")
                G.dma_start(out=fa(F_X1G + sl * 1024, 1024),
                            in_=bass.AP(dxT.tensor, xt_base + 1024, [[8 * S, 128], [1, 1024]]))\
                    .inc(("sXT2" if sl == 0 else "sXT3"), ev=f"xTl_{b}_{c8}")
                A.wait(f"xTlA_{b}_{c8}")
                A.wait(f"xTl_{b}_{c8}")
                if b == 1 and c8 == 0:
                    A.wait_if("x1p_0_15_1")
                A.activation(out=bf(O_XTB + c8 * S, 1024), in_=fa(F_STG + sl * 1024, 1024),
                             func=AF.Copy).inc("AC")
                A.activation(out=bf(O_XTB + c8 * S + 1024, 1024), in_=fa(F_X1G + sl * 1024, 1024),
                             func=AF.Copy).inc("AC", ev=f"xTc_{b}_{c8}")

            for which, (dW, biasT, O_DST) in enumerate([(dWk, bkpT, O_KC), (dWq, bqpT, O_QC)]):
                for g4 in range(4):
                    for sub in range(2):
                        et = 2 * g4 + sub
                        wsl = et % 2
                        wsl4 = et % 4
                        wstg = (F_WST + wsl4 * 1024) if wsl4 < 2 else (F_XPA + (wsl4 - 2) * 1024)
                        if et >= 4:
                            SY.wait(f"wcast_{b}_{which}_{et - 4}")
                            G.wait(f"wcast_{b}_{which}_{et - 4}")
                        elif which == 1:
                            SY.wait(f"wcast_{b}_0_{4 + et}")
                            G.wait(f"wcast_{b}_0_{4 + et}")
                        elif b == 1:
                            if wsl4 < 2:
                                for eng_ in (SY, G):
                                    eng_.wait_if("h0s_0_15")
                                    eng_.wait_if("sub1_0_15")
                                    eng_.wait_if("o2_0_15")
                            else:
                                for eng_ in (SY, G):
                                    eng_.wait_if("x1s_0_14")
                                    eng_.wait_if("x1s_0_15")
                                    eng_.wait_if("h0_0_15")
                                    eng_.wait_if("o1_0_15")
                                    eng_.wait_if("o16_0_15")
                        for half, ldeng in ((0, SY), (1, G)):
                            ldeng.dma_start(out=fa(wstg + half * 512, 512),
                                            in_=bass.AP(dW.tensor, et * 1024 + half * 512,
                                                        [[8 * D, 128], [1, 512]]))\
                                .inc(f"sWS{wsl4 * 2 + half}", ev=f"wldq{half}_{b}_{which}_{et}")
                        A.wait(f"wldq0_{b}_{which}_{et}")
                        A.wait(f"wldq1_{b}_{which}_{et}")
                        if et >= 2:
                            A.wait(f"pj_{b}_{which}_{(et - 2) * 4 + 3}")
                        elif which == 1:
                            A.wait(f"pj_{b}_0_{(6 + et) * 4 + 3}")
                        A.activation(out=bf(O_WSL + wsl * 1024, 1024), in_=fa(wstg, 1024),
                                     func=AF.Copy).inc("AC", ev=f"wcast_{b}_{which}_{et}")
                        gi = which * 4 + g4
                        PNL = O_SCR if gi % 2 == 0 else O_RT + 8192
                        for half in range(4):
                            pb = (et * 4 + half) % 2
                            T.wait(f"wcast_{b}_{which}_{et}")
                            if et * 4 + half >= 2:
                                T.wait(f"pje_{b}_{which}_{et * 4 + half - 2}")
                            for dc in range(8):
                                T.matmul(bk(pb), bf(O_WSL + wsl * 1024 + dc * 128, 128),
                                         bf(O_XTB + dc * S + half * 512, 512),
                                         start=(dc == 0), stop=(dc == 7))\
                                    .inc("PE", ev=(f"pj_{b}_{which}_{et * 4 + half}" if dc == 7 else None))
                            A.wait(f"pj_{b}_{which}_{et * 4 + half}")
                            if sub == 0 and half == 0:
                                if gi >= 2:
                                    wp, gp = divmod(gi - 2, 4)
                                    A.wait(f"perm_{b}_{wp}_{gp}_1")
                                    A.wait(f"perm_{b}_{wp}_{gp}_3")
                                elif b == 1 and gi == 0:
                                    A.wait_if("gred_63")
                            A.activation(out=bf(PNL + sub * 2048 + half * 512, 512),
                                         in_=bk(pb), func=AF.Identity, bias=biasT[:, et:et + 1])\
                                .inc("AC", ev=f"pje_{b}_{which}_{et * 4 + half}")
                    V.wait(f"pje_{b}_{which}_{(2 * g4 + 1) * 4 + 3}")
                    vdr()
                    x1 = bf(PNL, 2048)
                    x2 = bf(PNL + 2048, 2048)
                    tA = bf(O_RT, 2048)
                    tB = bf(O_RT + 2048, 2048)
                    tC = bf(O_RT + 4096, 2048)
                    tD = bf(O_RT + 6144, 2048)
                    V.tensor_tensor(out=tA, in0=x1, in1=cosT[:, :], op=AL.mult).inc("DV")
                    V.tensor_tensor(out=tB, in0=x2, in1=sinT[:, :], op=AL.mult).inc("DV")
                    V.tensor_tensor(out=tC, in0=x1, in1=sinT[:, :], op=AL.mult).inc("DV")
                    V.tensor_tensor(out=tD, in0=x2, in1=cosT[:, :], op=AL.mult).inc("DV")
                    vdr()
                    V.tensor_tensor(out=x1, in0=tA, in1=tB, op=AL.subtract).inc("DV")
                    V.tensor_tensor(out=x2, in0=tC, in1=tD, op=AL.add).inc("DV", ev=f"rope_{b}_{which}_{g4}")
                    SY.wait(f"rope_{b}_{which}_{g4}")
                    G.wait(f"rope_{b}_{which}_{g4}")
                    for j in range(4):
                        h = 4 * g4 + j
                        pc = h // 2
                        po = (h % 2) * 64
                        if j < 2:
                            peng, pstr = SY, ("gPRM" if gi % 2 == 0 else "gPR2")
                        else:
                            peng, pstr = G, ("aPRM" if gi % 2 == 0 else "aPR2")
                        peng.dma_start(out=bf(O_DST + pc * S, S, po, 32),
                                       in_=bf(PNL, S, j * 32, 32)).inc(pstr)
                        peng.dma_start(out=bf(O_DST + pc * S, S, po + 32, 32),
                                       in_=bf(PNL + 2048, S, j * 32, 32))\
                            .inc(pstr, ev=f"perm_{b}_{which}_{g4}_{j}")

            for ec in range(2):
                if ec == 0:
                    A.wait(f"xTc_{b}_6")
                    A.wait(f"xTc_{b}_7")
                else:
                    A.wait(f"wvc_{b}_0")
                for half in range(2):
                    A.dma_start(out=fa(F_STG + half * 1024, 1024),
                                in_=bass.AP(dWv.tensor, ec * 512 + half * 2 * D,
                                            [[8 * D, 128], [D, 2], [1, 512]]))\
                        .inc("sWV" if half == 0 else "sWV2", ev=f"wvlA{half}_{b}_{ec}")
                    A.dma_start(out=fa(F_X1G + half * 1024, 1024),
                                in_=bass.AP(dWv.tensor, 4 * D + ec * 512 + half * 2 * D,
                                            [[8 * D, 128], [D, 2], [1, 512]]))\
                        .inc("sWV3" if half == 0 else "sWV4", ev=f"wvlB{half}_{b}_{ec}")
                for evn in (f"wvlA0_{b}_{ec}", f"wvlA1_{b}_{ec}",
                            f"wvlB0_{b}_{ec}", f"wvlB1_{b}_{ec}"):
                    A.wait(evn)
                if ec == 0:
                    A.wait(f"perm_{b}_1_2_1")
                    A.wait(f"perm_{b}_1_2_3")
                else:
                    A.wait(f"vpe_{b}_0_15")
                A.activation(out=bf(O_SCR, 2048), in_=fa(F_STG, 2048), func=AF.Copy).inc("AC")
                A.activation(out=bf(O_SCR + 2048, 2048), in_=fa(F_X1G, 2048), func=AF.Copy)\
                    .inc("AC", ev=f"wvc_{b}_{ec}")
                for tt in range(16):
                    pb = tt % 2
                    T.wait(f"wvc_{b}_{ec}")
                    if tt >= 2:
                        T.wait(f"vpe_{b}_{ec}_{tt - 2}")
                    for dc in range(8):
                        T.matmul(bk(pb), bf(O_XTB + dc * S + tt * 128, 128),
                                 bf(O_SCR + dc * 512, 512),
                                 start=(dc == 0), stop=False).inc("PE")
                    T.matmul(bk(pb), onesrb[0:1, :], bvT[0:1, ec * 512:(ec + 1) * 512],
                             start=False, stop=True).inc("PE", ev=f"vp_{b}_{ec}_{tt}")
                    V.wait(f"vp_{b}_{ec}_{tt}")
                    if tt == 0 and ec == 0:
                        vdr()
                    if tt == 8:
                        V.wait(f"perm_{b}_1_3_1")
                        V.wait(f"perm_{b}_1_3_3")
                    V.tensor_copy(out=bf(O_VN + tt * 1024 + ec * 512, 512), in_=bk(pb))\
                        .inc("DV", ev=f"vpe_{b}_{ec}_{tt}")

            for c8 in range(8):
                sl = c8 % 2
                if c8 >= 2:
                    A.wait(f"woc_{b}_{c8 - 2}")
                else:
                    A.wait(f"wvc_{b}_1")
                for half in range(2):
                    A.dma_start(out=fa(F_STG + sl * 1024 + half * 512, 512),
                                in_=dWo[:, c8 * 1024 + half * 512: c8 * 1024 + half * 512 + 512])\
                        .inc(f"sWO{sl * 2 + half}", ev=f"wol{half}_{b}_{c8}")
                A.wait(f"wol0_{b}_{c8}")
                A.wait(f"wol1_{b}_{c8}")
                if c8 == 0:
                    A.wait(f"vpe_{b}_1_15")
                A.activation(out=bf(O_WO + c8 * 1024, 1024), in_=fa(F_STG + sl * 1024, 1024),
                             func=AF.Copy).inc("AC", ev=f"woc_{b}_{c8}")
            T.wait("woc_%d_7" % b)

            if stage == 12:
                continue
            pend = []
            if b == 0:
                emit_gamma_load(0)
                emit_gamma_load(1)
            for qq in range(4):
                if qq > 0:
                    T.wait(f"x1p_{b}_{qq * 4 - 2}_0")
                else:
                    T.wait(f"perm_{b}_1_3_1")
                    T.wait(f"perm_{b}_1_3_3")
                    if b == 1:
                        T.wait("x1p_0_14_0")
                for h in range(H):
                    if b == 0:
                        emit_gamma_load(qq * 16 + h + 2)
                        emit_gamma_red(qq * 16 + h)
                    if pend and h >= 1:
                        pend.pop(0)()
                        if pend and h <= 5:
                            pend.pop(0)()
                    pc = h // 2
                    po = (h % 2) * 64
                    cb = 4 + 2 * (h % 2)
                    nkt = 4 * qq + 4

                    def emit_scores(kt):
                        c0 = max(kt * 128 - qq * 512, 0)
                        sb_ = kt % 2
                        if kt >= 2:
                            T.wait(f"exp_{b}_{qq}_{h}_{kt - 2}")
                        T.matmul(bk(sb_, c0, 512 - c0),
                                 bf(O_KC + pc * S + kt * 128, 128, po, 64),
                                 bf(O_QC + pc * S + qq * 512 + c0, 512 - c0, po, 64),
                                 start=True, stop=(kt < 4 * qq)).inc("PE")
                        if kt >= 4 * qq:
                            T.matmul(bk(sb_, c0, 128), dnegT[:, :], u2T[:, :],
                                     start=False, stop=True).inc("PE")
                        K.event(f"sc_{b}_{qq}_{h}_{kt}", "PE")

                    def emit_exp(kt):
                        c0 = max(kt * 128 - qq * 512, 0)
                        sb_ = kt % 2
                        A.wait(f"sc_{b}_{qq}_{h}_{kt}")
                        if kt >= 2:
                            A.wait(f"cx_{b}_{qq}_{h}_{kt - 2}")
                        A.activation(out=bf(O_PT + sb_ * 1024 + c0, 512 - c0),
                                     in_=bk(sb_, c0, 512 - c0),
                                     func=AF.Exp, bias=cval[:, 0:1], scale=cval[:, 1:2])\
                            .inc("AC", ev=f"exp_{b}_{qq}_{h}_{kt}")

                    def emit_ctx(kt):
                        c0 = max(kt * 128 - qq * 512, 0)
                        sb_ = kt % 2
                        T.wait(f"exp_{b}_{qq}_{h}_{kt}")
                        if kt == 0 and h < 2:
                            if qq > 0:
                                T.wait(f"x1p_{b}_{qq * 4 - 1}_{h}")
                            elif b == 1:
                                T.wait(f"x1p_0_15_{h}")
                        if kt == 0 and (qq > 0 or h >= 2):
                            ph = h - 2 if h >= 2 else H - 2 + h
                            pq = qq if h >= 2 else qq - 1
                            T.wait(f"cxn_{b}_{pq}_{ph}")
                        if b == 1 and qq == 0 and h == 1 and kt == 0:
                            T.wait_if("gq_rep")
                            T.wait_if("g_row")
                        T.matmul(PS[0:64, cb * 512 + c0: cb * 512 + 512],
                                 bf(O_VN + kt * 1024 + h * 64, 64),
                                 bf(O_PT + sb_ * 1024 + c0, 512 - c0),
                                 start=(kt == 0), stop=(kt == nkt - 1)).inc("PE")
                        T.matmul(PS[0:64, (cb + 1) * 512 + c0: (cb + 1) * 512 + 512],
                                 ones64[:, 0:64],
                                 bf(O_PT + sb_ * 1024 + c0, 512 - c0),
                                 start=(kt == 0), stop=(kt == nkt - 1)).inc("PE", ev=f"cx_{b}_{qq}_{h}_{kt}")

                    for kt in range(nkt):
                        emit_scores(kt)
                        emit_exp(kt)
                        if kt > 0:
                            emit_ctx(kt - 1)
                    emit_ctx(nkt - 1)
                    rp = (h % 2) * 64
                    A.wait(f"cx_{b}_{qq}_{h}_{nkt - 1}")
                    A.activation(out=fa(F_STG, 512, rp, 64),
                                 in_=PS[0:64, (cb + 1) * 512:(cb + 1) * 512 + 512],
                                 func=AF.Ln).inc("AC")
                    adr()
                    A.activation(out=fa(F_STG + 512, 512, rp, 64), in_=fa(F_STG, 512, rp, 64),
                                 func=AF.Exp, scale=cval[rp:rp + 64, 3:4]).inc("AC", ev=f"rcp_{b}_{qq}_{h}")
                    V.wait(f"rcp_{b}_{qq}_{h}")
                    V.tensor_tensor(out=bf(O_CXN + pc * 512, 512, po, 64),
                                    in0=PS[0:64, cb * 512:cb * 512 + 512],
                                    in1=fa(F_STG + 512, 512, rp, 64), op=AL.mult)\
                        .inc("DV", ev=f"cxn_{b}_{qq}_{h}")
                if stage == 13:
                    continue
                T.wait(f"cxn_{b}_{qq}_{H - 1}")
                T.wait(f"cxn_{b}_{qq}_{H - 2}")
                for ti in range(4):
                    tt = qq * 4 + ti
                    xrs_off = F_XRS if tt % 2 == 0 else F_RCP
                    slo = xsl(tt)
                    if True:
                        if tt >= 2:
                            SY.wait_if(f"x1p_{b}_{tt - 2}_1")
                        elif tt == 1:
                            SY.wait(f"woc_{b}_7")
                        elif b == 1:
                            SY.wait_if("x1p_0_14_1")
                        SY.dma_start(out=fa(xrs_off, 1024), in_=dx[(b * S + tt * 128):(b * S + (tt + 1) * 128), :])\
                            .inc(("sXR0" if tt % 2 == 0 else "sXR1"), ev=f"xres_{b}_{tt}")
                    for ec in range(2):
                        pbA = (ti % 2) * 4 + ec * 2
                        pbB = pbA + 1
                        if tt >= 2:
                            T.wait(f"x1p_{b}_{tt - 2}_{ec}")
                        elif b == 1:
                            T.wait(f"x1p_0_{14 + tt}_{ec}")
                        for h in range(0, H, 2):
                            T.matmul(bk(pbA), bf(O_CXN + (h // 2) * 512 + ti * 128, 128, 0, 64),
                                     bf(O_WO + (h // 2) * 1024 + ec * 512, 512, 0, 64),
                                     start=(h == 0), stop=False).inc("PE")
                        T.matmul(bk(pbA), onesrb[0:1, :], boT[0:1, ec * 512:(ec + 1) * 512],
                                 start=False, stop=True).inc("PE")
                        for h in range(1, H, 2):
                            T.matmul(bk(pbB), bf(O_CXN + (h // 2) * 512 + ti * 128, 128, 64, 64),
                                     bf(O_WO + (h // 2) * 1024 + ec * 512, 512, 64, 64),
                                     start=(h == 1), stop=(h == H - 1)).inc("PE")
                        K.event(f"wo_{b}_{tt}_{ec}", "PE")
                        V.wait(f"wo_{b}_{tt}_{ec}")
                        if tt >= 4:
                            V.wait_if(f"x1s_{b}_{tt - 4}")
                            V.wait_if(f"h0_{b}_{tt - 4}")
                            V.wait_if(f"o1_{b}_{(tt - 4) | 1}")
                            V.wait_if(f"o16_{b}_{(tt - 4) | 3}")
                        elif b == 1:
                            V.wait_if(f"x1s_0_{12 + tt}")
                            V.wait_if(f"h0_0_{12 + tt}")
                            V.wait_if(f"o1_0_{(12 + tt) | 1}")
                            V.wait_if("o16_0_15")
                        V.wait(f"xres_{b}_{tt}")
                        if stage == 16:
                            V.wait_if(f"o16_{b}_{tt - 1}")
                        V.tensor_copy(out=fa(slo + ec * 512, 512), in_=bk(pbA)).inc("DV")
                        vdr()
                        V.tensor_tensor(out=fa(slo + ec * 512, 512),
                                        in0=fa(slo + ec * 512, 512),
                                        in1=bk(pbB), op=AL.add).inc("DV")
                        vdr()
                        V.tensor_tensor(out=fa(slo + ec * 512, 512),
                                        in0=fa(slo + ec * 512, 512),
                                        in1=fa(xrs_off + ec * 512, 512), op=AL.add)\
                            .inc("DV", ev=f"x1p_{b}_{tt}_{ec}")
                    if stage == 16:
                        SY.wait(f"x1p_{b}_{tt}_1")
                        SY.dma_start(out=dout[(b * S + tt * 128):(b * S + (tt + 1) * 128), :],
                                     in_=fa(slo, 1024)).inc("sDBG", ev=f"o16_{b}_{tt}")
                        continue
                if stage in (14, 16):
                    continue

                def mk_ln1(b_=b, qq_=qq, grp_=0):
                    base = F_X1G if grp_ == 0 else F_XPA

                    def s1():
                        vdr()
                        for k2 in range(2):
                            V.tensor_scalar(out=junk[:, k2 * 1024:k2 * 1024 + 1024],
                                            in0=fa(base + k2 * 1024, 1024),
                                            scalar1=0.0, scalar2=None, op0=AL.add, op1=AL.add,
                                            accum_out=stat[:, k2:k2 + 1]).inc("DV")
                        vdr()
                        V.tensor_single_scalar(out=stat[:, 2:4], in_=stat[:, 0:2], scalar=1.0 / D,
                                               op=AL.mult).inc("DV")
                        vdr()

                    def s2():
                        for k2 in range(2):
                            V.tensor_scalar(out=fa(base + k2 * 1024, 1024),
                                            in0=fa(base + k2 * 1024, 1024),
                                            scalar1=stat[:, 2 + k2:3 + k2], scalar2=None,
                                            op0=AL.subtract, op1=AL.bypass).inc("DV")
                        vdr()

                    def s3():
                        for k2 in range(2):
                            V.scalar_tensor_tensor(out=junk[:, k2 * 1024:k2 * 1024 + 1024],
                                                   in0=fa(base + k2 * 1024, 1024),
                                                   scalar=0.0, in1=fa(base + k2 * 1024, 1024),
                                                   op0=AL.add, op1=AL.mult,
                                                   accum_out=stat[:, 4 + k2:5 + k2]).inc("DV")
                        vdr()
                        V.tensor_scalar(out=stat[:, 6:8], in0=stat[:, 4:6], scalar1=1.0 / D,
                                        scalar2=EPS_LN, op0=AL.mult, op1=AL.add)\
                            .inc("DV", ev=f"v1_{b_}_{qq_}_{grp_}")
                        A.wait(f"v1_{b_}_{qq_}_{grp_}")
                        A.activation(out=stat[:, 8:10], in_=stat[:, 6:8], func=AF.Ln).inc("AC")
                        adr()
                        A.activation(out=stat[:, 6:8], in_=stat[:, 8:10], func=AF.Exp,
                                     scale=cval[:, 2:3]).inc("AC", ev=f"rs1_{b_}_{qq_}_{grp_}")

                    def mk_s4(k2):
                        def s4():
                            tt2 = qq_ * 4 + grp_ * 2 + k2
                            if k2 == 0:
                                V.wait(f"rs1_{b_}_{qq_}_{grp_}")
                            V.tensor_scalar(out=fa(base + k2 * 1024, 1024),
                                            in0=fa(base + k2 * 1024, 1024),
                                            scalar1=stat[:, 6 + k2:7 + k2], scalar2=None,
                                            op0=AL.mult, op1=AL.bypass).inc("DV", ev=f"x1_{b_}_{tt2}")
                            SY.wait(f"x1_{b_}_{tt2}")
                            SY.dma_start(out=dx1[(b_ * S + tt2 * 128):(b_ * S + (tt2 + 1) * 128), :],
                                         in_=fa(base + k2 * 1024, 1024))\
                                .inc(f"sX1S{tt2 % 4}", ev=f"x1s_{b_}_{tt2}")
                            if stage == 1:
                                SY.dma_start(out=dout[(b_ * S + tt2 * 128):(b_ * S + (tt2 + 1) * 128), :],
                                             in_=fa(base + k2 * 1024, 1024)).inc("sDBG", ev=f"o1_{b_}_{tt2}")
                        return s4

                    return [s1, s2, s3, mk_s4(0), mk_s4(1)]

                def mk_ln2(b_=b, qq_=qq, grp_=0):
                    t0 = qq_ * 4 + grp_ * 2
                    base = F_X1G if grp_ == 0 else F_XPA

                    def t1():
                        if t0 >= 2:
                            V.wait_if(f"h0s_{b_}_{t0 - 1}")
                            V.wait_if(f"o2_{b_}_{t0 - 1}")
                        elif b_ == 1:
                            V.wait_if("h0s_0_15")
                            V.wait_if("o2_0_15")
                        vdr()
                        for k2 in range(2):
                            V.tensor_scalar(out=junk[:, k2 * 1024:k2 * 1024 + 1024],
                                            in0=fa(base + k2 * 1024, 1024),
                                            scalar1=0.0, scalar2=None, op0=AL.add, op1=AL.add,
                                            accum_out=stat[:, k2:k2 + 1]).inc("DV")
                        vdr()
                        V.tensor_single_scalar(out=stat[:, 2:4], in_=stat[:, 0:2], scalar=1.0 / D,
                                               op=AL.mult).inc("DV")
                        vdr()

                    def t2():
                        for k2 in range(2):
                            V.tensor_scalar(out=fa(F_H0G + k2 * 1024, 1024),
                                            in0=fa(base + k2 * 1024, 1024),
                                            scalar1=stat[:, 2 + k2:3 + k2], scalar2=None,
                                            op0=AL.subtract, op1=AL.bypass).inc("DV")
                        vdr()

                    def t3():
                        for k2 in range(2):
                            V.scalar_tensor_tensor(out=junk[:, k2 * 1024:k2 * 1024 + 1024],
                                                   in0=fa(F_H0G + k2 * 1024, 1024),
                                                   scalar=0.0, in1=fa(F_H0G + k2 * 1024, 1024),
                                                   op0=AL.add, op1=AL.mult,
                                                   accum_out=stat[:, 4 + k2:5 + k2]).inc("DV")
                        vdr()
                        V.tensor_scalar(out=stat[:, 6:8], in0=stat[:, 4:6], scalar1=1.0 / D,
                                        scalar2=EPS_LN, op0=AL.mult, op1=AL.add)\
                            .inc("DV", ev=f"v2_{b_}_{qq_}_{grp_}")
                        A.wait(f"v2_{b_}_{qq_}_{grp_}")
                        A.activation(out=stat[:, 8:10], in_=stat[:, 6:8], func=AF.Ln).inc("AC")
                        adr()
                        A.activation(out=stat[:, 6:8], in_=stat[:, 8:10], func=AF.Exp,
                                     scale=cval[:, 2:3]).inc("AC", ev=f"rs2_{b_}_{qq_}_{grp_}")

                    def mk_t4(k2):
                        def t4():
                            tt2 = qq_ * 4 + grp_ * 2 + k2
                            if k2 == 0:
                                V.wait(f"rs2_{b_}_{qq_}_{grp_}")
                            V.tensor_scalar(out=fa(F_H0G + k2 * 1024, 1024),
                                            in0=fa(F_H0G + k2 * 1024, 1024),
                                            scalar1=stat[:, 6 + k2:7 + k2], scalar2=None,
                                            op0=AL.mult, op1=AL.bypass).inc("DV")
                            vdr()
                            V.tensor_tensor(out=fa(F_H0G + k2 * 1024, 1024),
                                            in0=fa(F_H0G + k2 * 1024, 1024), in1=lngT[:, :],
                                            op=AL.mult).inc("DV")
                            vdr()
                            V.tensor_tensor(out=fa(F_H0G + k2 * 1024, 1024),
                                            in0=fa(F_H0G + k2 * 1024, 1024), in1=lnbT[:, :],
                                            op=AL.add).inc("DV", ev=f"h0_{b_}_{tt2}")
                            vdr()
                            SY.wait(f"h0_{b_}_{tt2}")
                            SY.dma_start(out=dh0[(b_ * S + tt2 * 128):(b_ * S + (tt2 + 1) * 128), :],
                                         in_=fa(F_H0G + k2 * 1024, 1024)).inc("sH0S", ev=f"h0s_{b_}_{tt2}")
                            if stage == 2:
                                SY.dma_start(out=dout[(b_ * S + tt2 * 128):(b_ * S + (tt2 + 1) * 128), :],
                                             in_=fa(F_H0G + k2 * 1024, 1024)).inc("sDBG", ev=f"o2_{b_}_{tt2}")
                            gt = b_ * 16 + tt2
                            V.tensor_copy(out=S1[:, gt * 32:(gt + 1) * 32],
                                          in_=bass.AP(FA[:, :].tensor, F_H0G + k2 * 1024,
                                                      [[FAW, 128], [32, 32]]))\
                                .inc("DV", ev=f"sub1_{b_}_{tt2}")
                        return t4

                    return [t1, t2, t3, mk_t4(0), mk_t4(1)]

                assert not pend
                pend.extend(mk_ln1(grp_=0))
                if stage != 15:
                    pend.extend(mk_ln2(grp_=0))
                pend.extend(mk_ln1(grp_=1))
                if stage != 15:
                    pend.extend(mk_ln2(grp_=1))
                if qq == 3:
                    while pend:
                        pend.pop(0)()

            if b == 0 and stage != 12:
                emit_gamma_finish()

        if stage in (12, 13, 14, 15, 16):
            if stage == 16:
                G.wait(f"o16_{BPC - 1}_{15}")
                return
            finish_debug()
            return
        if stage <= 2:
            if stage == 1:
                G.wait(f"o1_{BPC - 1}_{15}")
            else:
                G.wait(f"o2_{BPC - 1}_{15}")
            return

        V.wait(f"sub1_{BPC - 1}_{15}")
        emit_bisect(S1, 1024, 5, "q1", "sub1_1_15")
        emit_stats_to_ario("q1")
        emit_ar(0, "q1")
        emit_thresholds("q1", 1024)
        for tt in range(NT):
            sl = tt % 2
            if tt >= 2:
                SY.wait(f"q1ct_{tt - 2}")
            elif tt == 0:
                SY.wait_if("cxn_1_3_15")
                SY.wait_if("h0s_1_15")
            else:
                SY.wait_if("x1p_1_15_1")
            SY.dma_start(out=fa(F_STG + sl * 1024, 1024), in_=dh0[tt * 128:(tt + 1) * 128, :])\
                .inc(("sQ1a" if sl == 0 else "sQ1b"), ev=f"q1l_{tt}")
            A.wait(f"q1l_{tt}")
            if tt == 0:
                A.wait("q1_rep")
                A.wait_if("h0s_1_15")
            elif tt >= 2:
                A.wait(f"q1ct_{tt - 2}")
            A.activation(out=fa(F_H0G + sl * 1024, 1024), in_=fa(F_STG + sl * 1024, 1024), func=AF.Abs)\
                .inc("AC", ev=f"q1a_{tt}")
            V.wait(f"q1a_{tt}")
            for j in range(4):
                V.tensor_scalar(out=bf(O_XQT + (j + 4 * sl) * 1024, 1024), in0=fa(F_H0G + sl * 1024, 1024),
                                scalar1=thr4[:, j:j + 1],
                                scalar2=None, op0=AL.is_gt, op1=AL.add,
                                accum_out=cntall[:, j * 64 + tt:j * 64 + tt + 1])\
                    .inc("DV", ev=(f"q1ct_{tt}" if j == 3 else None))
        vdr()
        emit_counts_finish("q1c", NT)
        emit_ar(1, "q1c")
        emit_interp("q1c", G1_STAR, N1, 0, 12)
        V.tensor_copy(out=dbgT[:, 0:4], in_=thr4[:, :]).inc("DV")
        V.tensor_copy(out=dbgT[:, 4:8], in_=cnt4[:, :]).inc("DV")
        V.tensor_copy(out=dbgT[:, 8:10], in_=sca[:, 0:2]).inc("DV")
        V.tensor_copy(out=dbgT[:, 10:12], in_=gq[:, 0:2]).inc("DV")
        V.tensor_copy(out=dbgT[0:1, 12:20], in_=ario[0:1, :]).inc("DV")

        if stage == 21:
            finish_debug()
            return
        for grp in range(8):
            for ti in range(4):
                tt = grp * 4 + ti
                sl = ti % 2
                if ti == 0 and grp >= 1:
                    V.wait(f"xqt_{grp - 1}_7")
                if tt >= 2:
                    SY.wait(f"xq_{tt - 2}")
                else:
                    SY.wait(f"q1ct_{30 + tt}")
                SY.dma_start(out=fa(F_STG + sl * 1024, 1024), in_=dh0[tt * 128:(tt + 1) * 128, :])\
                    .inc(("sXQa" if sl == 0 else "sXQb"), ev=f"xql_{tt}")
                V.wait(f"xql_{tt}")
                if tt == 0:
                    V.wait("q1c_screp")
                V.tensor_scalar(out=fa(F_H0G, 1024), in0=fa(F_STG + sl * 1024, 1024),
                                scalar1=sca[:, 0:1], scalar2=MAGIC, op0=AL.mult, op1=AL.add).inc("DV")
                vdr()
                V.tensor_scalar(out=fa(F_H0G, 1024), in0=fa(F_H0G, 1024),
                                scalar1=MAGIC, scalar2=-128.0, op0=AL.subtract, op1=AL.max).inc("DV")
                vdr()
                V.tensor_single_scalar(out=bf(O_XQG + ti * 1024, 1024), in_=fa(F_H0G, 1024),
                                       scalar=127.0, op=AL.min).inc("DV", ev=f"xq_{tt}")
                vdr()
            PSB = PS[:, :].bitcast(BF16)
            for dc in range(8):
                pb = dc % 2
                for ti in range(4):
                    T.wait(f"xq_{grp * 4 + ti}")
                if dc >= 2:
                    T.wait(f"xqe_{grp}_{dc - 2}")
                for ti in range(4):
                    T.transpose(PSB[:, pb * 1024 + ti * 128: pb * 1024 + (ti + 1) * 128],
                                bf(O_XQG + ti * 1024 + dc * 128, 128),
                                identT[:, :]).inc("PE", ev=(f"xqt_{grp}_{dc}" if ti == 3 else None))
                V.wait(f"xqt_{grp}_{dc}")
                V.tensor_copy(out=bf(O_XQT + dc * 4096 + grp * 512, 512),
                              in_=PSB[:, pb * 1024: pb * 1024 + 512])\
                    .inc("DV", ev=f"xqe_{grp}_{dc}")

        if stage == 22:
            finish_debug()
            return
        for ci in range(16):
            wsl = F_STG if ci % 2 == 0 else F_X1G
            if ci >= 2:
                SY.wait(f"w1q_{ci - 2}")
                G.wait(f"w1q_{ci - 2}")
            else:
                SY.wait("xq_31")
                G.wait("xq_31")
                for eng_ in (SY, G):
                    eng_.wait_if("x1s_1_12")
                    eng_.wait_if("x1s_1_13")
                    eng_.wait_if("h0_1_13")
                    eng_.wait_if("o1_1_13")
            SY.dma_start(out=fa(wsl, 1024), in_=dw1[:, ci * 2048:ci * 2048 + 1024])\
                .inc("sW1L" if ci % 2 == 0 else "sW1La", ev=f"w1lA_{ci}")
            G.dma_start(out=fa(wsl + 1024, 1024), in_=dw1[:, ci * 2048 + 1024:ci * 2048 + 2048])\
                .inc("sW1G" if ci % 2 == 0 else "sW1Gb", ev=f"w1l_{ci}")
            V.wait(f"w1lA_{ci}")
            V.wait(f"w1l_{ci}")
            if ci == 0:
                V.wait("gq_rep")
            V.tensor_scalar(out=fa(F_H0G, 1024), in0=fa(wsl, 1024), scalar1=gq[:, 0:1],
                            scalar2=MAGIC, op0=AL.mult, op1=AL.add).inc("DV")
            V.tensor_scalar(out=fa(F_H0G + 1024, 1024), in0=fa(wsl + 1024, 1024), scalar1=gq[:, 0:1],
                            scalar2=MAGIC, op0=AL.mult, op1=AL.add).inc("DV")
            vdr()
            V.tensor_scalar(out=fa(F_H0G, 2048), in0=fa(F_H0G, 2048),
                            scalar1=MAGIC, scalar2=-1.0, op0=AL.subtract, op1=AL.max).inc("DV")
            vdr()
            V.tensor_single_scalar(out=bf(O_W1Q + ci * 2048, 2048), in_=fa(F_H0G, 2048),
                                   scalar=1.0, op=AL.min).inc("DV", ev=f"w1q_{ci}")
            vdr()

        if stage == 23:
            finish_debug()
            return
        for jt in range(32):
            for tc in range(8):
                pb = (jt * 8 + tc) % 2
                if jt == 0 and tc == 0:
                    T.wait("w1q_15")
                    T.wait("xqe_7_7")
                if jt * 8 + tc >= 2:
                    T.wait(f"ge_{(jt * 8 + tc - 2) // 8}_{(jt * 8 + tc - 2) % 8}")
                for dc in range(8):
                    T.matmul(bk(pb), bf(O_W1Q + dc * 4096 + jt * 128, 128),
                             bf(O_XQT + dc * 4096 + tc * 512, 512),
                             start=(dc == 0), stop=(dc == 7)).inc("PE", ev=(f"m1_{jt}_{tc}" if dc == 7 else None))
                A.wait(f"m1_{jt}_{tc}")
                hsl = tc % 2
                if jt * 8 + tc >= 2:
                    A.wait(f"hs_{(jt * 8 + tc - 2) // 8}_{(jt * 8 + tc - 2) % 8}")
                    A.wait(f"sub2_{(jt * 8 + tc - 2) // 8}_{(jt * 8 + tc - 2) % 8}")
                else:
                    A.wait_if(f"x1s_1_{12 + hsl}")
                A.activation(out=fa(F_X1G + hsl * 1024, 512), in_=bk(pb), func=AF.Gelu,
                             scale=sca[:, 1:2]).inc("AC", ev=f"ge_{jt}_{tc}")
                speng = SY if hsl == 0 else G
                speng.wait(f"ge_{jt}_{tc}")
                speng.dma_start(out=bass.AP(dh.tensor, jt * (128 * TOK) + tc * 512, [[TOK, 128], [1, 512]]),
                                in_=fa(F_X1G + hsl * 1024, 512))\
                    .inc(("sHSa" if hsl == 0 else "sHSb"), ev=f"hs_{jt}_{tc}")
                V.wait(f"ge_{jt}_{tc}")
                V.tensor_copy(out=S2[:, jt * 32 + tc * 4: jt * 32 + tc * 4 + 4],
                              in_=bass.AP(FA[:, :].tensor, F_X1G + hsl * 1024, [[FAW, 128], [128, 4]]))\
                    .inc("DV", ev=f"sub2_{jt}_{tc}")

        if stage == 24:
            finish_debug()
            return
        V.wait("sub2_31_7")
        emit_bisect(S2, 1024, 5, "q2", "sub2_31_7")
        emit_stats_to_ario("q2")
        emit_ar(2, "q2")
        emit_thresholds("q2", 1024)
        for pi in range(32):
            for hf in range(2):
                ii = pi * 2 + hf
                pr = ii % 2
                if ii >= 2:
                    SY.wait(f"q2ct_{ii - 2}")
                    G.wait(f"q2ct_{ii - 2}")
                elif ii == 0:
                    for eng_ in (SY, G):
                        eng_.wait("hs_31_6")
                        eng_.wait("hs_31_7")
                        eng_.wait("sub2_31_7")
                ldA = fa(F_STG, 1024) if pr == 0 else bff(0, 1024)
                ldB = fa(F_X1G, 1024) if pr == 0 else bff(2048, 1024)
                dst = fa(F_H0G, 2048) if pr == 0 else bff(4096, 2048)
                SY.dma_start(out=ldA,
                             in_=bass.AP(dh.tensor, pi * (128 * TOK) + hf * 2048, [[TOK, 128], [1, 1024]]))\
                    .inc("sQ2L" if pr == 0 else "sQ2c", ev=f"q2lA_{ii}")
                G.dma_start(out=ldB,
                            in_=bass.AP(dh.tensor, pi * (128 * TOK) + hf * 2048 + 1024, [[TOK, 128], [1, 1024]]))\
                    .inc("sQ2b" if pr == 0 else "sQ2d", ev=f"q2l_{ii}")
                A.wait(f"q2lA_{ii}")
                A.wait(f"q2l_{ii}")
                if ii == 0:
                    A.wait("q2_rep")
                elif ii >= 2:
                    A.wait(f"q2ct_{ii - 2}")
                A.activation(out=dst[:, 0:1024], in_=ldA, func=AF.Abs).inc("AC")
                A.activation(out=dst[:, 1024:2048], in_=ldB, func=AF.Abs).inc("AC", ev=f"q2a_{ii}")
                V.wait(f"q2a_{ii}")
                for j in range(4):
                    V.tensor_scalar(out=bf(16384 + (j + 4 * pr) * 2048, 2048), in0=dst,
                                    scalar1=thr4[:, j:j + 1],
                                    scalar2=None, op0=AL.is_gt, op1=AL.add,
                                    accum_out=cntall[:, j * 64 + ii:j * 64 + ii + 1])\
                        .inc("DV", ev=(f"q2ct_{ii}" if j == 3 else None))
        vdr()
        emit_counts_finish("q2c", 64)
        emit_ar(3, "q2c")
        emit_interp("q2c", G2_STAR, N2, 2, 13)
        V.tensor_copy(out=dbgT[:, 20:24], in_=thr4[:, :]).inc("DV")
        V.tensor_copy(out=dbgT[:, 24:28], in_=cnt4[:, :]).inc("DV")
        V.tensor_copy(out=dbgT[:, 28:30], in_=sca[:, 2:4]).inc("DV")
        V.tensor_copy(out=dbgT[0:1, 30:38], in_=ario[0:1, :]).inc("DV", ev="dbg_done")

        if stage == 25:
            finish_debug()
            return
        for ci in range(16):
            wsl = F_STG if ci % 2 == 0 else F_X1G
            if ci >= 2:
                SY.wait(f"w2q_{ci - 2}")
                G.wait(f"w2q_{ci - 2}")
            else:
                SY.wait("q2a_63")
                G.wait("q2a_63")
            SY.dma_start(out=fa(wsl, 1024), in_=dw2[:, ci * 2048:ci * 2048 + 1024])\
                .inc("sW2L" if ci % 2 == 0 else "sW2La", ev=f"w2lA_{ci}")
            G.dma_start(out=fa(wsl + 1024, 1024), in_=dw2[:, ci * 2048 + 1024:ci * 2048 + 2048])\
                .inc("sW2G" if ci % 2 == 0 else "sW2Gb", ev=f"w2l_{ci}")
            V.wait(f"w2lA_{ci}")
            V.wait(f"w2l_{ci}")
            if ci == 0:
                V.wait("q2c_screp")
            V.tensor_scalar(out=fa(F_H0G, 1024), in0=fa(wsl, 1024), scalar1=gq[:, 1:2],
                            scalar2=MAGIC, op0=AL.mult, op1=AL.add).inc("DV")
            V.tensor_scalar(out=fa(F_H0G + 1024, 1024), in0=fa(wsl + 1024, 1024), scalar1=gq[:, 1:2],
                            scalar2=MAGIC, op0=AL.mult, op1=AL.add).inc("DV")
            vdr()
            V.tensor_scalar(out=fa(F_H0G, 2048), in0=fa(F_H0G, 2048),
                            scalar1=MAGIC, scalar2=-1.0, op0=AL.subtract, op1=AL.max).inc("DV")
            vdr()
            V.tensor_single_scalar(out=bf(O_W2Q + ci * 2048, 2048), in_=fa(F_H0G, 2048),
                                   scalar=1.0, op=AL.min).inc("DV", ev=f"w2q_{ci}")
            vdr()

        if stage == 26:
            finish_debug()
            return
        for tb in range(32):
            pr = tb % 2
            hblk = fa(0, 4096) if pr == 0 else bff(40960, 4096)
            hqs = O_HQ + pr * 4096
            ogc = 4096 if pr == 0 else None
            ogx = fa(4096, 1024) if pr == 0 else bff(51200, 1024)
            oox = fa(5120, 1024) if pr == 0 else bff(53248, 1024)
            x1r = fa(6144, 1024) if pr == 0 else bff(49152, 1024)
            sldx = "sX1R" if pr == 0 else "sX1R2"
            for jt in range(32):
                if jt == 0:
                    if tb >= 2:
                        SY.wait(f"hqq_{tb - 2}")
                    elif tb == 0:
                        SY.wait("w2q_15")
                SY.dma_start(out=hblk[:, jt * 128:(jt + 1) * 128],
                             in_=bass.AP(dh.tensor, jt * (128 * TOK) + tb * 128, [[TOK, 128], [1, 128]]))\
                    .inc("sHB" if pr == 0 else "sHB3",
                         ev=(f"hbl_{tb}" if jt == 31 else None))
            V.wait(f"hbl_{tb}")
            if tb >= 2:
                V.wait(f"m2_{tb - 2}_1")
            V.tensor_scalar(out=hblk, in0=hblk, scalar1=sca[:, 2:3],
                            scalar2=MAGIC, op0=AL.mult, op1=AL.add).inc("DV")
            vdr()
            V.tensor_scalar(out=hblk, in0=hblk,
                            scalar1=MAGIC, scalar2=-128.0, op0=AL.subtract, op1=AL.max).inc("DV")
            vdr()
            V.tensor_single_scalar(out=bf(hqs, 4096), in_=hblk, scalar=127.0, op=AL.min)\
                .inc("DV", ev=f"hqq_{tb}")
            if tb >= 2:
                SY.wait(f"oge_{tb - 2}_1")
            elif tb == 0:
                for k4 in range(4):
                    SY.wait_if(f"x1s_1_{12 + k4}")
            SY.dma_start(out=x1r, in_=dx1[tb * 128:(tb + 1) * 128, :])\
                .inc(sldx, ev=f"x1r_{tb}")
            for ec in range(2):
                pb = pr * 2 + ec
                T.wait(f"hqq_{tb}")
                if tb >= 2:
                    T.wait(f"oge_{tb - 2}_{ec}")
                for jc in range(32):
                    T.matmul(bk(pb), bf(hqs + jc * 128, 128),
                             bf(O_W2Q + jc * 1024 + ec * 512, 512),
                             start=(jc == 0), stop=(jc == 31)).inc("PE", ev=(f"m2_{tb}_{ec}" if jc == 31 else None))
                V.wait(f"m2_{tb}_{ec}")
                V.wait(f"x1r_{tb}")
                V.tensor_scalar(out=ogx[:, ec * 512:(ec + 1) * 512], in0=bk(pb), scalar1=sca[:, 3:4],
                                scalar2=None, op0=AL.mult, op1=AL.bypass).inc("DV")
                vdr()
                V.tensor_tensor(out=ogx[:, ec * 512:(ec + 1) * 512], in0=ogx[:, ec * 512:(ec + 1) * 512],
                                in1=x1r[:, ec * 512:(ec + 1) * 512], op=AL.add).inc("DV", ev=f"oge_{tb}_{ec}")
            vdr()
            V.tensor_scalar(out=junk[:, 0:1024], in0=ogx, scalar1=0.0, scalar2=None,
                            op0=AL.add, op1=AL.add, accum_out=stat[:, 0:1]).inc("DV")
            vdr()
            V.tensor_single_scalar(out=stat[:, 1:2], in_=stat[:, 0:1], scalar=1.0 / D, op=AL.mult).inc("DV")
            vdr()
            V.tensor_scalar(out=ogx, in0=ogx, scalar1=stat[:, 1:2],
                            scalar2=None, op0=AL.subtract, op1=AL.bypass).inc("DV")
            vdr()
            V.scalar_tensor_tensor(out=junk[:, 0:1024], in0=ogx, scalar=0.0, in1=ogx,
                                   op0=AL.add, op1=AL.mult, accum_out=stat[:, 2:3]).inc("DV")
            vdr()
            V.tensor_scalar(out=stat[:, 3:4], in0=stat[:, 2:3], scalar1=1.0 / D, scalar2=EPS_LN,
                            op0=AL.mult, op1=AL.add).inc("DV", ev=f"v3_{tb}")
            A.wait(f"v3_{tb}")
            A.activation(out=stat[:, 4:5], in_=stat[:, 3:4], func=AF.Ln).inc("AC")
            adr()
            A.activation(out=stat[:, 3:4], in_=stat[:, 4:5], func=AF.Exp, scale=cval[:, 2:3])\
                .inc("AC", ev=f"rs3_{tb}")
            V.wait(f"rs3_{tb}")
            if tb >= 2:
                V.wait(f"os_{tb - 2}")
            V.tensor_scalar(out=oox, in0=ogx, scalar1=stat[:, 3:4],
                            scalar2=None, op0=AL.mult, op1=AL.bypass).inc("DV", ev=f"o3_{tb}")
            G.wait(f"o3_{tb}")
            G.dma_start(out=dout[tb * 128:(tb + 1) * 128, :], in_=oox)\
                .inc("gOUT" if pr == 0 else "sDBG", ev=f"os_{tb}")
        G.wait("dbg_done")
        G.dma_start(out=ddbg, in_=dbgT[:, :]).inc("gOUT", ev="dbg_st")
        G.wait("dbg_st")
        G.wait("os_31")

    K.begin("___dry___")
    program()

    s_ctx = [stack.enter_context(nc.semaphore("sem_" + n)) for n in K.sem_names]
    K.sems = dict(zip(K.sem_names, s_ctx))
    with nc.Block() as block:

        @block.sync
        def _(eng):
            K.engines["sync"] = eng
            K.begin("sync")
            program()

        @block.vector
        def _(eng):
            K.engines["vector"] = eng
            K.begin("vector")
            program()

        @block.scalar
        def _(eng):
            K.engines["scalar"] = eng
            K.begin("scalar")
            program()

        @block.tensor
        def _(eng):
            K.engines["tensor"] = eng
            K.begin("tensor")
            program()

        @block.gpsimd
        def _(eng):
            K.engines["gpsimd"] = eng
            K.begin("gpsimd")
            program()

    stack.close()
    return nc


_PROGRAM_CACHE = {}


def _get_program(stage=99):
    if stage not in _PROGRAM_CACHE:
        _PROGRAM_CACHE[stage] = build_program(stage)
    return _PROGRAM_CACHE[stage]


def kernel(L0_attn_x, L0_attn_Wq, L0_attn_bq, L0_attn_Wk, L0_attn_bk, L0_attn_Wv,
           L0_attn_bv, L0_attn_Wo, L0_attn_bo, L0_ffn_ln_g, L0_ffn_ln_b,
           L0_ffn_w1, L0_ffn_w2, _stage=99):
    x = np.ascontiguousarray(np.asarray(L0_attn_x, dtype=np.float32))
    Wq = np.asarray(L0_attn_Wq, dtype=np.float32)
    Wk = np.asarray(L0_attn_Wk, dtype=np.float32)
    Wv = np.asarray(L0_attn_Wv, dtype=np.float32)
    Wo = np.asarray(L0_attn_Wo, dtype=np.float32)
    bq = np.asarray(L0_attn_bq, dtype=np.float32)
    bk_ = np.asarray(L0_attn_bk, dtype=np.float32)
    bv = np.asarray(L0_attn_bv, dtype=np.float32)
    bo = np.asarray(L0_attn_bo, dtype=np.float32)
    g = np.asarray(L0_ffn_ln_g, dtype=np.float32)
    be = np.asarray(L0_ffn_ln_b, dtype=np.float32)
    w1 = np.asarray(L0_ffn_w1, dtype=np.float32)
    w2 = np.asarray(L0_ffn_w2, dtype=np.float32)

    perm = _perm_index()
    Wq_r = _reshape_et(np.ascontiguousarray(Wq[:, perm]))
    Wk_r = _reshape_et(np.ascontiguousarray(Wk[:, perm]))
    Wv_r = _reshape_r(Wv)
    Wo_r = _reshape_r(Wo)
    bqp = np.ascontiguousarray(bq[perm].reshape(8, 128).T)
    bkp = np.ascontiguousarray(bk_[perm].reshape(8, 128).T)
    w1T_r = _reshape_r(np.ascontiguousarray(w1.T))
    w2T_r = _reshape_r(np.ascontiguousarray(w2.T))
    lng = np.ascontiguousarray(np.tile(g[None, :], (128, 1)))
    lnb = np.ascontiguousarray(np.tile(be[None, :], (128, 1)))
    tabs = _host_tables()

    nc = _get_program(_stage)
    in_maps = []
    for c in range(NCORES):
        xc = x[2 * c:2 * c + 2]
        xflat = np.ascontiguousarray(xc.reshape(TOK, D))
        xT_r = np.stack([_reshape_r(np.ascontiguousarray(xc[bb].T)) for bb in range(BPC)])
        in_maps.append(dict(
            x=xflat, xT_r=xT_r, Wq_r=Wq_r, Wk_r=Wk_r, Wv_r=Wv_r, Wo_r=Wo_r,
            bqp=bqp, bkp=bkp, bv=bv.reshape(1, D), bo=bo.reshape(1, D),
            lng=lng, lnb=lnb, w1T_r=w1T_r, w2T_r=w2T_r,
            cos128=tabs["cos128"], sin128=tabs["sin128"], ident=tabs["ident"],
            u2=tabs["u2"], dneg=tabs["dneg"], ones_row_b=tabs["ones_row_b"],
            ones64_b=tabs["ones64_b"], ones_col_f=tabs["ones_col_f"],
            ones_row_f=tabs["ones_row_f"], cvals=tabs["cvals"],
        ))
    res = run_bass_kernel_spmd(nc, in_maps, core_ids=list(range(NCORES)))
    outs = [res.results[c]["out"].reshape(BPC, S, D) for c in range(NCORES)]
    return np.ascontiguousarray(np.concatenate(outs, axis=0), dtype=np.float32)


# version 12
# speedup vs baseline: 1.0369x; 1.0078x over previous
import math
import numpy as np
import ml_dtypes

import concourse.bass as bass
import concourse.mybir as mybir
from concourse.bass_utils import run_bass_kernel_spmd

F32 = mybir.dt.float32
BF16 = mybir.dt.bfloat16
U8 = mybir.dt.uint8
AL = mybir.AluOpType
AF = mybir.ActivationFunctionType
AX = mybir.AxisListType.X

NCORES = 8
S = 2048
D = 1024
H = 16
HD = 64
HID = 4096
BPC = 2
TOK = BPC * S
NT = TOK // 128
EXP_SHIFT = 20.0
MAGIC = 12582912.0
EPS_LN = 1e-5
N1 = 16 * S * D
N2 = 16 * S * HID
G1_STAR = float((N1 - 1) * 0.005)
G2_STAR = float((N2 - 1) * 0.005)
NBIS = 20
FAW = 9216


class _Inst:
    __slots__ = ("k", "inst")

    def __init__(self, k, inst):
        self.k, self.inst = k, inst

    def inc(self, sem, amt=None, ev=None):
        k = self.k
        amt = amt if amt is not None else (16 if sem in k.dma_sems else 1)
        k.counts[sem] += amt
        if self.inst is not None:
            self.inst.then_inc(k.sems[sem], amt)
        if ev is not None:
            k.event(ev, sem)
        return self


class _Eng:
    def __init__(self, k, name):
        self.k, self.name = k, name

    @property
    def real(self):
        return self.k.cur == self.name

    def __getattr__(self, attr):
        k = self.k
        if self.real:
            m = getattr(k.engines[self.name], attr)
            return lambda *a, **kw: _Inst(k, m(*a, **kw))
        return lambda *a, **kw: _Inst(k, None)

    def wait(self, ev, extra=0):
        if self.real:
            sem, val = self.k.events[ev]
            self.k.engines[self.name].wait_ge(self.k.sems[sem], val + extra)

    def wait_if(self, ev):
        if self.real and ev in self.k.events:
            sem, val = self.k.events[ev]
            self.k.engines[self.name].wait_ge(self.k.sems[sem], val)

    def drain(self):
        if self.real:
            self.k.engines[self.name].drain()


class _KB:
    DMA_STREAMS = ["sCON", "sGA", "sGB", "sXT0", "sXT1", "sXT2", "sXT3",
                   "sWS0", "sWS1", "sWS2", "sWS3", "sWS4", "sWS5", "sWS6", "sWS7",
                   "sWV", "sWV2", "sWV3", "sWV4",
                   "sWO0", "sWO1", "sWO2", "sWO3", "sWO4", "sWO5", "sWO6", "sWO7",
                   "sXR0", "sXR1",
                   "sX1S0", "sX1S1", "sX1S2", "sX1S3", "sH0S", "sDBG",
                   "sQ1a", "sQ1b", "sXQa", "sXQb", "sW1L", "sW1La", "sW1G", "sW1Gb",
                   "sHSa", "sHSb", "sQ2L", "sQ2b", "sQ2c", "sQ2d",
                   "sW2L", "sW2La", "sW2G", "sW2Gb",
                   "sHB", "sHB2", "sHB3", "sHB4", "sX1R", "sX1R2",
                   "gPRM", "aPRM", "gPR2", "aPR2", "gARB", "gOUT"]

    def __init__(self):
        self.dma_sems = set(self.DMA_STREAMS)
        self.sem_names = self.DMA_STREAMS + ["PE", "DV", "AC", "GP", "CC"]
        self.sems = {}
        self.events = {}
        self.counts = {}
        self.cur = "___dry___"
        self.engines = {}

    @property
    def dry(self):
        return self.cur == "___dry___"

    def begin(self, name):
        self.cur = name
        self.counts = {s: 0 for s in self.sem_names}

    def event(self, name, sem):
        v = (sem, self.counts[sem])
        if self.dry:
            assert name not in self.events, f"duplicate event {name}"
            self.events[name] = v
        else:
            assert self.events.get(name) == v, (self.cur, name, self.events.get(name), v)


def _host_tables():
    pos = np.arange(S, dtype=np.float64)
    inv_freq = np.exp(np.arange(0, HD, 2, dtype=np.float64) * (-math.log(10000.0) / HD))
    ang = pos[:, None] * inv_freq[None, :]
    cos128 = np.ascontiguousarray(np.tile(np.cos(ang).T, (4, 1))).astype(ml_dtypes.bfloat16)
    sin128 = np.ascontiguousarray(np.tile(np.sin(ang).T, (4, 1))).astype(ml_dtypes.bfloat16)
    ident = np.eye(128, dtype=ml_dtypes.bfloat16)
    u2 = (np.arange(128)[None, :] < np.arange(128)[:, None]).astype(ml_dtypes.bfloat16)
    dneg = (np.eye(128) * -10000.0).astype(ml_dtypes.bfloat16)
    ones_row_b = np.ones((1, 128), dtype=ml_dtypes.bfloat16)
    ones64_b = np.ones((128, 64), dtype=ml_dtypes.bfloat16)
    ones_col_f = np.ones((128, 1), dtype=np.float32)
    ones_row_f = np.ones((1, 128), dtype=np.float32)
    cvals = np.tile(np.array([[-EXP_SHIFT, 0.125, -0.5, -1.0, 0.5, 1.0, 0.0, 0.0]],
                             dtype=np.float32), (128, 1))
    return dict(cos128=cos128, sin128=sin128, ident=ident, u2=u2, dneg=dneg,
                ones_row_b=ones_row_b, ones64_b=ones64_b, ones_col_f=ones_col_f,
                ones_row_f=ones_row_f, cvals=cvals)


def _perm_index():
    idx = np.zeros(D, dtype=np.int64)
    for g in range(4):
        for j in range(4):
            h = 4 * g + j
            idx[(2 * g) * 128 + j * 32:(2 * g) * 128 + (j + 1) * 32] = np.arange(h * HD, h * HD + 32)
            idx[(2 * g + 1) * 128 + j * 32:(2 * g + 1) * 128 + (j + 1) * 32] = np.arange(h * HD + 32, h * HD + 64)
    return idx


def _reshape_r(w):
    R, C = w.shape
    nch = R // 128
    return np.ascontiguousarray(w.reshape(nch, 128, C).transpose(1, 0, 2).reshape(128, nch * C))


def _reshape_et(w):
    return np.ascontiguousarray(
        w.reshape(8, 128, 8, 128).transpose(1, 2, 0, 3).reshape(128, 8192))


def build_program(stage=99):
    nc = bass.Bass("TRN2", target_bir_lowering=False)

    dx = nc.dram_tensor("x", [TOK, D], F32, kind="ExternalInput").ap()
    dxT = nc.dram_tensor("xT_r", [BPC, 128, 8 * S], F32, kind="ExternalInput").ap()
    dWq = nc.dram_tensor("Wq_r", [128, 8 * D], F32, kind="ExternalInput").ap()
    dWk = nc.dram_tensor("Wk_r", [128, 8 * D], F32, kind="ExternalInput").ap()
    dWv = nc.dram_tensor("Wv_r", [128, 8 * D], F32, kind="ExternalInput").ap()
    dWo = nc.dram_tensor("Wo_r", [128, 8 * D], F32, kind="ExternalInput").ap()
    dbqp = nc.dram_tensor("bqp", [128, 8], F32, kind="ExternalInput").ap()
    dbkp = nc.dram_tensor("bkp", [128, 8], F32, kind="ExternalInput").ap()
    dbv = nc.dram_tensor("bv", [1, D], F32, kind="ExternalInput").ap()
    dbo = nc.dram_tensor("bo", [1, D], F32, kind="ExternalInput").ap()
    dlng = nc.dram_tensor("lng", [128, D], F32, kind="ExternalInput").ap()
    dlnb = nc.dram_tensor("lnb", [128, D], F32, kind="ExternalInput").ap()
    dw1 = nc.dram_tensor("w1T_r", [128, 8 * HID], F32, kind="ExternalInput").ap()
    dw2 = nc.dram_tensor("w2T_r", [128, 32 * D], F32, kind="ExternalInput").ap()
    dcos = nc.dram_tensor("cos128", [128, S], BF16, kind="ExternalInput").ap()
    dsin = nc.dram_tensor("sin128", [128, S], BF16, kind="ExternalInput").ap()
    dident = nc.dram_tensor("ident", [128, 128], BF16, kind="ExternalInput").ap()
    du2 = nc.dram_tensor("u2", [128, 128], BF16, kind="ExternalInput").ap()
    ddneg = nc.dram_tensor("dneg", [128, 128], BF16, kind="ExternalInput").ap()
    donesrb = nc.dram_tensor("ones_row_b", [1, 128], BF16, kind="ExternalInput").ap()
    dones64 = nc.dram_tensor("ones64_b", [128, 64], BF16, kind="ExternalInput").ap()
    donescf = nc.dram_tensor("ones_col_f", [128, 1], F32, kind="ExternalInput").ap()
    donesrf = nc.dram_tensor("ones_row_f", [1, 128], F32, kind="ExternalInput").ap()
    dcval = nc.dram_tensor("cvals", [128, 8], F32, kind="ExternalInput").ap()
    dout = nc.dram_tensor("out", [TOK, D], F32, kind="ExternalOutput").ap()
    ddbg = nc.dram_tensor("dbg", [128, 40], F32, kind="ExternalOutput").ap()

    dx1 = nc.dram_tensor("x1sp", [TOK, D], F32).ap()
    dh0 = nc.dram_tensor("h0sp", [TOK, D], F32).ap()
    dh = nc.dram_tensor("hsp", [32, 128, TOK], F32).ap()
    cbi = [nc.dram_tensor("cbi%d" % i, [1, 8], F32).ap() for i in range(4)]
    cbo = [nc.dram_tensor("cbo%d" % i, [1, 8], F32).ap() for i in range(4)]

    K = _KB()
    V = _Eng(K, "vector")
    A = _Eng(K, "scalar")
    T = _Eng(K, "tensor")
    G = _Eng(K, "gpsimd")
    SY = _Eng(K, "sync")

    import contextlib
    stack = contextlib.ExitStack()

    def sbuf(name, shape, dtype):
        return stack.enter_context(nc.sbuf_tensor(name, shape, dtype))

    cosT = sbuf("cosT", [128, S], BF16)
    sinT = sbuf("sinT", [128, S], BF16)
    identT = sbuf("identT", [128, 128], BF16)
    u2T = sbuf("u2T", [128, 128], BF16)
    dnegT = sbuf("dnegT", [128, 128], BF16)
    onesrb = sbuf("onesrb", [1, 128], BF16)
    ones64 = sbuf("ones64", [128, 64], BF16)
    onescf = sbuf("onescf", [128, 1], F32)
    onesrf = sbuf("onesrf", [1, 128], F32)
    cval = sbuf("cval", [128, 8], F32)
    bqpT = sbuf("bqpT", [128, 8], F32)
    bkpT = sbuf("bkpT", [128, 8], F32)
    bvT = sbuf("bvT", [1, D], BF16)
    boT = sbuf("boT", [1, D], BF16)
    lngT = sbuf("lngT", [128, D], F32)
    lnbT = sbuf("lnbT", [128, D], F32)
    S1 = sbuf("S1", [128, 1024], BF16)
    S2 = sbuf("S2", [128, 1024], BF16)
    qlo = sbuf("qlo", [128, 1], F32)
    qhi = sbuf("qhi", [128, 1], F32)
    qmid = sbuf("qmid", [128, 1], F32)
    qnmid = sbuf("qnmid", [128, 1], F32)
    qcnt = sbuf("qcnt", [128, 1], F32)
    qcnt2 = sbuf("qcnt2", [128, 1], F32)
    qmsk = sbuf("qmsk", [128, 1], U8)
    qvec = sbuf("qvec", [128, 2], F32)
    ario = sbuf("ario", [1, 8], F32)
    rowa = sbuf("rowa", [1, 16], F32)
    rowb = sbuf("rowb", [1, 16], F32)
    thr4 = sbuf("thr4", [128, 4], F32)
    cntall = sbuf("cntall", [128, 256], F32)
    cnt4 = sbuf("cnt4", [128, 4], F32)
    sca = sbuf("sca", [128, 4], F32)
    gq = sbuf("gq", [128, 2], F32)
    gsum = sbuf("gsum", [128, 34], F32)
    stat = sbuf("stat", [128, 16], F32)
    junk = sbuf("junk", [128, 2048], BF16)

    dbgT = sbuf("dbgT", [128, 40], F32)
    BFA = sbuf("BFA", [128, 71680], BF16)
    FA = sbuf("FA", [128, FAW], F32)
    PS = stack.enter_context(nc.psum_tensor("PS", [128, 4096], F32))

    def bk(i, off=0, w=512, p0=0, pn=128):
        return PS[p0:p0 + pn, i * 512 + off: i * 512 + off + w]

    def bf(off, w, p0=0, pn=128):
        return BFA[p0:p0 + pn, off:off + w]

    def fa(off, w, p0=0, pn=128):
        return FA[p0:p0 + pn, off:off + w]

    def bff(off_bf, w, p0=0, pn=128):
        return BFA[p0:p0 + pn, off_bf:off_bf + 2 * w].bitcast(F32)

    O_XTB, O_SCR, O_WSL, O_QC, O_KC, O_VN = 0, 16384, 20480, 22528, 38912, 55296
    O_RT = 55296
    O_VAS, O_PT, O_CXN, O_WO = 0, 2048, 4096, 8192
    O_XQT, O_W1Q, O_XQG = 0, 32768, 65536
    O_W2Q, O_HQ = 0, 32768
    F_STG, F_X1G, F_H0G, F_XRS, F_RCP = 0, 2048, 4096, 6144, 1024
    F_XPA = 7168
    F_WST = F_H0G

    def xsl(tt):
        m = tt % 4
        return (F_X1G + m * 1024) if m < 2 else (F_XPA + (m - 2) * 1024)

    def vdr():
        V.drain()

    def adr():
        A.drain()

    uid = [0]

    def _u():
        uid[0] += 1
        return uid[0]

    def replicate(row_ap, out_ap, n, tag, wait_ev=None):
        if wait_ev is not None:
            T.wait(wait_ev)
        T.matmul(PS[:, 7 * 512:7 * 512 + n], onesrf[0:1, :], row_ap, start=True, stop=True)\
            .inc("PE", ev=tag + "_mm")
        V.wait(tag + "_mm")
        V.tensor_copy(out=out_ap, in_=PS[:, 7 * 512:7 * 512 + n]).inc("DV", ev=tag)
        vdr()

    def cross_sum(cols_ap, n, tag, wait_ev=None):
        if wait_ev is not None:
            T.wait(wait_ev)
        T.matmul(PS[0:1, 7 * 512:7 * 512 + n], onescf[:, 0:1], cols_ap, start=True, stop=True)\
            .inc("PE", ev=tag)

    def emit_bisect(Sbuf, width, target, tag, abs_wait):
        A.wait(abs_wait)
        A.activation(out=junk[:, 0:width], in_=Sbuf[:, 0:width], func=AF.Abs)\
            .inc("AC", ev=tag + "_abs")
        V.memset(qlo[:, :], 0.0)
        V.tensor_reduce(out=qhi[:, :], in_=Sbuf[:, 0:width], axis=AX, op=AL.max,
                        apply_absolute_value=True).inc("DV")
        vdr()
        V.wait(tag + "_abs")
        for _ in range(NBIS):
            V.tensor_scalar(out=qmid[:, :], in0=qlo[:, :], scalar1=qhi[:, 0:1],
                            scalar2=0.5, op0=AL.add, op1=AL.mult).inc("DV")
            vdr()
            V.tensor_scalar(out=junk[:, 1024:1024 + width], in0=junk[:, 0:width],
                            scalar1=qmid[:, 0:1], scalar2=None, op0=AL.is_gt, op1=AL.add,
                            accum_out=qcnt[:, 0:1]).inc("DV")
            vdr()
            V.tensor_single_scalar(out=qmsk[:, :], in_=qcnt[:, :], scalar=float(target), op=AL.is_gt).inc("DV")
            vdr()
            V.copy_predicated(out=qlo[:, :], mask=qmsk[:, :], data=qmid[:, :]).inc("DV")
            vdr()
            V.tensor_single_scalar(out=qmsk[:, :], in_=qcnt[:, :], scalar=float(target), op=AL.is_le).inc("DV")
            vdr()
            V.copy_predicated(out=qhi[:, :], mask=qmsk[:, :], data=qmid[:, :]).inc("DV")
            vdr()
        V.tensor_copy(out=qvec[:, 0:1], in_=qhi[:, :]).inc("DV")
        V.tensor_tensor(out=qvec[:, 1:2], in0=qhi[:, :], in1=qhi[:, :], op=AL.mult).inc("DV", ev=tag + "_qv")
        vdr()

    def emit_ar(idx, tag):
        G.wait(tag + "_arin")
        G.dma_start(out=cbi[idx], in_=ario[0:1, :]).inc("gARB", ev=tag + "_cbi")
        G.wait(tag + "_cbi")
        G.collective_compute("AllReduce", AL.add, replica_groups=[list(range(NCORES))],
                             ins=[cbi[idx].opt()], outs=[cbo[idx].opt()]).inc("CC", ev=tag + "_cc")
        G.wait(tag + "_cc")
        G.dma_start(out=ario[0:1, :], in_=cbo[idx]).inc("gARB", ev=tag + "_arout")

    def emit_stats_to_ario(tag):
        cross_sum(qvec[:, 0:2], 2, tag + "_cs", wait_ev=tag + "_qv")
        A.wait(tag + "_cs")
        A.copy(out=ario[0:1, 0:2], in_=PS[0:1, 7 * 512:7 * 512 + 2]).inc("AC", ev=tag + "_arin")
        adr()

    def emit_thresholds(tag, nrows):
        V.wait(tag + "_arout")
        V.tensor_single_scalar(out=rowa[0:1, 0:1], in_=ario[0:1, 0:1], scalar=1.0 / nrows, op=AL.mult).inc("DV")
        V.tensor_single_scalar(out=rowa[0:1, 1:2], in_=ario[0:1, 1:2], scalar=1.0 / nrows, op=AL.mult).inc("DV")
        vdr()
        V.tensor_tensor(out=rowa[0:1, 2:3], in0=rowa[0:1, 0:1], in1=rowa[0:1, 0:1], op=AL.mult).inc("DV")
        vdr()
        V.tensor_tensor(out=rowa[0:1, 3:4], in0=rowa[0:1, 1:2], in1=rowa[0:1, 2:3], op=AL.subtract).inc("DV")
        vdr()
        V.tensor_scalar(out=rowa[0:1, 4:5], in0=rowa[0:1, 3:4], scalar1=1.0 / nrows,
                        scalar2=1e-32, op0=AL.mult, op1=AL.add).inc("DV", ev=tag + "_var")
        vdr()
        A.wait(tag + "_var")
        A.activation(out=rowa[0:1, 5:6], in_=rowa[0:1, 4:5], func=AF.Ln).inc("AC")
        adr()
        A.activation(out=rowa[0:1, 6:7], in_=rowa[0:1, 5:6], func=AF.Exp, scale=cval[0:1, 4:5])\
            .inc("AC", ev=tag + "_sqrt")
        V.wait(tag + "_sqrt")
        V.tensor_single_scalar(out=rowa[0:1, 6:7], in_=rowa[0:1, 6:7], scalar=1.6, op=AL.mult).inc("DV")
        vdr()
        for j, kk in enumerate([-14.0, -3.0, 3.0, 14.0]):
            V.tensor_single_scalar(out=rowb[0:1, j:j + 1], in_=rowa[0:1, 6:7], scalar=kk, op=AL.mult).inc("DV")
        vdr()
        for j in range(4):
            V.tensor_tensor(out=rowb[0:1, j:j + 1], in0=rowb[0:1, j:j + 1], in1=rowa[0:1, 0:1],
                            op=AL.add).inc("DV")
        vdr()
        V.tensor_single_scalar(out=rowb[0:1, 0:4], in_=rowb[0:1, 0:4], scalar=1e-20, op=AL.max)\
            .inc("DV", ev=tag + "_thr")
        vdr()
        replicate(rowb[0:1, 0:4], thr4[:, 0:4], 4, tag + "_rep", wait_ev=tag + "_thr")

    def emit_counts_finish(tag, nchunks):
        for j in range(4):
            V.tensor_reduce(out=cnt4[:, j:j + 1], in_=cntall[:, j * 64:j * 64 + nchunks],
                            axis=AX, op=AL.add).inc("DV")
        vdr()
        _e = V.tensor_copy(out=cnt4[:, 0:1], in_=cnt4[:, 0:1])
        _e.inc("DV", ev=tag + "_cnt4")
        vdr()
        cross_sum(cnt4[:, 0:4], 4, tag + "_ccs", wait_ev=tag + "_cnt4")
        A.wait(tag + "_ccs")
        A.copy(out=ario[0:1, 0:4], in_=PS[0:1, 7 * 512:7 * 512 + 4]).inc("AC", ev=tag + "_arin")
        adr()

    def emit_interp(tag, gstar, ntot, out_col, gamma_col):
        V.wait(tag + "_arout")
        V.memset(rowa[0:1, 8:9], 0.0)
        V.memset(rowa[0:1, 9:10], float(ntot))
        V.tensor_copy(out=rowa[0:1, 10:11], in_=rowb[0:1, 0:1]).inc("DV")
        V.tensor_copy(out=rowa[0:1, 11:12], in_=ario[0:1, 0:1]).inc("DV")
        vdr()
        for i in range(3):
            V.tensor_single_scalar(out=qmsk[0:1, 0:1], in_=ario[0:1, i:i + 1], scalar=gstar,
                                   op=AL.is_ge).inc("DV")
            vdr()
            V.copy_predicated(out=rowa[0:1, 8:9], mask=qmsk[0:1, 0:1], data=rowb[0:1, i:i + 1]).inc("DV")
            V.copy_predicated(out=rowa[0:1, 9:10], mask=qmsk[0:1, 0:1], data=ario[0:1, i:i + 1]).inc("DV")
            V.copy_predicated(out=rowa[0:1, 10:11], mask=qmsk[0:1, 0:1], data=rowb[0:1, i + 1:i + 2]).inc("DV")
            V.copy_predicated(out=rowa[0:1, 11:12], mask=qmsk[0:1, 0:1], data=ario[0:1, i + 1:i + 2]).inc("DV")
            vdr()
        V.tensor_tensor(out=rowa[0:1, 12:13], in0=rowa[0:1, 9:10], in1=rowa[0:1, 11:12], op=AL.subtract).inc("DV")
        V.tensor_single_scalar(out=rowa[0:1, 13:14], in_=rowa[0:1, 9:10], scalar=gstar, op=AL.subtract).inc("DV")
        V.tensor_tensor(out=rowa[0:1, 14:15], in0=rowa[0:1, 10:11], in1=rowa[0:1, 8:9], op=AL.subtract).inc("DV")
        vdr()
        V.tensor_single_scalar(out=rowa[0:1, 12:13], in_=rowa[0:1, 12:13], scalar=0.5, op=AL.max).inc("DV")
        vdr()
        V.reciprocal(out=rowa[0:1, 12:13], in_=rowa[0:1, 12:13]).inc("DV")
        vdr()
        V.tensor_tensor(out=rowa[0:1, 13:14], in0=rowa[0:1, 13:14], in1=rowa[0:1, 12:13], op=AL.mult).inc("DV")
        vdr()
        V.tensor_tensor(out=rowa[0:1, 13:14], in0=rowa[0:1, 13:14], in1=rowa[0:1, 14:15], op=AL.mult).inc("DV")
        vdr()
        V.tensor_tensor(out=rowa[0:1, 13:14], in0=rowa[0:1, 13:14], in1=rowa[0:1, 8:9], op=AL.add).inc("DV")
        vdr()
        V.tensor_single_scalar(out=rowa[0:1, 13:14], in_=rowa[0:1, 13:14], scalar=1e-5, op=AL.max).inc("DV")
        vdr()
        V.reciprocal(out=rowa[0:1, 14:15], in_=rowa[0:1, 13:14]).inc("DV")
        vdr()
        V.tensor_single_scalar(out=rowb[0:1, 8:9], in_=rowa[0:1, 14:15], scalar=127.0, op=AL.mult).inc("DV")
        V.tensor_single_scalar(out=rowb[0:1, 9:10], in_=rowa[0:1, 13:14], scalar=1.0 / 127.0, op=AL.mult).inc("DV")
        vdr()
        V.tensor_tensor(out=rowb[0:1, 9:10], in0=rowb[0:1, 9:10], in1=rowb[0:1, gamma_col:gamma_col + 1],
                        op=AL.mult).inc("DV", ev=tag + "_sca")
        vdr()
        replicate(rowb[0:1, 8:10], sca[:, out_col:out_col + 2], 2, tag + "_screp", wait_ev=tag + "_sca")

    def finish_debug():
        V.memset(junk[:, :], 0.0)
        vdr()
        _e = V.tensor_copy(out=junk[:, 0:1024], in_=junk[:, 0:1024])
        _e.inc("DV", ev="dbg_fin")
        G.wait("dbg_fin")
        G.dma_start(out=dout[0:128, :], in_=junk[:, 0:1024]).inc("gOUT", ev="dbg_out")
        G.wait("dbg_out")

    def program():
        for dst, src in [(cosT[:, :], dcos), (sinT[:, :], dsin), (identT[:, :], dident),
                         (u2T[:, :], du2), (dnegT[:, :], ddneg), (onesrb[:, :], donesrb),
                         (ones64[:, :], dones64), (onescf[:, :], donescf), (onesrf[:, :], donesrf),
                         (cval[:, :], dcval), (bqpT[:, :], dbqp), (bkpT[:, :], dbkp),
                         (lngT[:, :], dlng), (lnbT[:, :], dlnb)]:
            SY.dma_start(out=dst, in_=src).inc("sCON")
        SY.dma_start(out=fa(F_STG, 1024, 0, 1), in_=dbv).inc("sCON")
        SY.dma_start(out=fa(F_X1G, 1024, 0, 1), in_=dbo).inc("sCON", ev="consts_ld")
        V.wait("consts_ld")
        V.memset(ario[0:1, :], 0.0)
        V.memset(dbgT[:, :], 0.0)
        V.tensor_copy(out=bvT[:, :], in_=fa(F_STG, 1024, 0, 1)).inc("DV")
        V.tensor_copy(out=boT[:, :], in_=fa(F_X1G, 1024, 0, 1)).inc("DV", ev="bias_cast")

        def emit_gamma_load(ci):
            if ci >= 64:
                return
            wi, cc = divmod(ci, 32)
            dsrc = dw1 if wi == 0 else dw2
            if ci >= 2:
                G.wait(f"gred_{ci - 2}")
            else:
                G.wait("vp_0_1_15")
            G.dma_start(out=bff(O_SCR + (ci % 2) * 2048, 1024),
                        in_=dsrc[:, cc * 1024:(cc + 1) * 1024])\
                .inc("sGA" if ci % 2 == 0 else "sGB", ev=f"gld_{ci}")

        def emit_gamma_red(ci):
            wi, cc = divmod(ci, 32)
            V.wait(f"gld_{ci}")
            V.tensor_reduce(out=gsum[:, 2 + cc:3 + cc],
                            in_=bff(O_SCR + (ci % 2) * 2048, 1024), axis=AX, op=AL.add,
                            apply_absolute_value=True).inc("DV", ev=f"gred_{ci}")
            if cc == 31:
                vdr()
                V.tensor_reduce(out=gsum[:, wi:wi + 1], in_=gsum[:, 2:34], axis=AX,
                                op=AL.add).inc("DV", ev=f"gtot_{wi}")
                vdr()

        def emit_gamma_finish():
            T.wait_if("x1p_0_15_1")
            cross_sum(gsum[:, 0:2], 2, "g_cs", wait_ev="gtot_1")
            A.wait("g_cs")
            A.copy(out=rowb[0:1, 10:12], in_=PS[0:1, 7 * 512:7 * 512 + 2]).inc("AC", ev="g_row")
            adr()
            V.wait("g_row")
            V.tensor_scalar(out=rowb[0:1, 12:13], in0=rowb[0:1, 10:11], scalar1=1.0 / (HID * D),
                            scalar2=1e-5, op0=AL.mult, op1=AL.add).inc("DV")
            V.tensor_scalar(out=rowb[0:1, 13:14], in0=rowb[0:1, 11:12], scalar1=1.0 / (HID * D),
                            scalar2=1e-5, op0=AL.mult, op1=AL.add).inc("DV")
            vdr()
            V.reciprocal(out=rowb[0:1, 14:15], in_=rowb[0:1, 12:13]).inc("DV")
            V.reciprocal(out=rowb[0:1, 15:16], in_=rowb[0:1, 13:14]).inc("DV", ev="ginv")
            vdr()
            replicate(rowb[0:1, 14:16], gq[:, 0:2], 2, "gq_rep", wait_ev="ginv")

        if stage == 11:
            finish_debug()
            return

        for b in range(BPC):
            for c8 in range(8):
                sl = c8 % 2
                if c8 >= 2:
                    SY.wait(f"xTc_{b}_{c8 - 2}")
                    G.wait(f"xTc_{b}_{c8 - 2}")
                elif b == 0:
                    SY.wait("bias_cast")
                    G.wait("bias_cast")
                elif b == 1:
                    SY.wait_if("x1p_0_15_1")
                    SY.wait_if("cxn_0_3_15")
                    G.wait_if("h0_0_13")
                    G.wait_if("x1s_0_12")
                    G.wait_if("x1s_0_13")
                    G.wait_if("o1_0_13")
                    G.wait_if("o16_0_15")
                xt_base = b * (128 * 8 * S) + c8 * S
                SY.dma_start(out=fa(F_STG + sl * 1024, 1024),
                             in_=bass.AP(dxT.tensor, xt_base, [[8 * S, 128], [1, 1024]]))\
                    .inc(("sXT0" if sl == 0 else "sXT1"), ev=f"xTlA_{b}_{c8}")
                G.dma_start(out=fa(F_X1G + sl * 1024, 1024),
                            in_=bass.AP(dxT.tensor, xt_base + 1024, [[8 * S, 128], [1, 1024]]))\
                    .inc(("sXT2" if sl == 0 else "sXT3"), ev=f"xTl_{b}_{c8}")
                A.wait(f"xTlA_{b}_{c8}")
                A.wait(f"xTl_{b}_{c8}")
                if b == 1 and c8 == 0:
                    A.wait_if("x1p_0_15_1")
                A.activation(out=bf(O_XTB + c8 * S, 1024), in_=fa(F_STG + sl * 1024, 1024),
                             func=AF.Copy).inc("AC")
                A.activation(out=bf(O_XTB + c8 * S + 1024, 1024), in_=fa(F_X1G + sl * 1024, 1024),
                             func=AF.Copy).inc("AC", ev=f"xTc_{b}_{c8}")

            for which, (dW, biasT, O_DST) in enumerate([(dWk, bkpT, O_KC), (dWq, bqpT, O_QC)]):
                for g4 in range(4):
                    for sub in range(2):
                        et = 2 * g4 + sub
                        wsl = et % 2
                        wsl4 = et % 4
                        wstg = (F_WST + wsl4 * 1024) if wsl4 < 2 else (F_XPA + (wsl4 - 2) * 1024)
                        if et >= 4:
                            SY.wait(f"wcast_{b}_{which}_{et - 4}")
                            G.wait(f"wcast_{b}_{which}_{et - 4}")
                        elif which == 1:
                            SY.wait(f"wcast_{b}_0_{4 + et}")
                            G.wait(f"wcast_{b}_0_{4 + et}")
                        elif b == 1:
                            if wsl4 < 2:
                                for eng_ in (SY, G):
                                    eng_.wait_if("h0s_0_15")
                                    eng_.wait_if("sub1_0_15")
                                    eng_.wait_if("o2_0_15")
                            else:
                                for eng_ in (SY, G):
                                    eng_.wait_if("x1s_0_14")
                                    eng_.wait_if("x1s_0_15")
                                    eng_.wait_if("h0_0_15")
                                    eng_.wait_if("o1_0_15")
                                    eng_.wait_if("o16_0_15")
                        for half, ldeng in ((0, SY), (1, G)):
                            ldeng.dma_start(out=fa(wstg + half * 512, 512),
                                            in_=bass.AP(dW.tensor, et * 1024 + half * 512,
                                                        [[8 * D, 128], [1, 512]]))\
                                .inc(f"sWS{wsl4 * 2 + half}", ev=f"wldq{half}_{b}_{which}_{et}")
                        A.wait(f"wldq0_{b}_{which}_{et}")
                        A.wait(f"wldq1_{b}_{which}_{et}")
                        if et >= 2:
                            A.wait(f"pj_{b}_{which}_{(et - 2) * 4 + 3}")
                        elif which == 1:
                            A.wait(f"pj_{b}_0_{(6 + et) * 4 + 3}")
                        A.activation(out=bf(O_WSL + wsl * 1024, 1024), in_=fa(wstg, 1024),
                                     func=AF.Copy).inc("AC", ev=f"wcast_{b}_{which}_{et}")
                        gi = which * 4 + g4
                        PNL = O_SCR if gi % 2 == 0 else O_RT + 8192
                        for half in range(4):
                            pb = (et * 4 + half) % 2
                            T.wait(f"wcast_{b}_{which}_{et}")
                            if et * 4 + half >= 2:
                                T.wait(f"pje_{b}_{which}_{et * 4 + half - 2}")
                            for dc in range(8):
                                T.matmul(bk(pb), bf(O_WSL + wsl * 1024 + dc * 128, 128),
                                         bf(O_XTB + dc * S + half * 512, 512),
                                         start=(dc == 0), stop=(dc == 7))\
                                    .inc("PE", ev=(f"pj_{b}_{which}_{et * 4 + half}" if dc == 7 else None))
                            A.wait(f"pj_{b}_{which}_{et * 4 + half}")
                            if sub == 0 and half == 0:
                                if gi >= 2:
                                    wp, gp = divmod(gi - 2, 4)
                                    A.wait(f"perm_{b}_{wp}_{gp}_1")
                                    A.wait(f"perm_{b}_{wp}_{gp}_3")
                                elif b == 1 and gi == 0:
                                    A.wait_if("gred_63")
                            A.activation(out=bf(PNL + sub * 2048 + half * 512, 512),
                                         in_=bk(pb), func=AF.Identity, bias=biasT[:, et:et + 1])\
                                .inc("AC", ev=f"pje_{b}_{which}_{et * 4 + half}")
                    V.wait(f"pje_{b}_{which}_{(2 * g4 + 1) * 4 + 3}")
                    vdr()
                    x1 = bf(PNL, 2048)
                    x2 = bf(PNL + 2048, 2048)
                    tA = bf(O_RT, 2048)
                    tB = bf(O_RT + 2048, 2048)
                    tC = bf(O_RT + 4096, 2048)
                    tD = bf(O_RT + 6144, 2048)
                    V.tensor_tensor(out=tA, in0=x1, in1=cosT[:, :], op=AL.mult).inc("DV")
                    V.tensor_tensor(out=tB, in0=x2, in1=sinT[:, :], op=AL.mult).inc("DV")
                    V.tensor_tensor(out=tC, in0=x1, in1=sinT[:, :], op=AL.mult).inc("DV")
                    V.tensor_tensor(out=tD, in0=x2, in1=cosT[:, :], op=AL.mult).inc("DV")
                    vdr()
                    V.tensor_tensor(out=x1, in0=tA, in1=tB, op=AL.subtract).inc("DV")
                    V.tensor_tensor(out=x2, in0=tC, in1=tD, op=AL.add).inc("DV", ev=f"rope_{b}_{which}_{g4}")
                    SY.wait(f"rope_{b}_{which}_{g4}")
                    G.wait(f"rope_{b}_{which}_{g4}")
                    for j in range(4):
                        h = 4 * g4 + j
                        pc = h // 2
                        po = (h % 2) * 64
                        if j < 2:
                            peng, pstr = SY, ("gPRM" if gi % 2 == 0 else "gPR2")
                        else:
                            peng, pstr = G, ("aPRM" if gi % 2 == 0 else "aPR2")
                        peng.dma_start(out=bf(O_DST + pc * S, S, po, 32),
                                       in_=bf(PNL, S, j * 32, 32)).inc(pstr)
                        peng.dma_start(out=bf(O_DST + pc * S, S, po + 32, 32),
                                       in_=bf(PNL + 2048, S, j * 32, 32))\
                            .inc(pstr, ev=f"perm_{b}_{which}_{g4}_{j}")

            def wo_issue(c8, b_=b):
                sl4 = c8 % 4
                wst = (F_WST + sl4 * 1024) if sl4 < 2 else (F_XPA + (sl4 - 2) * 1024)
                for half in range(2):
                    A.dma_start(out=fa(wst + half * 512, 512),
                                in_=dWo[:, c8 * 1024 + half * 512: c8 * 1024 + half * 512 + 512])\
                        .inc(f"sWO{sl4 * 2 + half}", ev=f"wol{half}_{b_}_{c8}")

            adr()
            for c8 in range(4):
                wo_issue(c8)

            for ec in range(2):
                if ec == 0:
                    A.wait(f"xTc_{b}_6")
                    A.wait(f"xTc_{b}_7")
                else:
                    A.wait(f"wvc_{b}_0")
                for half in range(2):
                    A.dma_start(out=fa(F_STG + half * 1024, 1024),
                                in_=bass.AP(dWv.tensor, ec * 512 + half * 2 * D,
                                            [[8 * D, 128], [D, 2], [1, 512]]))\
                        .inc("sWV" if half == 0 else "sWV2", ev=f"wvlA{half}_{b}_{ec}")
                    A.dma_start(out=fa(F_X1G + half * 1024, 1024),
                                in_=bass.AP(dWv.tensor, 4 * D + ec * 512 + half * 2 * D,
                                            [[8 * D, 128], [D, 2], [1, 512]]))\
                        .inc("sWV3" if half == 0 else "sWV4", ev=f"wvlB{half}_{b}_{ec}")
                for evn in (f"wvlA0_{b}_{ec}", f"wvlA1_{b}_{ec}",
                            f"wvlB0_{b}_{ec}", f"wvlB1_{b}_{ec}"):
                    A.wait(evn)
                if ec == 0:
                    A.wait(f"perm_{b}_1_2_1")
                    A.wait(f"perm_{b}_1_2_3")
                else:
                    A.wait(f"vpe_{b}_0_15")
                A.activation(out=bf(O_SCR, 2048), in_=fa(F_STG, 2048), func=AF.Copy).inc("AC")
                A.activation(out=bf(O_SCR + 2048, 2048), in_=fa(F_X1G, 2048), func=AF.Copy)\
                    .inc("AC", ev=f"wvc_{b}_{ec}")
                for tt in range(16):
                    pb = tt % 2
                    T.wait(f"wvc_{b}_{ec}")
                    if tt >= 2:
                        T.wait(f"vpe_{b}_{ec}_{tt - 2}")
                    for dc in range(8):
                        T.matmul(bk(pb), bf(O_XTB + dc * S + tt * 128, 128),
                                 bf(O_SCR + dc * 512, 512),
                                 start=(dc == 0), stop=False).inc("PE")
                    T.matmul(bk(pb), onesrb[0:1, :], bvT[0:1, ec * 512:(ec + 1) * 512],
                             start=False, stop=True).inc("PE", ev=f"vp_{b}_{ec}_{tt}")
                    V.wait(f"vp_{b}_{ec}_{tt}")
                    if tt == 0 and ec == 0:
                        vdr()
                    if tt == 8:
                        V.wait(f"perm_{b}_1_3_1")
                        V.wait(f"perm_{b}_1_3_3")
                    V.tensor_copy(out=bf(O_VN + tt * 1024 + ec * 512, 512), in_=bk(pb))\
                        .inc("DV", ev=f"vpe_{b}_{ec}_{tt}")

            for c8 in range(8):
                sl4 = c8 % 4
                wst = (F_WST + sl4 * 1024) if sl4 < 2 else (F_XPA + (sl4 - 2) * 1024)
                A.wait(f"wol0_{b}_{c8}")
                A.wait(f"wol1_{b}_{c8}")
                if c8 == 0:
                    A.wait(f"vpe_{b}_1_15")
                A.activation(out=bf(O_WO + c8 * 1024, 1024), in_=fa(wst, 1024),
                             func=AF.Copy).inc("AC", ev=f"woc_{b}_{c8}")
                if c8 + 4 < 8:
                    adr()
                    wo_issue(c8 + 4)
            T.wait("woc_%d_7" % b)

            if stage == 12:
                continue
            pend = []
            if b == 0:
                emit_gamma_load(0)
                emit_gamma_load(1)
            for qq in range(4):
                if qq > 0:
                    T.wait(f"x1p_{b}_{qq * 4 - 2}_0")
                else:
                    T.wait(f"perm_{b}_1_3_1")
                    T.wait(f"perm_{b}_1_3_3")
                    if b == 1:
                        T.wait("x1p_0_14_0")
                for h in range(H):
                    if b == 0:
                        emit_gamma_load(qq * 16 + h + 2)
                        emit_gamma_red(qq * 16 + h)
                    if pend and h >= 1:
                        pend.pop(0)()
                        if pend and h <= 5:
                            pend.pop(0)()
                    pc = h // 2
                    po = (h % 2) * 64
                    cb = 4 + 2 * (h % 2)
                    nkt = 4 * qq + 4

                    def emit_scores(kt):
                        c0 = max(kt * 128 - qq * 512, 0)
                        sb_ = kt % 2
                        if kt >= 2:
                            T.wait(f"exp_{b}_{qq}_{h}_{kt - 2}")
                        T.matmul(bk(sb_, c0, 512 - c0),
                                 bf(O_KC + pc * S + kt * 128, 128, po, 64),
                                 bf(O_QC + pc * S + qq * 512 + c0, 512 - c0, po, 64),
                                 start=True, stop=(kt < 4 * qq)).inc("PE")
                        if kt >= 4 * qq:
                            T.matmul(bk(sb_, c0, 128), dnegT[:, :], u2T[:, :],
                                     start=False, stop=True).inc("PE")
                        K.event(f"sc_{b}_{qq}_{h}_{kt}", "PE")

                    def emit_exp(kt):
                        c0 = max(kt * 128 - qq * 512, 0)
                        sb_ = kt % 2
                        A.wait(f"sc_{b}_{qq}_{h}_{kt}")
                        if kt >= 2:
                            A.wait(f"cx_{b}_{qq}_{h}_{kt - 2}")
                        A.activation(out=bf(O_PT + sb_ * 1024 + c0, 512 - c0),
                                     in_=bk(sb_, c0, 512 - c0),
                                     func=AF.Exp, bias=cval[:, 0:1], scale=cval[:, 1:2])\
                            .inc("AC", ev=f"exp_{b}_{qq}_{h}_{kt}")

                    def emit_ctx(kt):
                        c0 = max(kt * 128 - qq * 512, 0)
                        sb_ = kt % 2
                        T.wait(f"exp_{b}_{qq}_{h}_{kt}")
                        if kt == 0 and h < 2:
                            if qq > 0:
                                T.wait(f"x1p_{b}_{qq * 4 - 1}_{h}")
                            elif b == 1:
                                T.wait(f"x1p_0_15_{h}")
                        if kt == 0 and (qq > 0 or h >= 2):
                            ph = h - 2 if h >= 2 else H - 2 + h
                            pq = qq if h >= 2 else qq - 1
                            T.wait(f"cxn_{b}_{pq}_{ph}")
                        if b == 1 and qq == 0 and h == 1 and kt == 0:
                            T.wait_if("gq_rep")
                            T.wait_if("g_row")
                        T.matmul(PS[0:64, cb * 512 + c0: cb * 512 + 512],
                                 bf(O_VN + kt * 1024 + h * 64, 64),
                                 bf(O_PT + sb_ * 1024 + c0, 512 - c0),
                                 start=(kt == 0), stop=(kt == nkt - 1)).inc("PE")
                        T.matmul(PS[0:64, (cb + 1) * 512 + c0: (cb + 1) * 512 + 512],
                                 ones64[:, 0:64],
                                 bf(O_PT + sb_ * 1024 + c0, 512 - c0),
                                 start=(kt == 0), stop=(kt == nkt - 1)).inc("PE", ev=f"cx_{b}_{qq}_{h}_{kt}")

                    for kt in range(nkt):
                        emit_scores(kt)
                        emit_exp(kt)
                        if kt > 0:
                            emit_ctx(kt - 1)
                    emit_ctx(nkt - 1)
                    rp = (h % 2) * 64
                    A.wait(f"cx_{b}_{qq}_{h}_{nkt - 1}")
                    A.activation(out=fa(F_STG, 512, rp, 64),
                                 in_=PS[0:64, (cb + 1) * 512:(cb + 1) * 512 + 512],
                                 func=AF.Ln).inc("AC")
                    adr()
                    A.activation(out=fa(F_STG + 512, 512, rp, 64), in_=fa(F_STG, 512, rp, 64),
                                 func=AF.Exp, scale=cval[rp:rp + 64, 3:4]).inc("AC", ev=f"rcp_{b}_{qq}_{h}")
                    V.wait(f"rcp_{b}_{qq}_{h}")
                    V.tensor_tensor(out=bf(O_CXN + pc * 512, 512, po, 64),
                                    in0=PS[0:64, cb * 512:cb * 512 + 512],
                                    in1=fa(F_STG + 512, 512, rp, 64), op=AL.mult)\
                        .inc("DV", ev=f"cxn_{b}_{qq}_{h}")
                if stage == 13:
                    continue
                T.wait(f"cxn_{b}_{qq}_{H - 1}")
                T.wait(f"cxn_{b}_{qq}_{H - 2}")
                for ti in range(4):
                    tt = qq * 4 + ti
                    xrs_off = F_XRS if tt % 2 == 0 else F_RCP
                    slo = xsl(tt)
                    if True:
                        if tt >= 2:
                            SY.wait_if(f"x1p_{b}_{tt - 2}_1")
                        elif tt == 1:
                            SY.wait(f"woc_{b}_7")
                        elif b == 1:
                            SY.wait_if("x1p_0_14_1")
                        SY.dma_start(out=fa(xrs_off, 1024), in_=dx[(b * S + tt * 128):(b * S + (tt + 1) * 128), :])\
                            .inc(("sXR0" if tt % 2 == 0 else "sXR1"), ev=f"xres_{b}_{tt}")
                    for ec in range(2):
                        pbA = (ti % 2) * 4 + ec * 2
                        pbB = pbA + 1
                        if tt >= 2:
                            T.wait(f"x1p_{b}_{tt - 2}_{ec}")
                        elif b == 1:
                            T.wait(f"x1p_0_{14 + tt}_{ec}")
                        for h in range(0, H, 2):
                            T.matmul(bk(pbA), bf(O_CXN + (h // 2) * 512 + ti * 128, 128, 0, 64),
                                     bf(O_WO + (h // 2) * 1024 + ec * 512, 512, 0, 64),
                                     start=(h == 0), stop=False).inc("PE")
                        T.matmul(bk(pbA), onesrb[0:1, :], boT[0:1, ec * 512:(ec + 1) * 512],
                                 start=False, stop=True).inc("PE")
                        for h in range(1, H, 2):
                            T.matmul(bk(pbB), bf(O_CXN + (h // 2) * 512 + ti * 128, 128, 64, 64),
                                     bf(O_WO + (h // 2) * 1024 + ec * 512, 512, 64, 64),
                                     start=(h == 1), stop=(h == H - 1)).inc("PE")
                        K.event(f"wo_{b}_{tt}_{ec}", "PE")
                        V.wait(f"wo_{b}_{tt}_{ec}")
                        if tt >= 4:
                            V.wait_if(f"x1s_{b}_{tt - 4}")
                            V.wait_if(f"h0_{b}_{tt - 4}")
                            V.wait_if(f"o1_{b}_{(tt - 4) | 1}")
                            V.wait_if(f"o16_{b}_{(tt - 4) | 3}")
                        elif b == 1:
                            V.wait_if(f"x1s_0_{12 + tt}")
                            V.wait_if(f"h0_0_{12 + tt}")
                            V.wait_if(f"o1_0_{(12 + tt) | 1}")
                            V.wait_if("o16_0_15")
                        V.wait(f"xres_{b}_{tt}")
                        if stage == 16:
                            V.wait_if(f"o16_{b}_{tt - 1}")
                        V.tensor_copy(out=fa(slo + ec * 512, 512), in_=bk(pbA)).inc("DV")
                        vdr()
                        V.tensor_tensor(out=fa(slo + ec * 512, 512),
                                        in0=fa(slo + ec * 512, 512),
                                        in1=bk(pbB), op=AL.add).inc("DV")
                        vdr()
                        V.tensor_tensor(out=fa(slo + ec * 512, 512),
                                        in0=fa(slo + ec * 512, 512),
                                        in1=fa(xrs_off + ec * 512, 512), op=AL.add)\
                            .inc("DV", ev=f"x1p_{b}_{tt}_{ec}")
                    if stage == 16:
                        SY.wait(f"x1p_{b}_{tt}_1")
                        SY.dma_start(out=dout[(b * S + tt * 128):(b * S + (tt + 1) * 128), :],
                                     in_=fa(slo, 1024)).inc("sDBG", ev=f"o16_{b}_{tt}")
                        continue
                if stage in (14, 16):
                    continue

                def mk_ln1(b_=b, qq_=qq, grp_=0):
                    base = F_X1G if grp_ == 0 else F_XPA

                    def s1():
                        vdr()
                        for k2 in range(2):
                            V.tensor_scalar(out=junk[:, k2 * 1024:k2 * 1024 + 1024],
                                            in0=fa(base + k2 * 1024, 1024),
                                            scalar1=0.0, scalar2=None, op0=AL.add, op1=AL.add,
                                            accum_out=stat[:, k2:k2 + 1]).inc("DV")
                        vdr()
                        V.tensor_single_scalar(out=stat[:, 2:4], in_=stat[:, 0:2], scalar=1.0 / D,
                                               op=AL.mult).inc("DV")
                        vdr()

                    def s2():
                        for k2 in range(2):
                            V.tensor_scalar(out=fa(base + k2 * 1024, 1024),
                                            in0=fa(base + k2 * 1024, 1024),
                                            scalar1=stat[:, 2 + k2:3 + k2], scalar2=None,
                                            op0=AL.subtract, op1=AL.bypass).inc("DV")
                        vdr()

                    def s3():
                        for k2 in range(2):
                            V.scalar_tensor_tensor(out=junk[:, k2 * 1024:k2 * 1024 + 1024],
                                                   in0=fa(base + k2 * 1024, 1024),
                                                   scalar=0.0, in1=fa(base + k2 * 1024, 1024),
                                                   op0=AL.add, op1=AL.mult,
                                                   accum_out=stat[:, 4 + k2:5 + k2]).inc("DV")
                        vdr()
                        V.tensor_scalar(out=stat[:, 6:8], in0=stat[:, 4:6], scalar1=1.0 / D,
                                        scalar2=EPS_LN, op0=AL.mult, op1=AL.add)\
                            .inc("DV", ev=f"v1_{b_}_{qq_}_{grp_}")
                        A.wait(f"v1_{b_}_{qq_}_{grp_}")
                        A.activation(out=stat[:, 8:10], in_=stat[:, 6:8], func=AF.Ln).inc("AC")
                        adr()
                        A.activation(out=stat[:, 6:8], in_=stat[:, 8:10], func=AF.Exp,
                                     scale=cval[:, 2:3]).inc("AC", ev=f"rs1_{b_}_{qq_}_{grp_}")

                    def mk_s4(k2):
                        def s4():
                            tt2 = qq_ * 4 + grp_ * 2 + k2
                            if k2 == 0:
                                V.wait(f"rs1_{b_}_{qq_}_{grp_}")
                            V.tensor_scalar(out=fa(base + k2 * 1024, 1024),
                                            in0=fa(base + k2 * 1024, 1024),
                                            scalar1=stat[:, 6 + k2:7 + k2], scalar2=None,
                                            op0=AL.mult, op1=AL.bypass).inc("DV", ev=f"x1_{b_}_{tt2}")
                            SY.wait(f"x1_{b_}_{tt2}")
                            SY.dma_start(out=dx1[(b_ * S + tt2 * 128):(b_ * S + (tt2 + 1) * 128), :],
                                         in_=fa(base + k2 * 1024, 1024))\
                                .inc(f"sX1S{tt2 % 4}", ev=f"x1s_{b_}_{tt2}")
                            if stage == 1:
                                SY.dma_start(out=dout[(b_ * S + tt2 * 128):(b_ * S + (tt2 + 1) * 128), :],
                                             in_=fa(base + k2 * 1024, 1024)).inc("sDBG", ev=f"o1_{b_}_{tt2}")
                        return s4

                    return [s1, s2, s3, mk_s4(0), mk_s4(1)]

                def mk_ln2(b_=b, qq_=qq, grp_=0):
                    t0 = qq_ * 4 + grp_ * 2
                    base = F_X1G if grp_ == 0 else F_XPA

                    def t1():
                        if t0 >= 2:
                            V.wait_if(f"h0s_{b_}_{t0 - 1}")
                            V.wait_if(f"o2_{b_}_{t0 - 1}")
                        elif b_ == 1:
                            V.wait_if("h0s_0_15")
                            V.wait_if("o2_0_15")
                        vdr()
                        for k2 in range(2):
                            V.tensor_scalar(out=junk[:, k2 * 1024:k2 * 1024 + 1024],
                                            in0=fa(base + k2 * 1024, 1024),
                                            scalar1=0.0, scalar2=None, op0=AL.add, op1=AL.add,
                                            accum_out=stat[:, k2:k2 + 1]).inc("DV")
                        vdr()
                        V.tensor_single_scalar(out=stat[:, 2:4], in_=stat[:, 0:2], scalar=1.0 / D,
                                               op=AL.mult).inc("DV")
                        vdr()

                    def t2():
                        for k2 in range(2):
                            V.tensor_scalar(out=fa(F_H0G + k2 * 1024, 1024),
                                            in0=fa(base + k2 * 1024, 1024),
                                            scalar1=stat[:, 2 + k2:3 + k2], scalar2=None,
                                            op0=AL.subtract, op1=AL.bypass).inc("DV")
                        vdr()

                    def t3():
                        for k2 in range(2):
                            V.scalar_tensor_tensor(out=junk[:, k2 * 1024:k2 * 1024 + 1024],
                                                   in0=fa(F_H0G + k2 * 1024, 1024),
                                                   scalar=0.0, in1=fa(F_H0G + k2 * 1024, 1024),
                                                   op0=AL.add, op1=AL.mult,
                                                   accum_out=stat[:, 4 + k2:5 + k2]).inc("DV")
                        vdr()
                        V.tensor_scalar(out=stat[:, 6:8], in0=stat[:, 4:6], scalar1=1.0 / D,
                                        scalar2=EPS_LN, op0=AL.mult, op1=AL.add)\
                            .inc("DV", ev=f"v2_{b_}_{qq_}_{grp_}")
                        A.wait(f"v2_{b_}_{qq_}_{grp_}")
                        A.activation(out=stat[:, 8:10], in_=stat[:, 6:8], func=AF.Ln).inc("AC")
                        adr()
                        A.activation(out=stat[:, 6:8], in_=stat[:, 8:10], func=AF.Exp,
                                     scale=cval[:, 2:3]).inc("AC", ev=f"rs2_{b_}_{qq_}_{grp_}")

                    def mk_t4(k2):
                        def t4():
                            tt2 = qq_ * 4 + grp_ * 2 + k2
                            if k2 == 0:
                                V.wait(f"rs2_{b_}_{qq_}_{grp_}")
                            V.tensor_scalar(out=fa(F_H0G + k2 * 1024, 1024),
                                            in0=fa(F_H0G + k2 * 1024, 1024),
                                            scalar1=stat[:, 6 + k2:7 + k2], scalar2=None,
                                            op0=AL.mult, op1=AL.bypass).inc("DV")
                            vdr()
                            V.tensor_tensor(out=fa(F_H0G + k2 * 1024, 1024),
                                            in0=fa(F_H0G + k2 * 1024, 1024), in1=lngT[:, :],
                                            op=AL.mult).inc("DV")
                            vdr()
                            V.tensor_tensor(out=fa(F_H0G + k2 * 1024, 1024),
                                            in0=fa(F_H0G + k2 * 1024, 1024), in1=lnbT[:, :],
                                            op=AL.add).inc("DV", ev=f"h0_{b_}_{tt2}")
                            vdr()
                            SY.wait(f"h0_{b_}_{tt2}")
                            SY.dma_start(out=dh0[(b_ * S + tt2 * 128):(b_ * S + (tt2 + 1) * 128), :],
                                         in_=fa(F_H0G + k2 * 1024, 1024)).inc("sH0S", ev=f"h0s_{b_}_{tt2}")
                            if stage == 2:
                                SY.dma_start(out=dout[(b_ * S + tt2 * 128):(b_ * S + (tt2 + 1) * 128), :],
                                             in_=fa(F_H0G + k2 * 1024, 1024)).inc("sDBG", ev=f"o2_{b_}_{tt2}")
                            gt = b_ * 16 + tt2
                            V.tensor_copy(out=S1[:, gt * 32:(gt + 1) * 32],
                                          in_=bass.AP(FA[:, :].tensor, F_H0G + k2 * 1024,
                                                      [[FAW, 128], [32, 32]]))\
                                .inc("DV", ev=f"sub1_{b_}_{tt2}")
                        return t4

                    return [t1, t2, t3, mk_t4(0), mk_t4(1)]

                assert not pend
                pend.extend(mk_ln1(grp_=0))
                if stage != 15:
                    pend.extend(mk_ln2(grp_=0))
                pend.extend(mk_ln1(grp_=1))
                if stage != 15:
                    pend.extend(mk_ln2(grp_=1))
                if qq == 3:
                    while pend:
                        pend.pop(0)()

            if b == 0 and stage != 12:
                emit_gamma_finish()

        if stage in (12, 13, 14, 15, 16):
            if stage == 16:
                G.wait(f"o16_{BPC - 1}_{15}")
                return
            finish_debug()
            return
        if stage <= 2:
            if stage == 1:
                G.wait(f"o1_{BPC - 1}_{15}")
            else:
                G.wait(f"o2_{BPC - 1}_{15}")
            return

        V.wait(f"sub1_{BPC - 1}_{15}")
        emit_bisect(S1, 1024, 5, "q1", "sub1_1_15")
        emit_stats_to_ario("q1")
        emit_ar(0, "q1")
        emit_thresholds("q1", 1024)
        for tt in range(NT):
            sl = tt % 2
            if tt >= 2:
                SY.wait(f"q1ct_{tt - 2}")
            elif tt == 0:
                SY.wait_if("cxn_1_3_15")
                SY.wait_if("h0s_1_15")
            else:
                SY.wait_if("x1p_1_15_1")
            SY.dma_start(out=fa(F_STG + sl * 1024, 1024), in_=dh0[tt * 128:(tt + 1) * 128, :])\
                .inc(("sQ1a" if sl == 0 else "sQ1b"), ev=f"q1l_{tt}")
            A.wait(f"q1l_{tt}")
            if tt == 0:
                A.wait("q1_rep")
                A.wait_if("h0s_1_15")
            elif tt >= 2:
                A.wait(f"q1ct_{tt - 2}")
            A.activation(out=fa(F_H0G + sl * 1024, 1024), in_=fa(F_STG + sl * 1024, 1024), func=AF.Abs)\
                .inc("AC", ev=f"q1a_{tt}")
            V.wait(f"q1a_{tt}")
            for j in range(4):
                V.tensor_scalar(out=bf(O_XQT + (j + 4 * sl) * 1024, 1024), in0=fa(F_H0G + sl * 1024, 1024),
                                scalar1=thr4[:, j:j + 1],
                                scalar2=None, op0=AL.is_gt, op1=AL.add,
                                accum_out=cntall[:, j * 64 + tt:j * 64 + tt + 1])\
                    .inc("DV", ev=(f"q1ct_{tt}" if j == 3 else None))
        vdr()
        emit_counts_finish("q1c", NT)
        emit_ar(1, "q1c")
        emit_interp("q1c", G1_STAR, N1, 0, 12)
        V.tensor_copy(out=dbgT[:, 0:4], in_=thr4[:, :]).inc("DV")
        V.tensor_copy(out=dbgT[:, 4:8], in_=cnt4[:, :]).inc("DV")
        V.tensor_copy(out=dbgT[:, 8:10], in_=sca[:, 0:2]).inc("DV")
        V.tensor_copy(out=dbgT[:, 10:12], in_=gq[:, 0:2]).inc("DV")
        V.tensor_copy(out=dbgT[0:1, 12:20], in_=ario[0:1, :]).inc("DV")

        if stage == 21:
            finish_debug()
            return
        for grp in range(8):
            for ti in range(4):
                tt = grp * 4 + ti
                sl = ti % 2
                if ti == 0 and grp >= 1:
                    V.wait(f"xqt_{grp - 1}_7")
                if tt >= 2:
                    SY.wait(f"xq_{tt - 2}")
                else:
                    SY.wait(f"q1ct_{30 + tt}")
                SY.dma_start(out=fa(F_STG + sl * 1024, 1024), in_=dh0[tt * 128:(tt + 1) * 128, :])\
                    .inc(("sXQa" if sl == 0 else "sXQb"), ev=f"xql_{tt}")
                V.wait(f"xql_{tt}")
                if tt == 0:
                    V.wait("q1c_screp")
                V.tensor_scalar(out=fa(F_H0G, 1024), in0=fa(F_STG + sl * 1024, 1024),
                                scalar1=sca[:, 0:1], scalar2=MAGIC, op0=AL.mult, op1=AL.add).inc("DV")
                vdr()
                V.tensor_scalar(out=fa(F_H0G, 1024), in0=fa(F_H0G, 1024),
                                scalar1=MAGIC, scalar2=-128.0, op0=AL.subtract, op1=AL.max).inc("DV")
                vdr()
                V.tensor_single_scalar(out=bf(O_XQG + ti * 1024, 1024), in_=fa(F_H0G, 1024),
                                       scalar=127.0, op=AL.min).inc("DV", ev=f"xq_{tt}")
                vdr()
            PSB = PS[:, :].bitcast(BF16)
            for dc in range(8):
                pb = dc % 2
                for ti in range(4):
                    T.wait(f"xq_{grp * 4 + ti}")
                if dc >= 2:
                    T.wait(f"xqe_{grp}_{dc - 2}")
                for ti in range(4):
                    T.transpose(PSB[:, pb * 1024 + ti * 128: pb * 1024 + (ti + 1) * 128],
                                bf(O_XQG + ti * 1024 + dc * 128, 128),
                                identT[:, :]).inc("PE", ev=(f"xqt_{grp}_{dc}" if ti == 3 else None))
                V.wait(f"xqt_{grp}_{dc}")
                V.tensor_copy(out=bf(O_XQT + dc * 4096 + grp * 512, 512),
                              in_=PSB[:, pb * 1024: pb * 1024 + 512])\
                    .inc("DV", ev=f"xqe_{grp}_{dc}")

        if stage == 22:
            finish_debug()
            return
        for ci in range(16):
            wsl = F_STG if ci % 2 == 0 else F_X1G
            if ci >= 2:
                SY.wait(f"w1q_{ci - 2}")
                G.wait(f"w1q_{ci - 2}")
            else:
                SY.wait("xq_31")
                G.wait("xq_31")
                for eng_ in (SY, G):
                    eng_.wait_if("x1s_1_12")
                    eng_.wait_if("x1s_1_13")
                    eng_.wait_if("h0_1_13")
                    eng_.wait_if("o1_1_13")
            SY.dma_start(out=fa(wsl, 1024), in_=dw1[:, ci * 2048:ci * 2048 + 1024])\
                .inc("sW1L" if ci % 2 == 0 else "sW1La", ev=f"w1lA_{ci}")
            G.dma_start(out=fa(wsl + 1024, 1024), in_=dw1[:, ci * 2048 + 1024:ci * 2048 + 2048])\
                .inc("sW1G" if ci % 2 == 0 else "sW1Gb", ev=f"w1l_{ci}")
            V.wait(f"w1lA_{ci}")
            V.wait(f"w1l_{ci}")
            if ci == 0:
                V.wait("gq_rep")
            V.tensor_scalar(out=fa(F_H0G, 1024), in0=fa(wsl, 1024), scalar1=gq[:, 0:1],
                            scalar2=MAGIC, op0=AL.mult, op1=AL.add).inc("DV")
            V.tensor_scalar(out=fa(F_H0G + 1024, 1024), in0=fa(wsl + 1024, 1024), scalar1=gq[:, 0:1],
                            scalar2=MAGIC, op0=AL.mult, op1=AL.add).inc("DV")
            vdr()
            V.tensor_scalar(out=fa(F_H0G, 2048), in0=fa(F_H0G, 2048),
                            scalar1=MAGIC, scalar2=-1.0, op0=AL.subtract, op1=AL.max).inc("DV")
            vdr()
            V.tensor_single_scalar(out=bf(O_W1Q + ci * 2048, 2048), in_=fa(F_H0G, 2048),
                                   scalar=1.0, op=AL.min).inc("DV", ev=f"w1q_{ci}")
            vdr()

        if stage == 23:
            finish_debug()
            return
        for jt in range(32):
            for tc in range(8):
                pb = (jt * 8 + tc) % 2
                if jt == 0 and tc == 0:
                    T.wait("w1q_15")
                    T.wait("xqe_7_7")
                if jt * 8 + tc >= 2:
                    T.wait(f"ge_{(jt * 8 + tc - 2) // 8}_{(jt * 8 + tc - 2) % 8}")
                for dc in range(8):
                    T.matmul(bk(pb), bf(O_W1Q + dc * 4096 + jt * 128, 128),
                             bf(O_XQT + dc * 4096 + tc * 512, 512),
                             start=(dc == 0), stop=(dc == 7)).inc("PE", ev=(f"m1_{jt}_{tc}" if dc == 7 else None))
                A.wait(f"m1_{jt}_{tc}")
                hsl = tc % 2
                if jt * 8 + tc >= 2:
                    A.wait(f"hs_{(jt * 8 + tc - 2) // 8}_{(jt * 8 + tc - 2) % 8}")
                    A.wait(f"sub2_{(jt * 8 + tc - 2) // 8}_{(jt * 8 + tc - 2) % 8}")
                else:
                    A.wait_if(f"x1s_1_{12 + hsl}")
                A.activation(out=fa(F_X1G + hsl * 1024, 512), in_=bk(pb), func=AF.Gelu,
                             scale=sca[:, 1:2]).inc("AC", ev=f"ge_{jt}_{tc}")
                speng = SY if hsl == 0 else G
                speng.wait(f"ge_{jt}_{tc}")
                speng.dma_start(out=bass.AP(dh.tensor, jt * (128 * TOK) + tc * 512, [[TOK, 128], [1, 512]]),
                                in_=fa(F_X1G + hsl * 1024, 512))\
                    .inc(("sHSa" if hsl == 0 else "sHSb"), ev=f"hs_{jt}_{tc}")
                V.wait(f"ge_{jt}_{tc}")
                V.tensor_copy(out=S2[:, jt * 32 + tc * 4: jt * 32 + tc * 4 + 4],
                              in_=bass.AP(FA[:, :].tensor, F_X1G + hsl * 1024, [[FAW, 128], [128, 4]]))\
                    .inc("DV", ev=f"sub2_{jt}_{tc}")

        if stage == 24:
            finish_debug()
            return
        V.wait("sub2_31_7")
        emit_bisect(S2, 1024, 5, "q2", "sub2_31_7")
        emit_stats_to_ario("q2")
        emit_ar(2, "q2")
        emit_thresholds("q2", 1024)
        for pi in range(32):
            for hf in range(2):
                ii = pi * 2 + hf
                pr = ii % 2
                if ii >= 2:
                    SY.wait(f"q2ct_{ii - 2}")
                    G.wait(f"q2ct_{ii - 2}")
                elif ii == 0:
                    for eng_ in (SY, G):
                        eng_.wait("hs_31_6")
                        eng_.wait("hs_31_7")
                        eng_.wait("sub2_31_7")
                ldA = fa(F_STG, 1024) if pr == 0 else bff(0, 1024)
                ldB = fa(F_X1G, 1024) if pr == 0 else bff(2048, 1024)
                dst = fa(F_H0G, 2048) if pr == 0 else bff(4096, 2048)
                SY.dma_start(out=ldA,
                             in_=bass.AP(dh.tensor, pi * (128 * TOK) + hf * 2048, [[TOK, 128], [1, 1024]]))\
                    .inc("sQ2L" if pr == 0 else "sQ2c", ev=f"q2lA_{ii}")
                G.dma_start(out=ldB,
                            in_=bass.AP(dh.tensor, pi * (128 * TOK) + hf * 2048 + 1024, [[TOK, 128], [1, 1024]]))\
                    .inc("sQ2b" if pr == 0 else "sQ2d", ev=f"q2l_{ii}")
                A.wait(f"q2lA_{ii}")
                A.wait(f"q2l_{ii}")
                if ii == 0:
                    A.wait("q2_rep")
                elif ii >= 2:
                    A.wait(f"q2ct_{ii - 2}")
                A.activation(out=dst[:, 0:1024], in_=ldA, func=AF.Abs).inc("AC")
                A.activation(out=dst[:, 1024:2048], in_=ldB, func=AF.Abs).inc("AC", ev=f"q2a_{ii}")
                V.wait(f"q2a_{ii}")
                for j in range(4):
                    V.tensor_scalar(out=bf(16384 + (j + 4 * pr) * 2048, 2048), in0=dst,
                                    scalar1=thr4[:, j:j + 1],
                                    scalar2=None, op0=AL.is_gt, op1=AL.add,
                                    accum_out=cntall[:, j * 64 + ii:j * 64 + ii + 1])\
                        .inc("DV", ev=(f"q2ct_{ii}" if j == 3 else None))
        vdr()
        emit_counts_finish("q2c", 64)
        emit_ar(3, "q2c")
        emit_interp("q2c", G2_STAR, N2, 2, 13)
        V.tensor_copy(out=dbgT[:, 20:24], in_=thr4[:, :]).inc("DV")
        V.tensor_copy(out=dbgT[:, 24:28], in_=cnt4[:, :]).inc("DV")
        V.tensor_copy(out=dbgT[:, 28:30], in_=sca[:, 2:4]).inc("DV")
        V.tensor_copy(out=dbgT[0:1, 30:38], in_=ario[0:1, :]).inc("DV", ev="dbg_done")

        if stage == 25:
            finish_debug()
            return
        for ci in range(16):
            wsl = F_STG if ci % 2 == 0 else F_X1G
            if ci >= 2:
                SY.wait(f"w2q_{ci - 2}")
                G.wait(f"w2q_{ci - 2}")
            else:
                SY.wait("q2a_63")
                G.wait("q2a_63")
            SY.dma_start(out=fa(wsl, 1024), in_=dw2[:, ci * 2048:ci * 2048 + 1024])\
                .inc("sW2L" if ci % 2 == 0 else "sW2La", ev=f"w2lA_{ci}")
            G.dma_start(out=fa(wsl + 1024, 1024), in_=dw2[:, ci * 2048 + 1024:ci * 2048 + 2048])\
                .inc("sW2G" if ci % 2 == 0 else "sW2Gb", ev=f"w2l_{ci}")
            V.wait(f"w2lA_{ci}")
            V.wait(f"w2l_{ci}")
            if ci == 0:
                V.wait("q2c_screp")
            V.tensor_scalar(out=fa(F_H0G, 1024), in0=fa(wsl, 1024), scalar1=gq[:, 1:2],
                            scalar2=MAGIC, op0=AL.mult, op1=AL.add).inc("DV")
            V.tensor_scalar(out=fa(F_H0G + 1024, 1024), in0=fa(wsl + 1024, 1024), scalar1=gq[:, 1:2],
                            scalar2=MAGIC, op0=AL.mult, op1=AL.add).inc("DV")
            vdr()
            V.tensor_scalar(out=fa(F_H0G, 2048), in0=fa(F_H0G, 2048),
                            scalar1=MAGIC, scalar2=-1.0, op0=AL.subtract, op1=AL.max).inc("DV")
            vdr()
            V.tensor_single_scalar(out=bf(O_W2Q + ci * 2048, 2048), in_=fa(F_H0G, 2048),
                                   scalar=1.0, op=AL.min).inc("DV", ev=f"w2q_{ci}")
            vdr()

        if stage == 26:
            finish_debug()
            return
        for tb in range(32):
            pr = tb % 2
            hblk = fa(0, 4096) if pr == 0 else bff(40960, 4096)
            hqs = O_HQ + pr * 4096
            ogc = 4096 if pr == 0 else None
            ogx = fa(4096, 1024) if pr == 0 else bff(51200, 1024)
            oox = fa(5120, 1024) if pr == 0 else bff(53248, 1024)
            x1r = fa(6144, 1024) if pr == 0 else bff(49152, 1024)
            sldx = "sX1R" if pr == 0 else "sX1R2"
            for jt in range(32):
                if jt == 0:
                    if tb >= 2:
                        SY.wait(f"hqq_{tb - 2}")
                    elif tb == 0:
                        SY.wait("w2q_15")
                SY.dma_start(out=hblk[:, jt * 128:(jt + 1) * 128],
                             in_=bass.AP(dh.tensor, jt * (128 * TOK) + tb * 128, [[TOK, 128], [1, 128]]))\
                    .inc("sHB" if pr == 0 else "sHB3",
                         ev=(f"hbl_{tb}" if jt == 31 else None))
            V.wait(f"hbl_{tb}")
            if tb >= 2:
                V.wait(f"m2_{tb - 2}_1")
            V.tensor_scalar(out=hblk, in0=hblk, scalar1=sca[:, 2:3],
                            scalar2=MAGIC, op0=AL.mult, op1=AL.add).inc("DV")
            vdr()
            V.tensor_scalar(out=hblk, in0=hblk,
                            scalar1=MAGIC, scalar2=-128.0, op0=AL.subtract, op1=AL.max).inc("DV")
            vdr()
            V.tensor_single_scalar(out=bf(hqs, 4096), in_=hblk, scalar=127.0, op=AL.min)\
                .inc("DV", ev=f"hqq_{tb}")
            if tb >= 2:
                SY.wait(f"oge_{tb - 2}_1")
            elif tb == 0:
                for k4 in range(4):
                    SY.wait_if(f"x1s_1_{12 + k4}")
            SY.dma_start(out=x1r, in_=dx1[tb * 128:(tb + 1) * 128, :])\
                .inc(sldx, ev=f"x1r_{tb}")
            for ec in range(2):
                pb = pr * 2 + ec
                T.wait(f"hqq_{tb}")
                if tb >= 2:
                    T.wait(f"oge_{tb - 2}_{ec}")
                for jc in range(32):
                    T.matmul(bk(pb), bf(hqs + jc * 128, 128),
                             bf(O_W2Q + jc * 1024 + ec * 512, 512),
                             start=(jc == 0), stop=(jc == 31)).inc("PE", ev=(f"m2_{tb}_{ec}" if jc == 31 else None))
                V.wait(f"m2_{tb}_{ec}")
                V.wait(f"x1r_{tb}")
                V.tensor_scalar(out=ogx[:, ec * 512:(ec + 1) * 512], in0=bk(pb), scalar1=sca[:, 3:4],
                                scalar2=None, op0=AL.mult, op1=AL.bypass).inc("DV")
                vdr()
                V.tensor_tensor(out=ogx[:, ec * 512:(ec + 1) * 512], in0=ogx[:, ec * 512:(ec + 1) * 512],
                                in1=x1r[:, ec * 512:(ec + 1) * 512], op=AL.add).inc("DV", ev=f"oge_{tb}_{ec}")
            vdr()
            V.tensor_scalar(out=junk[:, 0:1024], in0=ogx, scalar1=0.0, scalar2=None,
                            op0=AL.add, op1=AL.add, accum_out=stat[:, 0:1]).inc("DV")
            vdr()
            V.tensor_single_scalar(out=stat[:, 1:2], in_=stat[:, 0:1], scalar=1.0 / D, op=AL.mult).inc("DV")
            vdr()
            V.tensor_scalar(out=ogx, in0=ogx, scalar1=stat[:, 1:2],
                            scalar2=None, op0=AL.subtract, op1=AL.bypass).inc("DV")
            vdr()
            V.scalar_tensor_tensor(out=junk[:, 0:1024], in0=ogx, scalar=0.0, in1=ogx,
                                   op0=AL.add, op1=AL.mult, accum_out=stat[:, 2:3]).inc("DV")
            vdr()
            V.tensor_scalar(out=stat[:, 3:4], in0=stat[:, 2:3], scalar1=1.0 / D, scalar2=EPS_LN,
                            op0=AL.mult, op1=AL.add).inc("DV", ev=f"v3_{tb}")
            A.wait(f"v3_{tb}")
            A.activation(out=stat[:, 4:5], in_=stat[:, 3:4], func=AF.Ln).inc("AC")
            adr()
            A.activation(out=stat[:, 3:4], in_=stat[:, 4:5], func=AF.Exp, scale=cval[:, 2:3])\
                .inc("AC", ev=f"rs3_{tb}")
            V.wait(f"rs3_{tb}")
            if tb >= 2:
                V.wait(f"os_{tb - 2}")
            V.tensor_scalar(out=oox, in0=ogx, scalar1=stat[:, 3:4],
                            scalar2=None, op0=AL.mult, op1=AL.bypass).inc("DV", ev=f"o3_{tb}")
            G.wait(f"o3_{tb}")
            G.dma_start(out=dout[tb * 128:(tb + 1) * 128, :], in_=oox)\
                .inc("gOUT" if pr == 0 else "sDBG", ev=f"os_{tb}")
        G.wait("dbg_done")
        G.dma_start(out=ddbg, in_=dbgT[:, :]).inc("gOUT", ev="dbg_st")
        G.wait("dbg_st")
        G.wait("os_31")

    K.begin("___dry___")
    program()

    s_ctx = [stack.enter_context(nc.semaphore("sem_" + n)) for n in K.sem_names]
    K.sems = dict(zip(K.sem_names, s_ctx))
    with nc.Block() as block:

        @block.sync
        def _(eng):
            K.engines["sync"] = eng
            K.begin("sync")
            program()

        @block.vector
        def _(eng):
            K.engines["vector"] = eng
            K.begin("vector")
            program()

        @block.scalar
        def _(eng):
            K.engines["scalar"] = eng
            K.begin("scalar")
            program()

        @block.tensor
        def _(eng):
            K.engines["tensor"] = eng
            K.begin("tensor")
            program()

        @block.gpsimd
        def _(eng):
            K.engines["gpsimd"] = eng
            K.begin("gpsimd")
            program()

    stack.close()
    return nc


_PROGRAM_CACHE = {}


def _get_program(stage=99):
    if stage not in _PROGRAM_CACHE:
        _PROGRAM_CACHE[stage] = build_program(stage)
    return _PROGRAM_CACHE[stage]


def kernel(L0_attn_x, L0_attn_Wq, L0_attn_bq, L0_attn_Wk, L0_attn_bk, L0_attn_Wv,
           L0_attn_bv, L0_attn_Wo, L0_attn_bo, L0_ffn_ln_g, L0_ffn_ln_b,
           L0_ffn_w1, L0_ffn_w2, _stage=99):
    x = np.ascontiguousarray(np.asarray(L0_attn_x, dtype=np.float32))
    Wq = np.asarray(L0_attn_Wq, dtype=np.float32)
    Wk = np.asarray(L0_attn_Wk, dtype=np.float32)
    Wv = np.asarray(L0_attn_Wv, dtype=np.float32)
    Wo = np.asarray(L0_attn_Wo, dtype=np.float32)
    bq = np.asarray(L0_attn_bq, dtype=np.float32)
    bk_ = np.asarray(L0_attn_bk, dtype=np.float32)
    bv = np.asarray(L0_attn_bv, dtype=np.float32)
    bo = np.asarray(L0_attn_bo, dtype=np.float32)
    g = np.asarray(L0_ffn_ln_g, dtype=np.float32)
    be = np.asarray(L0_ffn_ln_b, dtype=np.float32)
    w1 = np.asarray(L0_ffn_w1, dtype=np.float32)
    w2 = np.asarray(L0_ffn_w2, dtype=np.float32)

    perm = _perm_index()
    Wq_r = _reshape_et(np.ascontiguousarray(Wq[:, perm]))
    Wk_r = _reshape_et(np.ascontiguousarray(Wk[:, perm]))
    Wv_r = _reshape_r(Wv)
    Wo_r = _reshape_r(Wo)
    bqp = np.ascontiguousarray(bq[perm].reshape(8, 128).T)
    bkp = np.ascontiguousarray(bk_[perm].reshape(8, 128).T)
    w1T_r = _reshape_r(np.ascontiguousarray(w1.T))
    w2T_r = _reshape_r(np.ascontiguousarray(w2.T))
    lng = np.ascontiguousarray(np.tile(g[None, :], (128, 1)))
    lnb = np.ascontiguousarray(np.tile(be[None, :], (128, 1)))
    tabs = _host_tables()

    nc = _get_program(_stage)
    in_maps = []
    for c in range(NCORES):
        xc = x[2 * c:2 * c + 2]
        xflat = np.ascontiguousarray(xc.reshape(TOK, D))
        xT_r = np.stack([_reshape_r(np.ascontiguousarray(xc[bb].T)) for bb in range(BPC)])
        in_maps.append(dict(
            x=xflat, xT_r=xT_r, Wq_r=Wq_r, Wk_r=Wk_r, Wv_r=Wv_r, Wo_r=Wo_r,
            bqp=bqp, bkp=bkp, bv=bv.reshape(1, D), bo=bo.reshape(1, D),
            lng=lng, lnb=lnb, w1T_r=w1T_r, w2T_r=w2T_r,
            cos128=tabs["cos128"], sin128=tabs["sin128"], ident=tabs["ident"],
            u2=tabs["u2"], dneg=tabs["dneg"], ones_row_b=tabs["ones_row_b"],
            ones64_b=tabs["ones64_b"], ones_col_f=tabs["ones_col_f"],
            ones_row_f=tabs["ones_row_f"], cvals=tabs["cvals"],
        ))
    res = run_bass_kernel_spmd(nc, in_maps, core_ids=list(range(NCORES)))
    outs = [res.results[c]["out"].reshape(BPC, S, D) for c in range(NCORES)]
    return np.ascontiguousarray(np.concatenate(outs, axis=0), dtype=np.float32)
